# version 12
# baseline (speedup 1.0000x reference)
import numpy as np
from contextlib import ExitStack
import concourse.bass as bass
import concourse.mybir as mybir

F32 = mybir.dt.float32
BF16 = mybir.dt.bfloat16
I16 = mybir.dt.int16
AF = mybir.ActivationFunctionType
ALU = mybir.AluOpType
AX = mybir.AxisListType

C = 1024
H = 16
HS = 64
L = 128
FFI = 3072
RND = 12582912.0
QMAX = 32767.0
WQMAX = 7.0
BLK = 32


def ceil_div(a, b):
    return (a + b - 1) // b


class Prog:
    def __init__(self, name="prog"):
        self.nc = bass.Bass("TRN2", target_bir_lowering=False)
        self.ctx = ExitStack()
        self.sems = {}
        self.cnt = {}

    def sb(self, name, shape, dtype=F32):
        return self.ctx.enter_context(self.nc.sbuf_tensor(name, list(shape), dtype))

    def ps(self, name, shape, dtype=F32):
        return self.ctx.enter_context(self.nc.psum_tensor(name, list(shape), dtype))

    def sem(self, name):
        if name not in self.sems:
            self.sems[name] = self.ctx.enter_context(self.nc.semaphore(name))
            self.cnt[name] = 0
        return self.sems[name]

    def inc(self, instr, name, n=1):
        instr.then_inc(self.sems[name], n)
        self.cnt[name] += n
        return self.cnt[name]

    def wait(self, eng, name, v=None):
        eng.wait_ge(self.sems[name], self.cnt[name] if v is None else v)

    def constcol(self, value):
        if not hasattr(self, "consts"):
            self.consts = {}
        if value not in self.consts:
            t = self.sb(f"cc_{len(self.consts)}", [128, 1], F32)
            self.consts[value] = t
        return self.consts[value][:, 0:1]

    def emit_const_memsets(self, eng, semname):
        if not hasattr(self, "consts"):
            self.consts = {}
        ii = None
        for v, t in self.consts.items():
            ii = eng.memset(t[:], float(v))
        if ii is not None:
            self.inc(ii, semname)

    def din(self, name, shape, dtype=F32):
        return self.nc.dram_tensor(name, list(shape), dtype, kind="ExternalInput").ap()

    def dout(self, name, shape, dtype=F32):
        return self.nc.dram_tensor(name, list(shape), dtype, kind="ExternalOutput").ap()


def fm3(ap, p=128):
    return ap.rearrange("(t p) x -> p t x", p=p)


def build_L1(TC=2048, TSL=512):
    NCH = TC // L
    NSL = TC // TSL
    CPS = TSL // L
    TTP = ceil_div(TC + 1, 128) * 128
    NTT = TTP // 128
    p = Prog("L1")
    nc = p.nc

    x_tm = p.din("x_tm", [TTP, C])
    x_fm = p.din("x_fm", [C, TC + 1])
    WkT = p.din("WkT", [C, C])
    WvT = p.din("WvT", [C, C])
    WrT = p.din("WrT", [C, C])
    A_lora = p.din("A_lora", [C, 160])
    B_lora = p.din("B_lora", [5, 32, C])
    td_A = p.din("td_A", [C, 64])
    td_B = p.din("td_B", [64, C])
    lam_col = p.din("lam_col", [128, 5 * 8])
    mx_col = p.din("mx_col", [128, 8])
    tdmiu_col = p.din("tdmiu_col", [128, 8])
    u16 = p.din("u16", [128, 8 * 16])
    ident = p.din("ident", [128, 128])
    ident2 = p.din("ident2", [128, 64])

    murstd = p.dout("murstd", [2, TTP])
    pxn_o = p.dout("pxn", [128, 1])
    qt_o = p.dout("qt", [C, TC], BF16)
    kt_o = p.dout("kt", [C, TC], BF16)
    vq_o = p.dout("vq", [TC, C], BF16)
    xg_o = p.dout("xg", [C, TC], BF16)
    b_o = p.dout("b_row", [16, TC])
    utot_o = p.dout("utot", [128, 512])
    traj_o = p.dout("traj", [128, NCH * 512])
    dpre_o = p.dout("dpre", [128, NCH * 8])

    wk = p.sb("wk", [128, 8, C], BF16)
    wv = p.sb("wv", [128, 8, C], BF16)
    wr = p.sb("wr", [128, 8, C], BF16)
    al = p.sb("al", [128, 8, 160], BF16)
    bl = p.sb("bl", [128, 5, C], BF16)
    tda = p.sb("tda", [128, 8, 64], BF16)
    tdb = p.sb("tdb", [64, C], BF16)
    lam = p.sb("lam", [128, 5, 8])
    mxc = p.sb("mxc", [128, 8])
    tdm = p.sb("tdm", [128, 8])
    u16s = p.sb("u16s", [128, 8, 16], BF16)
    idn = p.sb("idn", [128, 128])
    idnb = p.sb("idnb", [128, 64], BF16)

    xtm = [p.sb(f"xtm{i}", [128, C]) for i in range(2)]
    trash = p.sb("trash", [128, C], BF16)
    muA = p.sb("muA", [128, NTT])
    rsA = p.sb("rsA", [128, NTT])
    mrP = p.sb("mrP", [128, 2 * NTT])

    xfs = p.sb("xfs", [128, 8, TSL + 1])
    mue = p.sb("mue", [128, TSL + 1])
    rse = p.sb("rse", [128, TSL + 1])
    xns = p.sb("xns", [128, 8, TSL + 1])
    dxa = p.sb("dxa", [128, 8, TSL])
    xm = p.sb("xm", [128, 8, TSL], BF16)
    lhA = p.sb("lhA", [96, TSL], BF16)
    lh2 = p.sb("lh2", [64, TSL], BF16)
    tmpm = p.sb("tmpm", [128, 8, TSL])
    xxb = [p.sb(f"xx{i}", [128, 8, TSL], BF16) for i in range(2)]
    th = p.sb("th", [64, TSL], BF16)
    expw = p.sb("expw", [128, 8, TSL])
    spr = p.sb("spr", [128, 8, TSL])
    epos = p.sb("epos", [128, 8, TSL])
    dtot = p.sb("dtot", [128, 8, CPS])
    rf = p.sb("rf", [128, 8, TSL])
    ktb = p.sb("ktb", [128, 8, TSL], BF16)
    khb = p.sb("khb", [128, 8, TSL], BF16)
    qtb = p.sb("qtb", [128, 8, TSL], BF16)
    vtb = p.sb("vtb", [128, CPS, C], BF16)
    khT = p.sb("khT", [128, 8, 128], BF16)
    bsl = p.sb("bsl", [16, TSL])
    sown = [p.sb(f"sown{i}", [128, 8, 64]) for i in range(2)]
    dprer = [p.sb(f"dprer{i}", [128, 8]) for i in range(2)]
    pxn_t = p.sb("pxn_t", [128, NSL + 1])
    zer = p.sb("zer", [128, L])

    psA = p.ps("psA", [128, 512])
    psX = p.ps("psX", [64, 512])
    PSW = max(TSL, 512)
    psP = [p.ps(f"psP{i}", [128, PSW]) for i in range(2)]
    psU = p.ps("psU", [128, 512])

    eps_col = p.constcol(1e-5)

    for s in ("qS", "qA", "qP", "qO", "qB0", "qB1", "qT0", "qT1", "sV", "sA", "sP", "sT"):
        p.sem(s)

    with nc.Block() as blk:
        @blk.sync
        def _(e):
            p.inc(e.dma_start(out=lam[:], in_=lam_col.rearrange("p (f t) -> p f t", f=5)), "qS", 16)
            p.inc(e.dma_start(out=mxc[:], in_=mx_col[:, :]), "qS", 16)
            p.inc(e.dma_start(out=tdm[:], in_=tdmiu_col[:, :]), "qS", 16)
            p.inc(e.dma_start(out=idn[:], in_=ident[:, :]), "qS", 16)
        qS_const = p.cnt["qS"]

        @blk.gpsimd
        def _(e):
            p.inc(e.dma_start(out=wk[:], in_=fm3(WkT)), "qP", 16)
            p.inc(e.dma_start(out=wv[:], in_=fm3(WvT)), "qP", 16)
            p.inc(e.dma_start(out=wr[:], in_=fm3(WrT)), "qP", 16)
            p.inc(e.dma_start(out=al[:], in_=fm3(A_lora)), "qP", 16)
            p.inc(e.dma_start(out=bl[0:32], in_=B_lora.rearrange("f i c -> i f c")), "qP", 16)
            p.inc(e.dma_start(out=bl[32:64], in_=B_lora.rearrange("f i c -> i f c")), "qP", 16)
            p.inc(e.dma_start(out=bl[64:96], in_=B_lora.rearrange("f i c -> i f c")), "qP", 16)
            p.inc(e.dma_start(out=tda[:], in_=fm3(td_A)), "qP", 16)
            p.inc(e.dma_start(out=tdb[:], in_=td_B[:, :]), "qP", 16)
            p.inc(e.dma_start(out=u16s[:], in_=u16.rearrange("p (t h) -> p t h", t=8)), "qP", 16)
            p.inc(e.dma_start(out=idnb[:], in_=ident2[:, :]), "qP", 16)
        qP_const = p.cnt["qP"]

        @blk.vector
        def _(e):
            e.memset(zer[:], 0.0)
            e.memset(sown[0][:], 0.0)
            e.memset(dprer[0][:], 1.0)
            p.emit_const_memsets(e, "sV")
            p.inc(e.memset(pxn_t[:], 0.0), "sV")
        sV_init = p.cnt["sV"]

        sV_b0 = p.cnt["sV"]
        sA_b0 = p.cnt["sA"]

        @blk.scalar
        def _(e):
            for j in range(NTT):
                if j >= 2:
                    p.wait(e, "sV", sV_b0 + (j - 1))
                    p.wait(e, "sA", sA_b0 + (j - 1))
                p.inc(e.dma_start(out=xtm[j % 2][:], in_=x_tm[128 * j:128 * (j + 1), :]), "qA", 16)
                e.wait_ge(p.sems["qA"], 16 * (j + 1))
                p.inc(e.activation(out=trash[:], in_=xtm[j % 2][:], func=AF.Square,
                                   accum_out=rsA[:, j:j + 1]), "sA")
        sA_sq = p.cnt["sA"]

        @blk.vector
        def _(e):
            for j in range(NTT):
                e.wait_ge(p.sems["qA"], 16 * (j + 1))
                p.inc(e.tensor_reduce(out=muA[:, j:j + 1], in_=xtm[j % 2][:], axis=AX.X,
                                      op=ALU.add), "sV")
        sV_sums = p.cnt["sV"]

        sc1 = p.sb("sc1", [128, NTT])
        sc2 = p.sb("sc2", [128, NTT])

        @blk.vector
        def _(e):
            p.wait(e, "sA", sA_sq)
            p.wait(e, "sV", sV_sums)
            e.tensor_scalar(out=mrP[:, 0:NTT], in0=muA[:], scalar1=1.0 / 1024.0, scalar2=None,
                            op0=ALU.mult)
            p.inc(e.tensor_scalar(out=sc1[:], in0=rsA[:], scalar1=1.0 / 1024.0, scalar2=None,
                                  op0=ALU.mult), "sV")
            p.wait(e, "sV")
            p.inc(e.tensor_tensor(out=sc2[:], in0=mrP[:, 0:NTT], in1=mrP[:, 0:NTT], op=ALU.mult), "sV")
            p.wait(e, "sV")
            p.inc(e.tensor_tensor(out=rsA[:], in0=sc1[:], in1=sc2[:], op=ALU.subtract), "sV")
        sV_var = p.cnt["sV"]

        @blk.scalar
        def _(e):
            p.wait(e, "sV", sV_var)
            p.inc(e.activation(out=sc1[:], in_=rsA[:], func=AF.Sqrt, bias=eps_col, scale=1.0), "sA")
        sA_sqrt = p.cnt["sA"]

        @blk.vector
        def _(e):
            p.wait(e, "sA", sA_sqrt)
            p.inc(e.reciprocal(out=mrP[:, NTT:2 * NTT], in_=sc1[:]), "sV")
        sV_mr = p.cnt["sV"]

        @blk.sync
        def _(e):
            p.wait(e, "sV", sV_mr)
            e.wait_ge(p.sems["qS"], qS_const)
            for j in range(NTT):
                p.inc(e.dma_start(out=murstd[0:1, 128 * j:128 * (j + 1)], in_=mrP[:, j:j + 1]), "qS", 16)
                p.inc(e.dma_start(out=murstd[1:2, 128 * j:128 * (j + 1)], in_=mrP[:, NTT + j:NTT + j + 1]), "qS", 16)
        qS_mr = p.cnt["qS"]

        for s in range(NSL):
            c0 = s * TSL
            sl1 = slice(c0, c0 + TSL + 1)
            slt = slice(c0, c0 + TSL)

            @blk.sync
            def _(e):
                if s > 0:
                    p.wait(e, "sV")
                    p.wait(e, "sA")
                    p.wait(e, "sT")
                    p.wait(e, "sP")
                e.wait_ge(p.sems["qS"], qS_mr)
                p.inc(e.dma_start(out=xfs[:], in_=fm3(x_fm)[:, :, sl1]), "qS", 16)
                p.inc(e.dma_start(out=mue[:], in_=murstd[0:1, c0:c0 + TSL + 1].broadcast_to([128, TSL + 1])), "qS", 16)
                p.inc(e.dma_start(out=rse[:], in_=murstd[1:2, c0:c0 + TSL + 1].broadcast_to([128, TSL + 1])), "qS", 16)
            qS_ld = p.cnt["qS"]

            @blk.vector
            def _(e):
                e.wait_ge(p.sems["qS"], qS_ld)
                p.inc(e.tensor_tensor(out=xns[:], in0=xfs[:],
                                in1=mue.rearrange("p (o c) -> p o c", o=1).broadcast_to([128, 8, TSL + 1]),
                                op=ALU.subtract), "sV")
                p.wait(e, "sV")
                p.inc(e.tensor_tensor(out=xns[:], in0=xns[:],
                                in1=rse.rearrange("p (o c) -> p o c", o=1).broadcast_to([128, 8, TSL + 1]),
                                op=ALU.mult), "sV")
                p.wait(e, "sV")
                e.tensor_reduce(out=pxn_t[:, s:s + 1], in_=xns[:, :, 1:TSL + 1], axis=AX.XY,
                                op=ALU.max, apply_absolute_value=True)
                p.inc(e.tensor_tensor(out=dxa[:], in0=xns[:, :, 0:TSL], in1=xns[:, :, 1:TSL + 1],
                                op=ALU.subtract), "sV")
                p.wait(e, "sV")
                for t in range(8):
                    ii = e.scalar_tensor_tensor(out=xm[:, t, :], in0=dxa[:, t, :],
                                                scalar=mxc[:, t:t + 1], in1=xns[:, t, 1:TSL + 1],
                                                op0=ALU.mult, op1=ALU.add)
                p.inc(ii, "sV")
            sV_xm = p.cnt["sV"]

            @blk.tensor
            def _(e):
                p.wait(e, "sV", sV_xm)
                p.wait(e, "sV")
                e.wait_ge(p.sems["qP"], qP_const)
                for kc in range(8):
                    e.matmul(psA[0:96, 0:TSL], al[:, kc, 0:96], xm[:, kc, :],
                             start=(kc == 0), stop=(kc == 7))
                for kc in range(8):
                    ii = e.matmul(psX[0:64, 0:TSL], al[:, kc, 96:160], xm[:, kc, :],
                                  start=(kc == 0), stop=(kc == 7))
                p.inc(ii, "sT")
            sT_lora = p.cnt["sT"]

            @blk.scalar
            def _(e):
                p.wait(e, "sT", sT_lora)
                e.activation(out=lhA[0:96, :], in_=psA[0:96, 0:TSL], func=AF.Tanh)
                p.inc(e.activation(out=lh2[:], in_=psX[0:64, 0:TSL], func=AF.Tanh), "sA")
            sA_lh = p.cnt["sA"]

            def emit_xx(fi, slot, eng_name, prev_tmpm=None):
                ev = {-2: p.cnt["sA"], -1: p.cnt["sA"]}
                if fi < 3:
                    rhs_ap = lhA[32 * fi:32 * fi + 32, :]
                    lbase = 32 * fi
                else:
                    rhs_ap = lh2[32 * (fi - 3):32 * (fi - 3) + 32, :]
                    lbase = 32 * (fi - 3)
                for tt in range(8):
                    pp = psP[tt % 2]

                    @blk.tensor
                    def _(e, fi=fi, tt=tt, pp=pp, rhs_ap=rhs_ap, lbase=lbase):
                        p.wait(e, "sA", sA_lh)
                        p.wait(e, "sA", ev[tt - 2])
                        if tt == 0:
                            p.wait(e, "sV")
                        p.inc(e.matmul(pp[:, 0:TSL], bl[lbase:lbase + 32, fi, 128 * tt:128 * tt + 128],
                                       rhs_ap, start=True, stop=True), "sT")

                    @blk.scalar
                    def _(e, fi=fi, tt=tt, pp=pp, prev_tmpm=prev_tmpm):
                        p.wait(e, "sT")
                        if tt == 0 and prev_tmpm is not None:
                            p.wait(e, prev_tmpm[0], prev_tmpm[1])
                        p.inc(e.activation(out=tmpm[:, tt, :], in_=pp[:, 0:TSL], func=AF.Identity,
                                           bias=lam[:, fi, tt:tt + 1], scale=1.0), "sA")
                    ev[tt] = p.cnt["sA"]
                dev = blk.vector if eng_name == "sV" else blk.gpsimd

                mid = {}

                @dev
                def _(e, slot=slot, eng_name=eng_name):
                    p.wait(e, "sA")
                    p.wait(e, "sT")
                    e.wait_ge(p.sems["qO"], p.cnt["qO"])
                    p.inc(e.tensor_tensor(out=tmpm[:], in0=tmpm[:], in1=dxa[:], op=ALU.mult), eng_name)
                    mid["v"] = p.cnt[eng_name]
                    p.wait(e, eng_name)
                    p.inc(e.tensor_tensor(out=xxb[slot][:], in0=tmpm[:], in1=xns[:, :, 1:TSL + 1],
                                          op=ALU.add), eng_name)
                return (eng_name, p.cnt[eng_name], mid["v"])

            dep_w = emit_xx(0, 0, "sV")

            @blk.tensor
            def _(e):
                p.wait(e, dep_w[0], dep_w[1])
                p.wait(e, "sA")
                for kc in range(8):
                    ii = e.matmul(psA[0:64, 0:TSL], tda[:, kc, :], xxb[0][:, kc, :],
                                  start=(kc == 0), stop=(kc == 7))
                p.inc(ii, "sT")

            @blk.scalar
            def _(e):
                p.wait(e, "sT")
                p.inc(e.activation(out=th[:], in_=psA[0:64, 0:TSL], func=AF.Tanh), "sA")

            evw = {-2: p.cnt["sA"], -1: p.cnt["sA"]}
            for tt in range(8):
                pp = psP[tt % 2]

                @blk.tensor
                def _(e, tt=tt, pp=pp):
                    p.wait(e, "sA", evw[tt - 2])
                    if tt == 0:
                        p.wait(e, "sA")
                        p.wait(e, "sV")
                    p.inc(e.matmul(pp[:, 0:TSL], tdb[:, 128 * tt:128 * tt + 128], th[:],
                                   start=True, stop=True), "sT")

                @blk.scalar
                def _(e, tt=tt, pp=pp):
                    p.wait(e, "sT")
                    p.inc(e.activation(out=expw[:, tt, :], in_=pp[:, 0:TSL], func=AF.Exp,
                                       bias=tdm[:, tt:tt + 1], scale=1.0), "sA")
                evw[tt] = p.cnt["sA"]
            sA_expw = p.cnt["sA"]

            @blk.vector
            def _(e):
                p.wait(e, "sA", sA_expw)
                for tt in range(8):
                    for cc in range(CPS):
                        ii = e.tensor_tensor_scan(out=spr[:, tt, L * cc:L * cc + L],
                                                  data0=expw[:, tt, L * cc:L * cc + L],
                                                  data1=zer[:, 0:L], initial=0.0,
                                                  op0=ALU.add, op1=ALU.add)
                p.inc(ii, "sV")
            sV_scan = p.cnt["sV"]

            @blk.scalar
            def _(e):
                p.wait(e, "sV", sV_scan)
                for tt in range(8):
                    e.activation(out=epos[:, tt, 1:TSL], in_=spr[:, tt, 0:TSL - 1], func=AF.Exp, scale=-1.0)
                    e.activation(out=dtot[:, tt, :], in_=spr[:, tt, L - 1:TSL:L], func=AF.Exp, scale=-1.0)
                    p.inc(e.activation(out=expw[:, tt, :], in_=spr[:, tt, :], func=AF.Exp, scale=1.0), "sA")
            sA_dec = p.cnt["sA"]
            eneg = expw

            @blk.vector
            def _(e):
                p.wait(e, "sA", sA_dec)
                for tt in range(8):
                    e.memset(epos[:, tt, 0:TSL:L], 1.0)
                p.inc(e.tensor_tensor(
                    out=spr.rearrange("p t (c l) -> p t c l", l=L),
                    in0=eneg.rearrange("p t (c l) -> p t c l", l=L),
                    in1=dtot.rearrange("p t (c o) -> p t c o", o=1).broadcast_to([128, 8, CPS, L]),
                    op=ALU.mult), "sV")
            sV_fk = p.cnt["sV"]
            fk = spr

            dep_g = emit_xx(4, 1, "sP", prev_tmpm=(dep_w[0], dep_w[1]))

            @blk.sync
            def _(e):
                p.wait(e, dep_g[0], dep_g[1])
                e.wait_ge(p.sems["qO"], p.cnt["qO"])
                p.inc(e.dma_start(out=fm3(xg_o)[:, :, slt], in_=xxb[1][:]), "qO", 16)

            dep_r = emit_xx(3, 0, "sP", prev_tmpm=(dep_g[0], dep_g[1]))
            evr = {-2: (p.cnt["sA"], p.cnt["sV"]), -1: (p.cnt["sA"], p.cnt["sV"])}
            for tt in range(8):
                pp = psP[tt % 2]

                @blk.tensor
                def _(e, tt=tt, pp=pp):
                    p.wait(e, dep_r[0], dep_r[1])
                    p.wait(e, "sA", evr[tt - 2][0])
                    p.wait(e, "sV", evr[tt - 2][1])
                    if tt == 0:
                        p.wait(e, "sA")
                        p.wait(e, "sV")
                    for kc in range(8):
                        ii = e.matmul(pp[:, 0:TSL], wr[:, kc, 128 * tt:128 * tt + 128],
                                      xxb[0][:, kc, :], start=(kc == 0), stop=(kc == 7))
                    p.inc(ii, "sT")

                @blk.scalar
                def _(e, tt=tt, pp=pp):
                    p.wait(e, "sT")
                    p.inc(e.activation(out=rf[:, tt, :], in_=pp[:, 0:TSL], func=AF.Identity), "sA")

                @blk.vector
                def _(e, tt=tt, pp=pp):
                    p.wait(e, "sT")
                    if tt == 0:
                        p.wait(e, "sV", sV_fk)
                    p.inc(e.scalar_tensor_tensor(out=qtb[:, tt, :], in0=pp[:, 0:TSL], scalar=1.0,
                                                 in1=epos[:, tt, :], op0=ALU.mult, op1=ALU.mult), "sV")
                evr[tt] = (p.cnt["sA"], p.cnt["sV"])
            sA_r = p.cnt["sA"]
            sV_q = p.cnt["sV"]

            dep_k = emit_xx(1, 1, "sP", prev_tmpm=(dep_r[0], dep_r[1]))
            rkb = xm
            evk = {-2: p.cnt["sV"], -1: p.cnt["sV"]}
            for tt in range(8):
                pp = psP[tt % 2]

                @blk.tensor
                def _(e, tt=tt, pp=pp):
                    p.wait(e, dep_k[0], dep_k[1])
                    p.wait(e, "sV", evk[tt - 2])
                    if tt == 0:
                        p.wait(e, "sA")
                        p.wait(e, "sV")
                    for kc in range(8):
                        ii = e.matmul(pp[:, 0:TSL], wk[:, kc, 128 * tt:128 * tt + 128],
                                      xxb[1][:, kc, :], start=(kc == 0), stop=(kc == 7))
                    p.inc(ii, "sT")

                @blk.vector
                def _(e, tt=tt, pp=pp):
                    p.wait(e, "sT")
                    if tt == 0:
                        p.wait(e, "sA", sA_r)
                    e.scalar_tensor_tensor(out=ktb[:, tt, :], in0=pp[:, 0:TSL], scalar=1.0,
                                           in1=eneg[:, tt, :], op0=ALU.mult, op1=ALU.mult)
                    e.scalar_tensor_tensor(out=khb[:, tt, :], in0=pp[:, 0:TSL], scalar=1.0,
                                           in1=fk[:, tt, :], op0=ALU.mult, op1=ALU.mult)
                    p.inc(e.scalar_tensor_tensor(out=rkb[:, tt, :], in0=pp[:, 0:TSL], scalar=1.0,
                                                 in1=rf[:, tt, :], op0=ALU.mult, op1=ALU.mult), "sV")
                evk[tt] = p.cnt["sV"]
            sV_k = p.cnt["sV"]

            @blk.sync
            def _(e):
                p.wait(e, "sV", sV_k)
                e.wait_ge(p.sems["qO"], p.cnt["qO"])
                p.inc(e.dma_start(out=fm3(qt_o)[:, :, slt], in_=qtb[:]), "qO", 16)
                p.inc(e.dma_start(out=fm3(kt_o)[:, :, slt], in_=ktb[:]), "qO", 16)

            dep_v = emit_xx(2, 0, "sP", prev_tmpm=(dep_k[0], dep_k[1]))
            for st in range(CPS):
                for hh in range(2):
                    pp = psP[(st * 2 + hh) % 2]

                    @blk.tensor
                    def _(e, st=st, hh=hh, pp=pp):
                        p.wait(e, dep_v[0], dep_v[1])
                        p.wait(e, "sA")
                        p.wait(e, "sV")
                        for kc in range(8):
                            ii = e.matmul(pp[:, 0:512], xxb[0][:, kc, L * st:L * st + L],
                                          wv[:, kc, 512 * hh:512 * hh + 512],
                                          start=(kc == 0), stop=(kc == 7))
                        p.inc(ii, "sT")

                    @blk.scalar
                    def _(e, st=st, hh=hh, pp=pp):
                        p.wait(e, "sT")
                        p.inc(e.activation(out=vtb[:, st, 512 * hh:512 * hh + 512],
                                           in_=pp[:, 0:512], func=AF.Identity), "sA")
            sA_v = p.cnt["sA"]

            @blk.sync
            def _(e):
                p.wait(e, "sA", sA_v)
                e.wait_ge(p.sems["qO"], p.cnt["qO"])
                for st in range(CPS):
                    p.inc(e.dma_start(out=vq_o[c0 + L * st:c0 + L * st + L, :], in_=vtb[:, st, :]), "qO", 16)

            @blk.tensor
            def _(e):
                p.wait(e, "sV", sV_k)
                for tt in range(8):
                    ii = e.matmul(psX[0:16, 0:TSL], u16s[:, tt, :], rkb[:, tt, :],
                                  start=(tt == 0), stop=(tt == 7))
                p.inc(ii, "sT")

            @blk.scalar
            def _(e):
                p.wait(e, "sT")
                p.inc(e.activation(out=bsl[:], in_=psX[0:16, 0:TSL], func=AF.Identity), "sA")
            sA_b = p.cnt["sA"]

            @blk.sync
            def _(e):
                p.wait(e, "sA", sA_b)
                e.wait_ge(p.sems["qO"], p.cnt["qO"])
                p.inc(e.dma_start(out=b_o[:, slt], in_=bsl[:]), "qO", 16)

            for cc in range(CPS):
                gc = s * CPS + cc
                cur = gc % 2
                nxt = (gc + 1) % 2

                qtp = "qT%d" % (cc % 2)

                @blk.scalar
                def _(e, cc=cc, qtp=qtp):
                    p.wait(e, "sV", sV_k)
                    p.wait(e, "sT")
                    e.wait_ge(p.sems[qtp], p.cnt[qtp])
                    for g in range(8):
                        p.inc(e.dma_start(out=khT[:, g, :],
                                          in_=khb[:, g, L * cc:L * cc + L],
                                          transpose=True), qtp, 16)
                qT_k = p.cnt[qtp]

                @blk.tensor
                def _(e, cc=cc, qT_k=qT_k, qtp=qtp):
                    e.wait_ge(p.sems[qtp], qT_k)
                    p.wait(e, "sV")
                    for h in range(16):
                        ii = e.matmul(psU[64 * (h % 2):64 * (h % 2) + 64, 64 * (h // 2):64 * (h // 2) + 64],
                                      khT[:, h // 2, 64 * (h % 2):64 * (h % 2) + 64],
                                      vtb[:, cc, 64 * h:64 * h + 64], start=True, stop=True)
                    p.inc(ii, "sT")
                sT_U = p.cnt["sT"]

                @blk.sync
                def _(e, gc=gc, cur=cur):
                    p.wait(e, "sV")
                    e.wait_ge(p.sems["qO"], p.cnt["qO"])
                    p.inc(e.dma_start(out=traj_o[:, 512 * gc:512 * gc + 512],
                                      in_=sown[cur].rearrange("p g j -> p (g j)")), "qO", 16)
                    p.inc(e.dma_start(out=dpre_o[:, 8 * gc:8 * gc + 8], in_=dprer[cur][:]), "qO", 16)
                qS_st = p.cnt["qO"]

                @blk.vector
                def _(e, cc=cc, cur=cur, nxt=nxt, sT_U=sT_U, qS_st=qS_st):
                    p.wait(e, "sT", sT_U)
                    e.wait_ge(p.sems["qO"], qS_st)
                    for g in range(8):
                        e.scalar_tensor_tensor(out=sown[nxt][:, g, :], in0=sown[cur][:, g, :],
                                               scalar=dtot[:, g, cc:cc + 1],
                                               in1=psU[:, 64 * g:64 * g + 64],
                                               op0=ALU.mult, op1=ALU.add)
                    p.inc(e.tensor_tensor(out=dprer[nxt][:], in0=dprer[cur][:],
                                          in1=dtot[:, :, cc], op=ALU.mult), "sV")

        fin = NCH % 2

        @blk.vector
        def _(e):
            p.inc(e.tensor_reduce(out=pxn_t[:, NSL:NSL + 1], in_=pxn_t[:, 0:NSL],
                                  axis=AX.X, op=ALU.max), "sV")
        sV_pxn = p.cnt["sV"]

        @blk.sync
        def _(e):
            p.wait(e, "sV", sV_pxn)
            p.inc(e.dma_start(out=pxn_o[:, :], in_=pxn_t[:, NSL:NSL + 1]), "qO", 16)
            p.inc(e.dma_start(out=utot_o[:, :], in_=sown[fin].rearrange("p g j -> p (g j)")), "qO", 16)
            for q in ("qS", "qA", "qP", "qO", "qB0", "qB1", "qT0", "qT1"):
                e.wait_ge(p.sems[q], p.cnt[q])

    return p


def _sigma_cols(p, blk, maxrow_ap, nmax, NV, scr_dram, out_sb, qname="qA", bufs=None):
    nc = p.nc
    if bufs is None:
        mrow = p.sb(f"sgm_{id(maxrow_ap) % 997}", [1, nmax * NV])
        mred = p.sb(f"sgr_{id(maxrow_ap) % 997}", [1, 4 * nmax])
    else:
        mrow, mred = bufs

    @blk.scalar
    def _(e):
        p.inc(e.dma_start(out=mrow[:], in_=maxrow_ap[:, :]), qname, 16)
    qv0 = p.cnt[qname]

    @blk.vector
    def _(e):
        e.wait_ge(p.sems[qname], qv0)
        for i in range(nmax):
            p.inc(e.tensor_reduce(out=mred[:, 4 * i:4 * i + 1], in_=mrow[:, NV * i:NV * (i + 1)],
                                  axis=AX.X, op=ALU.max), "sV")
            p.wait(e, "sV")
            p.inc(e.tensor_scalar(out=mred[:, 4 * i + 1:4 * i + 2], in0=mred[:, 4 * i:4 * i + 1],
                                  scalar1=1.0 / QMAX, scalar2=1e-12, op0=ALU.mult, op1=ALU.max), "sV")
            p.wait(e, "sV")
            p.inc(e.reciprocal(out=mred[:, 4 * i + 2:4 * i + 3], in_=mred[:, 4 * i + 1:4 * i + 2]), "sV")
            p.wait(e, "sV")
            p.inc(e.tensor_scalar(out=mred[:, 4 * i + 3:4 * i + 4], in0=mred[:, 4 * i + 1:4 * i + 2],
                                  scalar1=1.0 / 65536.0, scalar2=None, op0=ALU.mult), "sV")
            p.wait(e, "sV")

    @blk.scalar
    def _(e):
        p.wait(e, "sV")
        p.inc(e.dma_start(out=scr_dram[0:1, :], in_=mred[:]), qname, 16)
        e.wait_ge(p.sems[qname], p.cnt[qname])
        p.inc(e.dma_start(out=out_sb[:], in_=scr_dram[0:1, :].broadcast_to([128, 4 * nmax])), qname, 16)
    return p.cnt[qname]


def emit_xq(p, blk, TC, TSL, x_fm, murstd, sgc, xq, bufs, sg_ready=None):
    NSL = TC // TSL
    xfs, mue, rse, xns, tq = bufs["xfs"], bufs["mue"], bufs["rse"], bufs["xns"], bufs["tq"]
    for s in range(NSL):
        c0 = s * TSL

        @blk.sync
        def _(e, s=s, c0=c0):
            if s > 0:
                p.wait(e, "sV")
            p.inc(e.dma_start(out=xfs[:, :, 0:TSL], in_=fm3(x_fm)[:, :, c0 + 1:c0 + TSL + 1]), "qS", 16)
            p.inc(e.dma_start(out=mue[:, 0:TSL], in_=murstd[0:1, c0 + 1:c0 + TSL + 1].broadcast_to([128, TSL])), "qS", 16)
            p.inc(e.dma_start(out=rse[:, 0:TSL], in_=murstd[1:2, c0 + 1:c0 + TSL + 1].broadcast_to([128, TSL])), "qS", 16)
        qS_ld = p.cnt["qS"]

        @blk.vector
        def _(e, s=s, c0=c0, qS_ld=qS_ld):
            e.wait_ge(p.sems["qS"], qS_ld)
            if sg_ready is not None:
                e.wait_ge(p.sems[sg_ready[0]], sg_ready[1])
            p.inc(e.tensor_tensor(out=xns[:, :, 0:TSL], in0=xfs[:, :, 0:TSL],
                                  in1=mue[:, 0:TSL].rearrange("p (o c) -> p o c", o=1).broadcast_to([128, 8, TSL]),
                                  op=ALU.subtract), "sV")
            p.wait(e, "sV")
            p.inc(e.tensor_tensor(out=xns[:, :, 0:TSL], in0=xns[:, :, 0:TSL],
                                  in1=rse[:, 0:TSL].rearrange("p (o c) -> p o c", o=1).broadcast_to([128, 8, TSL]),
                                  op=ALU.mult), "sV")
            p.wait(e, "sV")
            p.inc(e.tensor_scalar(out=tq[:, :, 0:TSL], in0=xns[:, :, 0:TSL], scalar1=sgc[:, 2:3],
                                  scalar2=RND, op0=ALU.mult, op1=ALU.add), "sV")
            p.wait(e, "sV")
            p.inc(e.tensor_scalar(out=xq[:, :, c0:c0 + TSL], in0=tq[:, :, 0:TSL], scalar1=RND,
                                  scalar2=sgc[:, 1:2], op0=ALU.subtract, op1=ALU.mult), "sV")


def emit_dequant(p, blk, W_dram, wdq_dram, O, tiles_sb, rnd_col, qd="qA"):
    CW = W_dram.shape[1]
    NB = CW // BLK
    NT = O // 128
    wtile, wq, babs, sblk, rblk = tiles_sb
    qp_mark = {}
    q_mark = {}
    m_mark = {}

    def stageA(j):
        bi = j % 2

        @blk.scalar
        def _(e, j=j, bi=bi):
            if j >= 2:
                p.wait(e, "sV", q_mark[j - 2])
            e.wait_ge(p.sems[qd], p.cnt[qd])
            p.inc(e.dma_start(out=wtile[bi][:], in_=W_dram[128 * j:128 * (j + 1), :]), qd, 16)
        qv = p.cnt[qd]

        @blk.vector
        def _(e, j=j, bi=bi, qv=qv):
            e.wait_ge(p.sems[qd], qv)
            if j >= 2:
                p.wait(e, "sV", q_mark[j - 2])
            p.inc(e.tensor_reduce(out=babs[bi][:], in_=wtile[bi].rearrange("p (b k) -> p b k", k=BLK),
                                  axis=AX.X, op=ALU.max, apply_absolute_value=True), "sV")
            p.wait(e, "sV")
            p.inc(e.tensor_scalar(out=sblk[bi][:], in0=babs[bi][:], scalar1=1.0 / WQMAX, scalar2=1e-12,
                                  op0=ALU.mult, op1=ALU.max), "sV")
            p.wait(e, "sV")
            p.inc(e.reciprocal(out=rblk[bi][:], in_=sblk[bi][:]), "sV")
            p.wait(e, "sV")
            p.inc(e.tensor_tensor(out=wtile[bi].rearrange("p (b k) -> p b k", k=BLK),
                                  in0=wtile[bi].rearrange("p (b k) -> p b k", k=BLK),
                                  in1=rblk[bi].rearrange("p (b o) -> p b o", o=1).broadcast_to([128, NB, BLK]),
                                  op=ALU.mult), "sV")
        m_mark[j] = p.cnt["sV"]

    def stageB(j):
        bi = j % 2
        sV_m = m_mark[j]

        @blk.scalar
        def _(e, j=j, bi=bi, sV_m=sV_m):
            p.wait(e, "sV", sV_m)
            p.inc(e.activation(out=wtile[bi][:], in_=wtile[bi][:], func=AF.Identity,
                               bias=rnd_col, scale=1.0), "sA")
        sA_m = p.cnt["sA"]

        @blk.vector
        def _(e, j=j, bi=bi, sA_m=sA_m):
            p.wait(e, "sA", sA_m)
            if j >= 2:
                e.wait_ge(p.sems[qp_mark[j - 2][0]], qp_mark[j - 2][1])
            p.inc(e.scalar_tensor_tensor(out=wq[bi].rearrange("p (b k) -> p b k", k=BLK),
                                         in0=wtile[bi].rearrange("p (b k) -> p b k", k=BLK),
                                         scalar=RND, in1=sblk[bi].rearrange("p (b o) -> p b o", o=1).broadcast_to([128, NB, BLK]),
                                         op0=ALU.subtract, op1=ALU.mult), "sV")
        q_mark[j] = p.cnt["sV"]

        @blk.gpsimd
        def _(e, j=j, bi=bi, sV_q=q_mark[j]):
            p.wait(e, "sV", sV_q)
            qps = "qP%d" % bi
            p.inc(e.dma_start(out=wdq_dram[128 * j:128 * (j + 1), :], in_=wq[bi][:]), qps, 16)
        qp_mark[j] = ("qP%d" % bi, p.cnt["qP%d" % bi])

    for it in range(NT + 1):
        if it < NT:
            stageA(it)
        if it >= 1:
            stageB(it - 1)
    return p.cnt["qP"]


def emit_upgate_mm(p, blk, wdq_dram, wT, xq, TC, psD, utile, pxacc, col0,
                   i16_out=None, qi16=None, sgu=None, sig_chain=None, qw="qA",
                   sg_ready=None):
    NT = FFI // 128
    NTC = TC // 512
    for kc in range(8):
        @blk.scalar
        def _(e, kc=kc):
            if kc == 0:
                p.wait(e, "sT")
            for qq in ("qP", "qP0", "qP1"):
                e.wait_ge(p.sems[qq], p.cnt[qq])
            p.inc(e.dma_start(out=wT[:, kc, :], in_=wdq_dram[0:FFI, 128 * kc:128 * (kc + 1)],
                              transpose=True), qw, 16)
    qv_w = p.cnt[qw]
    sV_xq = p.cnt["sV"]
    evac_done = {-2: p.cnt["sA"], -1: p.cnt["sA"]}
    cons_done = {-2: p.cnt["sV"], -1: p.cnt["sV"]}
    ship_done = {-2: ("qO0", p.cnt.get("qO0", 0)), -1: ("qO1", p.cnt.get("qO1", 0))}
    for ot in range(NT):
        for tch in range(NTC):
            idx = ot * NTC + tch
            pp = psD[idx % 2]

            @blk.tensor
            def _(e, ot=ot, tch=tch, pp=pp, idx=idx):
                e.wait_ge(p.sems[qw], qv_w)
                p.wait(e, "sV", sV_xq)
                p.wait(e, "sA", evac_done[idx - 2])
                for kc in range(8):
                    ii = e.matmul(pp[:, 0:512], wT[:, kc, 128 * ot:128 * ot + 128],
                                  xq[:, kc, 512 * tch:512 * tch + 512],
                                  start=(kc == 0), stop=(kc == 7))
                p.inc(ii, "sT")

            @blk.scalar
            def _(e, ot=ot, tch=tch, pp=pp, idx=idx):
                p.wait(e, "sT")
                p.wait(e, "sV", cons_done[idx - 2])
                if i16_out is None:
                    p.inc(e.activation(out=utile[idx % 2][:], in_=pp[:, 0:512], func=AF.Identity), "sA")
                else:
                    if sg_ready is not None:
                        e.wait_ge(p.sems[sg_ready[0]], sg_ready[1])
                    p.inc(e.activation(out=utile[idx % 2][:], in_=pp[:, 0:512], func=AF.Identity,
                                       bias=p.constcol(RND), scale=sgu[:, 2:3]), "sA")
            evac_done[idx] = p.cnt["sA"]
            sA_e = p.cnt["sA"]

            @blk.vector
            def _(e, ot=ot, tch=tch, idx=idx, sA_e=sA_e):
                p.wait(e, "sA", sA_e)
                ut = utile[idx % 2]
                if i16_out is None:
                    p.inc(e.tensor_reduce(out=pxacc[:, col0 + idx:col0 + idx + 1], in_=ut[:],
                                          axis=AX.X, op=ALU.max, apply_absolute_value=True), "sV")
                else:
                    e.wait_ge(p.sems[ship_done[idx - 2][0]], ship_done[idx - 2][1])
                    p.inc(e.tensor_scalar(out=qi16[idx % 2][:], in0=ut[:], scalar1=RND, scalar2=None,
                                          op0=ALU.subtract), "sV")
            if i16_out is not None:
                sV_q = p.cnt["sV"]

                qos = "qO%d" % (idx % 2)

                @blk.sync
                def _(e, ot=ot, tch=tch, idx=idx, sV_q=sV_q, qos=qos):
                    p.wait(e, "sV", sV_q)
                    p.inc(e.dma_start(out=i16_out[128 * ot:128 * (ot + 1), 512 * tch:512 * (tch + 1)],
                                      in_=qi16[idx % 2][:]), qos, 16)
                ship_done[idx] = (qos, p.cnt[qos])
                if sig_chain is not None:
                    gqf, sigt, sigq, g2t = sig_chain
                    bi = idx % 2

                    @blk.vector
                    def _(e, idx=idx, bi=bi, sV_q=sV_q):
                        p.wait(e, "sV", sV_q)
                        p.inc(e.tensor_scalar(out=gqf[bi][:], in0=qi16[bi][:], scalar1=sgu[:, 1:2],
                                              scalar2=None, op0=ALU.mult), "sV")
                    sV_g = p.cnt["sV"]

                    @blk.scalar
                    def _(e, idx=idx, bi=bi, sV_g=sV_g):
                        p.wait(e, "sV", sV_g)
                        p.inc(e.activation(out=sigt[bi][:], in_=gqf[bi][:], func=AF.Tanh, scale=0.5), "sA")
                    sA_s = p.cnt["sA"]

                    @blk.vector
                    def _(e, idx=idx, bi=bi, sA_s=sA_s):
                        p.wait(e, "sA", sA_s)
                        p.inc(e.tensor_scalar(out=sigq[bi][:], in0=sigt[bi][:], scalar1=32768.0, scalar2=32768.0 + RND,
                                              op0=ALU.mult, op1=ALU.add), "sV")
                        p.wait(e, "sV")
                        p.inc(e.tensor_scalar(out=sigq[bi][:], in0=sigq[bi][:], scalar1=RND + 65535.0,
                                              scalar2=RND, op0=ALU.min, op1=ALU.subtract), "sV")
                        p.wait(e, "sV")
                        p.inc(e.scalar_tensor_tensor(out=g2t[bi][:], in0=sigq[bi][:], scalar=1.0 / 65536.0,
                                                     in1=gqf[bi][:], op0=ALU.mult, op1=ALU.mult), "sV")
                        p.wait(e, "sV")
                        p.inc(e.tensor_reduce(out=pxacc[:, col0 + idx:col0 + idx + 1], in_=g2t[bi][:],
                                              axis=AX.X, op=ALU.max, apply_absolute_value=True), "sV")
            cons_done[idx] = p.cnt["sV"]
    return


def build_L2(TC=2048, TSL=512):
    TTP = ceil_div(TC + 1, 128) * 128
    NTC = TC // 512
    p = Prog("L2")
    nc = p.nc
    x_fm = p.din("x_fm", [C, TC + 1])
    murstd = p.din("murstd", [2, TTP])
    pxn_row = p.din("pxn_row", [1, 1024])
    w_up = p.din("w_up", [FFI, C])
    w_gate = p.din("w_gate", [FFI, C])
    pxu_o = p.dout("pxu", [128, 1])
    pxg_o = p.dout("pxg", [128, 1])
    wdqu_o = p.dout("wdq_up", [FFI, C], BF16)
    wdqg_o = p.dout("wdq_gate", [FFI, C], BF16)
    sgscr = nc.dram_tensor("sgscr", [1, 4], F32, kind="Internal").ap()

    sgx = p.sb("sgx", [128, 4])
    xq = p.sb("xq", [128, 8, TC], BF16)
    wT = p.sb("wT", [128, 8, FFI], BF16)
    wtile = [p.sb(f"wt{i}", [128, C]) for i in range(2)]
    wq2 = [p.sb(f"wq{i}", [128, C], BF16) for i in range(2)]
    babs = [p.sb(f"babs{i}", [128, C // BLK]) for i in range(2)]
    sblk = [p.sb(f"sblk{i}", [128, C // BLK]) for i in range(2)]
    rblk = [p.sb(f"rblk{i}", [128, C // BLK]) for i in range(2)]
    utile = [p.sb(f"ut{i}", [128, 512]) for i in range(2)]
    pxacc = p.sb("pxacc", [128, 2 * 24 * NTC + 2])
    bufs = dict(xfs=p.sb("xfs", [128, 8, TSL]), mue=p.sb("mue", [128, TSL]),
                rse=p.sb("rse", [128, TSL]), xns=p.sb("xns", [128, 8, TSL]),
                tq=p.sb("tq", [128, 8, TSL]))
    rnd_col = p.constcol(RND)
    psD = [p.ps(f"psD{i}", [128, 512]) for i in range(2)]

    for s in ("qS", "qA", "qP", "qP0", "qP1", "qG", "qD", "qW", "qO", "sV", "sA", "sP", "sT"):
        p.sem(s)

    with nc.Block() as blk:
        @blk.vector
        def _(e):
            p.emit_const_memsets(e, "sV")
        qsg = _sigma_cols(p, blk, pxn_row, 1, 1024, sgscr, sgx, "qG")
        emit_xq(p, blk, TC, TSL, x_fm, murstd, sgx, xq, bufs, sg_ready=("qG", qsg))
        emit_dequant(p, blk, w_up, wdqu_o, FFI, (wtile, wq2, babs, sblk, rblk), rnd_col, "qD")
        emit_upgate_mm(p, blk, wdqu_o, wT, xq, TC, psD, utile, pxacc, 0, qw="qW")
        emit_dequant(p, blk, w_gate, wdqg_o, FFI, (wtile, wq2, babs, sblk, rblk), rnd_col, "qD")
        emit_upgate_mm(p, blk, wdqg_o, wT, xq, TC, psD, utile, pxacc, 24 * NTC, qw="qW")

        NCOL = 24 * NTC

        @blk.vector
        def _(e):
            p.wait(e, "sA")
            p.wait(e, "sV")
            p.inc(e.tensor_reduce(out=pxacc[:, 2 * NCOL:2 * NCOL + 1], in_=pxacc[:, 0:NCOL],
                                  axis=AX.X, op=ALU.max), "sV")
            p.wait(e, "sV")
            p.inc(e.tensor_reduce(out=pxacc[:, 2 * NCOL + 1:2 * NCOL + 2], in_=pxacc[:, NCOL:2 * NCOL],
                                  axis=AX.X, op=ALU.max), "sV")
        sV_fin = p.cnt["sV"]

        @blk.sync
        def _(e):
            p.wait(e, "sV", sV_fin)
            p.inc(e.dma_start(out=pxu_o[:, :], in_=pxacc[:, 2 * NCOL:2 * NCOL + 1]), "qS", 16)
            p.inc(e.dma_start(out=pxg_o[:, :], in_=pxacc[:, 2 * NCOL + 1:2 * NCOL + 2]), "qS", 16)
            for q in ("qS", "qG", "qD", "qW", "qP", "qP0", "qP1", "qO"):
                e.wait_ge(p.sems[q], p.cnt[q])
    return p


def build_L3(TC=2048, TSL=512):
    TTP = ceil_div(TC + 1, 128) * 128
    NTC = TC // 512
    p = Prog("L3")
    nc = p.nc
    x_fm = p.din("x_fm", [C, TC + 1])
    murstd = p.din("murstd", [2, TTP])
    pxn_row = p.din("pxn_row", [1, 1024])
    pxug_row = p.din("pxug_row", [1, 2048])
    wdqu = p.din("wdq_up", [FFI, C], BF16)
    wdqg = p.din("wdq_gate", [FFI, C], BF16)
    upq_o = p.dout("upq", [FFI, TC], I16)
    gq_o = p.dout("gq", [FFI, TC], I16)
    pxg2_o = p.dout("pxg2", [128, 1])
    sgscr = nc.dram_tensor("sgscr", [1, 4], F32, kind="Internal").ap()
    sgscr2 = nc.dram_tensor("sgscr2", [1, 8], F32, kind="Internal").ap()

    sgx = p.sb("sgx", [128, 4])
    sgug = p.sb("sgug", [128, 8])
    xq = p.sb("xq", [128, 8, TC], BF16)
    wT = p.sb("wT", [128, 8, FFI], BF16)
    utile = [p.sb(f"ut{i}", [128, 512]) for i in range(2)]
    qi16 = [p.sb(f"qi{i}", [128, 512], I16) for i in range(2)]
    gqf = [p.sb(f"gqf{i}", [128, 512]) for i in range(2)]
    sigt = [p.sb(f"sigt{i}", [128, 512]) for i in range(2)]
    sigq = [p.sb(f"sigq{i}", [128, 512]) for i in range(2)]
    g2t = [p.sb(f"g2t{i}", [128, 512]) for i in range(2)]
    pxacc = p.sb("pxacc", [128, 24 * NTC + 1])
    bufs = dict(xfs=p.sb("xfs", [128, 8, TSL]), mue=p.sb("mue", [128, TSL]),
                rse=p.sb("rse", [128, TSL]), xns=p.sb("xns", [128, 8, TSL]),
                tq=p.sb("tq", [128, 8, TSL]))
    rnd_col = p.constcol(RND)
    psD = [p.ps(f"psD{i}", [128, 512]) for i in range(2)]

    for s in ("qS", "qA", "qP", "qP0", "qP1", "qG", "qG2", "qW", "qO", "qO0", "qO1",
              "sV", "sA", "sP", "sT"):
        p.sem(s)

    with nc.Block() as blk:
        @blk.vector
        def _(e):
            p.emit_const_memsets(e, "sV")
        qs1 = _sigma_cols(p, blk, pxn_row, 1, 1024, sgscr, sgx, "qG")
        qsg = _sigma_cols(p, blk, pxug_row, 2, 1024, sgscr2, sgug, "qG2")
        emit_xq(p, blk, TC, TSL, x_fm, murstd, sgx, xq, bufs, sg_ready=("qG", qs1))
        emit_upgate_mm(p, blk, wdqu, wT, xq, TC, psD, utile, pxacc, 0,
                       i16_out=upq_o, qi16=qi16, sgu=sgug[:, 0:4], qw="qW",
                       sg_ready=("qG2", qsg))
        emit_upgate_mm(p, blk, wdqg, wT, xq, TC, psD, utile, pxacc, 0,
                       i16_out=gq_o, qi16=qi16, sgu=sgug[:, 4:8],
                       sig_chain=(gqf, sigt, sigq, g2t), qw="qW",
                       sg_ready=("qG2", qsg))
        NCOL = 24 * NTC

        @blk.vector
        def _(e):
            p.wait(e, "sA")
            p.wait(e, "sV")
            p.inc(e.tensor_reduce(out=pxacc[:, NCOL:NCOL + 1], in_=pxacc[:, 0:NCOL],
                                  axis=AX.X, op=ALU.max), "sV")
        sV_fin = p.cnt["sV"]

        @blk.sync
        def _(e):
            p.wait(e, "sV", sV_fin)
            p.inc(e.dma_start(out=pxg2_o[:, :], in_=pxacc[:, NCOL:NCOL + 1]), "qS", 16)
            for q in ("qS", "qG", "qG2", "qW", "qO", "qO0", "qO1", "qP0", "qP1"):
                e.wait_ge(p.sems[q], p.cnt[q])
    return p


def build_L4(TC=2048):
    NT = FFI // 128
    NTC = TC // 512
    p = Prog("L4")
    nc = p.nc
    upq = p.din("upq", [FFI, TC], I16)
    gq = p.din("gq", [FFI, TC], I16)
    pxug_row = p.din("pxug_row", [1, 2048])
    pxg2_row = p.din("pxg2_row", [1, 1024])
    pxo_o = p.dout("pxo", [128, 1])
    sgscr2 = nc.dram_tensor("sgscr2", [1, 8], F32, kind="Internal").ap()
    sgscr3 = nc.dram_tensor("sgscr3", [1, 4], F32, kind="Internal").ap()

    sgug = p.sb("sgug", [128, 8])
    sg3 = p.sb("sg3", [128, 4])
    ui = [p.sb(f"ui{i}", [128, 512], I16) for i in range(2)]
    gi = [p.sb(f"gi{i}", [128, 512], I16) for i in range(2)]
    gqf = [p.sb(f"gqf{i}", [128, 512]) for i in range(2)]
    sigt = [p.sb(f"sigt{i}", [128, 512]) for i in range(2)]
    sigq = [p.sb(f"sigq{i}", [128, 512]) for i in range(2)]
    g2t = [p.sb(f"g2t{i}", [128, 512]) for i in range(2)]
    g2q = [p.sb(f"g2q{i}", [128, 512]) for i in range(2)]
    ot = [p.sb(f"ot{i}", [128, 512]) for i in range(2)]
    NTL = NT * NTC
    pxacc = p.sb("pxacc", [128, NTL + 1])
    rnd_col = p.constcol(RND)

    for s in ("qS", "qG", "qG2", "qL0", "qL1", "sV", "sA", "sP", "sT"):
        p.sem(s)

    with nc.Block() as blk:
        @blk.vector
        def _(e):
            p.emit_const_memsets(e, "sV")
        qg1 = _sigma_cols(p, blk, pxug_row, 2, 1024, sgscr2, sgug, "qG")
        qg2 = _sigma_cols(p, blk, pxg2_row, 1, 1024, sgscr3, sg3, "qG2")

        chain_end = {}
        for j in range(NT):
            for tc4 in range(NTC):
                idx = j * NTC + tc4
                bi = idx % 2
                ql = "qL%d" % bi

                @blk.sync
                def _(e, j=j, tc4=tc4, bi=bi, idx=idx, ql=ql):
                    if idx >= 2:
                        p.wait(e, "sV", chain_end[idx - 2])
                    p.inc(e.dma_start(out=ui[bi][:], in_=upq[128 * j:128 * (j + 1), 512 * tc4:512 * (tc4 + 1)]), ql, 16)
                    p.inc(e.dma_start(out=gi[bi][:], in_=gq[128 * j:128 * (j + 1), 512 * tc4:512 * (tc4 + 1)]), ql, 16)
                qld = p.cnt[ql]

                @blk.vector
                def _(e, bi=bi, idx=idx, qld=qld, ql=ql):
                    e.wait_ge(p.sems[ql], qld)
                    if idx == 0:
                        e.wait_ge(p.sems["qG"], qg1)
                        e.wait_ge(p.sems["qG2"], qg2)
                    if idx >= 1:
                        p.wait(e, "sV", chain_end.get(idx - 1, 0))
                    p.inc(e.tensor_scalar(out=gqf[bi][:], in0=gi[bi][:], scalar1=sgug[:, 5:6],
                                          scalar2=None, op0=ALU.mult), "sV")
                sV_g = p.cnt["sV"]

                @blk.scalar
                def _(e, bi=bi, sV_g=sV_g):
                    p.wait(e, "sV", sV_g)
                    p.inc(e.activation(out=sigt[bi][:], in_=gqf[bi][:], func=AF.Tanh, scale=0.5), "sA")
                sA_s = p.cnt["sA"]

                @blk.vector
                def _(e, bi=bi, idx=idx, sA_s=sA_s):
                    p.wait(e, "sA", sA_s)
                    p.inc(e.tensor_scalar(out=sigq[bi][:], in0=sigt[bi][:], scalar1=32768.0, scalar2=32768.0 + RND,
                                          op0=ALU.mult, op1=ALU.add), "sV")
                    p.wait(e, "sV")
                    p.inc(e.tensor_scalar(out=sigq[bi][:], in0=sigq[bi][:], scalar1=RND + 65535.0,
                                          scalar2=RND, op0=ALU.min, op1=ALU.subtract), "sV")
                    p.wait(e, "sV")
                    p.inc(e.scalar_tensor_tensor(out=g2t[bi][:], in0=sigq[bi][:], scalar=1.0 / 65536.0,
                                                 in1=gqf[bi][:], op0=ALU.mult, op1=ALU.mult), "sV")
                    p.wait(e, "sV")
                    p.inc(e.tensor_scalar(out=g2q[bi][:], in0=g2t[bi][:], scalar1=sg3[:, 2:3],
                                          scalar2=RND, op0=ALU.mult, op1=ALU.add), "sV")
                    p.wait(e, "sV")
                    p.inc(e.tensor_scalar(out=g2q[bi][:], in0=g2q[bi][:], scalar1=RND,
                                          scalar2=sg3[:, 1:2], op0=ALU.subtract, op1=ALU.mult), "sV")
                    p.wait(e, "sV")
                    p.inc(e.scalar_tensor_tensor(out=ot[bi][:], in0=ui[bi][:], scalar=sgug[:, 1:2],
                                                 in1=g2q[bi][:], op0=ALU.mult, op1=ALU.mult), "sV")
                    p.wait(e, "sV")
                    p.inc(e.tensor_reduce(out=pxacc[:, idx:idx + 1], in_=ot[bi][:],
                                          axis=AX.X, op=ALU.max, apply_absolute_value=True), "sV")
                chain_end[idx] = p.cnt["sV"]

        @blk.vector
        def _(e):
            p.wait(e, "sV")
            p.wait(e, "sA")
            p.inc(e.tensor_reduce(out=pxacc[:, NTL:NTL + 1], in_=pxacc[:, 0:NTL],
                                  axis=AX.X, op=ALU.max), "sV")
        sV_fin = p.cnt["sV"]

        @blk.sync
        def _(e):
            p.wait(e, "sV", sV_fin)
            p.inc(e.dma_start(out=pxo_o[:, :], in_=pxacc[:, NTL:NTL + 1]), "qS", 16)
            for q in ("qS", "qG", "qG2", "qL0", "qL1"):
                e.wait_ge(p.sems[q], p.cnt[q])
    return p


def build_L5(TC=2048):
    NCH = TC // L
    NQ = TC // 512
    p = Prog("L5")
    nc = p.nc
    x_fm = p.din("x_fm", [C, TC + 1])
    qt = p.din("qt", [C, TC], BF16)
    kt = p.din("kt", [C, TC], BF16)
    vq = p.din("vq", [TC, C], BF16)
    xg = p.din("xg", [C, TC], BF16)
    b_rowT = p.din("b_rowT", [TC, 16])
    utotP = p.din("utotP", [128, 512])
    traj = p.din("traj", [128, NCH * 512])
    dpre = p.din("dpre", [128, NCH * 8])
    WgT = p.din("WgT", [C, C])
    WoT = p.din("WoT", [C, C])
    w_down = p.din("w_down", [C, FFI])
    upq = p.din("upq", [FFI, TC], I16)
    gq = p.din("gq", [FFI, TC], I16)
    pxug_row = p.din("pxug_row", [1, 2048])
    pxg2_row = p.din("pxg2_row", [1, 1024])
    pxo_row = p.din("pxo_row", [1, 1024])
    gam_row = p.din("gam_row", [1, C])
    bet_row = p.din("bet_row", [1, C])
    tri = p.din("tri", [128, 128])
    ident = p.din("ident", [128, 128])
    out_o = p.dout("out_fm", [C, TC])
    attn_scr = nc.dram_tensor("attn_scr", [C, TC], F32, kind="Internal").ap()
    wdqd_scr = nc.dram_tensor("wdqd_scr", [C, FFI], BF16, kind="Internal").ap()
    sgscr2 = nc.dram_tensor("sgscr2", [1, 8], F32, kind="Internal").ap()
    sgscr3 = nc.dram_tensor("sgscr3", [1, 4], F32, kind="Internal").ap()
    sgscr4 = nc.dram_tensor("sgscr4", [1, 4], F32, kind="Internal").ap()

    sgug = p.sb("sgug", [128, 8])
    sg3 = p.sb("sg3", [128, 4])
    sg4 = p.sb("sg4", [128, 4])
    sgb1 = (p.sb("sgb1a", [1, 2048]), p.sb("sgb1b", [1, 8]))
    sgb2 = (p.sb("sgb2a", [1, 1024]), p.sb("sgb2b", [1, 4]))
    sgb3 = (p.sb("sgb3a", [1, 1024]), p.sb("sgb3b", [1, 4]))
    rnd_col = p.constcol(RND)
    epsg_col = p.constcol(16e-5)

    for s in ("qS", "qA", "qP", "qP0", "qP1", "qG", "qG2", "qG3", "qO", "qO0", "qO1",
              "qL0", "qL1", "qC0", "qC1", "qW", "qD", "qT0", "qT1", "sV", "sA", "sP", "sT"):
        p.sem(s)

    from contextlib import ExitStack
    subA = ExitStack()

    def sba(name, shape, dtype=F32):
        return subA.enter_context(nc.sbuf_tensor(name, list(shape), dtype))

    wg = sba("wg", [128, 8, C], BF16)
    wo = sba("wo", [128, 8, C], BF16)
    u0s = sba("u0s", [128, 8, 64])
    gex = sba("gex", [128, C])
    bex = sba("bex", [128, C])
    trim = sba("trim", [128, 128])
    idn = sba("idn", [128, 128])
    stgw = sba("stgw", [128, 2, C])
    qtc = [sba(f"qtc{i}", [128, 8, 128], BF16) for i in range(4)]
    ktc = [sba(f"ktc{i}", [128, 8, 128], BF16) for i in range(4)]
    vqc = [sba(f"vqc{i}", [128, C], BF16) for i in range(4)]
    xgc = [sba(f"xgc{i}", [128, 8, 128], BF16) for i in range(4)]
    trajc = [sba(f"trajc{i}", [128, 512]) for i in range(4)]
    dpc = [sba(f"dpc{i}", [128, 8]) for i in range(4)]
    bTc = [sba(f"bTc{i}", [128, 16]) for i in range(4)]
    scg = [sba(f"scg{i}", [128, 8, 64], BF16) for i in range(4)]
    amb = [sba(f"amb{i}", [128, 16 * 128], BF16) for i in range(4)]
    ytc = [sba(f"ytc{i}", [128, C]) for i in range(4)]
    ysq = [sba(f"ysq{i}", [128, C]) for i in range(4)]
    gnt = [sba(f"gnt{i}", [128, 80]) for i in range(4)]
    gtc = [sba(f"gtc{i}", [128, C]) for i in range(4)]
    gth = [sba(f"gth{i}", [128, 512], BF16) for i in range(4)]
    gth1 = [sba(f"gth1{i}", [128, 512]) for i in range(4)]
    ygb = [sba(f"ygb{i}", [128, C], BF16) for i in range(4)]
    ygf = [sba(f"ygf{i}", [128, 8, 128], BF16) for i in range(4)]
    afm = [sba(f"afm{i}", [128, 128]) for i in range(2)]

    sub_ps = ExitStack()
    psA2 = [sub_ps.enter_context(nc.psum_tensor(f"psA{i}", [128, 512], F32)) for i in range(2)]
    psY2 = [sub_ps.enter_context(nc.psum_tensor(f"psY{i}", [128, 512], F32)) for i in range(2)]
    psWo = [sub_ps.enter_context(nc.psum_tensor(f"psWo{i}", [128, 512], F32)) for i in range(2)]
    psG = [sub_ps.enter_context(nc.psum_tensor(f"psG{i}", [128, 512], F32)) for i in range(2)]

    with nc.Block() as blk:
        @blk.vector
        def _(e):
            p.emit_const_memsets(e, "sV")
        qg1 = _sigma_cols(p, blk, pxug_row, 2, 1024, sgscr2, sgug, "qG", bufs=sgb1)
        qg2 = _sigma_cols(p, blk, pxg2_row, 1, 1024, sgscr3, sg3, "qG2", bufs=sgb2)
        qg3 = _sigma_cols(p, blk, pxo_row, 1, 1024, sgscr4, sg4, "qG3", bufs=sgb3)

        @blk.sync
        def _(e):
            p.inc(e.dma_start(out=u0s[:], in_=utotP.rearrange("p (g j) -> p g j", j=64)), "qS", 16)
            p.inc(e.dma_start(out=gex[:], in_=gam_row[0:1, :].broadcast_to([128, C])), "qS", 16)
            p.inc(e.dma_start(out=bex[:], in_=bet_row[0:1, :].broadcast_to([128, C])), "qS", 16)
            p.inc(e.dma_start(out=trim[:], in_=tri[:, :]), "qS", 16)
            p.inc(e.dma_start(out=idn[:], in_=ident[:, :]), "qS", 16)
        qS_c = p.cnt["qS"]

        for wsb, whbm in ((wg, WgT), (wo, WoT)):
            for t2 in range(0, 8, 2):
                @blk.scalar
                def _(e, wsb=wsb, whbm=whbm, t2=t2):
                    p.wait(e, "sA")
                    p.inc(e.dma_start(out=stgw[:], in_=fm3(whbm)[:, t2:t2 + 2, :]), "qP", 16)
                    e.wait_ge(p.sems["qP"], p.cnt["qP"])
                    p.inc(e.activation(out=wsb[:, t2:t2 + 2, :], in_=stgw[:], func=AF.Identity), "sA")
        qP_c = p.cnt["sA"]

        chunk_done = {}
        for _i in range(-4, 0):
            chunk_done[_i] = (0, 0, 0)
        amb_done = {}
        ybank_done = {-2: p.cnt["sV"], -1: p.cnt["sV"]}
        ymm_sT = {-4: 0, -3: 0, -2: 0, -1: 0}
        ygb_free = {-1: ("qT0", 0), -2: ("qT0", 0), -3: ("qT0", 0), -4: ("qT0", 0)}
        apply_done = {}
        womm_sT = {-4: 0, -3: 0, -2: 0, -1: 0}
        gproj_done = {-2: (0, 0), -1: (p.cnt["sA"], p.cnt["sV"])}
        afm_done = {-2: p.cnt["sA"], -1: p.cnt["sA"]}
        woship = {-2: ("qO", 0), -1: ("qO", 0)}
        s1info = {}

        def emit_s1(gc):
            bi = gc % 4
            qc = "qC%d" % (gc % 2)
            t0 = gc * L

            @blk.sync
            def _(e, gc=gc, bi=bi, qc=qc, t0=t0):
                e.wait_ge(p.sems["qS"], qS_c)
                if gc >= 4:
                    cd = chunk_done[gc - 4]
                    p.wait(e, "sV", cd[0])
                    p.wait(e, "sA", cd[1])
                    p.wait(e, "sT", cd[2])
                p.inc(e.dma_start(out=qtc[bi][:], in_=fm3(qt)[:, :, t0:t0 + L]), qc, 16)
                p.inc(e.dma_start(out=ktc[bi][:], in_=fm3(kt)[:, :, t0:t0 + L]), qc, 16)
                p.inc(e.dma_start(out=vqc[bi][:], in_=vq[t0:t0 + L, :]), qc, 16)
                p.inc(e.dma_start(out=xgc[bi][:], in_=fm3(xg)[:, :, t0:t0 + L]), qc, 16)
                p.inc(e.dma_start(out=trajc[bi][:], in_=traj[:, 512 * gc:512 * gc + 512]), qc, 16)
                p.inc(e.dma_start(out=dpc[bi][:], in_=dpre[:, 8 * gc:8 * gc + 8]), qc, 16)
                p.inc(e.dma_start(out=bTc[bi][:], in_=b_rowT[t0:t0 + L, :]), qc, 16)
            qld = p.cnt[qc]

            @blk.vector
            def _(e, gc=gc, bi=bi, qld=qld, qc=qc):
                e.wait_ge(p.sems[qc], qld)
                e.wait_ge(p.sems["qG"], qg1)
                e.wait_ge(p.sems["qG2"], qg2)
                e.wait_ge(p.sems["qG3"], qg3)
                p.wait(e, "sT", chunk_done[gc - 4][2] if gc >= 4 else 0)
                for g in range(8):
                    ii = e.scalar_tensor_tensor(out=scg[bi][:, g, :], in0=u0s[:, g, :],
                                                scalar=dpc[bi][:, g:g + 1], in1=trajc[bi][:, 64 * g:64 * g + 64],
                                                op0=ALU.mult, op1=ALU.add)
                p.inc(ii, "sV")

            for half in range(2):
                for par in range(2):
                    pb = psA2[par]
                    bp = 64 * par

                    @blk.tensor
                    def _(e, gc=gc, bi=bi, half=half, par=par, pb=pb, bp=bp, qld=qld, qc=qc):
                        e.wait_ge(p.sems[qc], qld)
                        pk = (gc, half - 1, par) if half > 0 else (gc - 1, 1, par)
                        if pk in amb_done:
                            p.wait(e, "sV", amb_done[pk])
                        for hh in range(4):
                            h = half * 8 + 2 * hh + par
                            ii = e.matmul(pb[:, 128 * hh:128 * hh + 128],
                                          ktc[bi][bp:bp + 64, h // 2, :],
                                          qtc[bi][bp:bp + 64, h // 2, :], start=True, stop=True)
                        p.inc(ii, "sT")
                    sT_a = p.cnt["sT"]

                    @blk.vector
                    def _(e, gc=gc, half=half, par=par, pb=pb, sT_a=sT_a):
                        p.wait(e, "sT", sT_a)
                        if half == 0:
                            p.wait(e, "sT", ymm_sT.get(gc - 4, 0))
                        oview = amb[gc % 4].rearrange("p (u v) -> p u v", v=128)[:, 8 * half + par:8 * half + 8:2, :]
                        p.inc(e.tensor_tensor(
                            out=oview,
                            in0=pb[:, :].rearrange("p (u v) -> p u v", v=128),
                            in1=trim.rearrange("p (o v) -> p o v", o=1).broadcast_to([128, 4, 128]),
                            op=ALU.mult), "sV")
                    amb_done[(gc, half, par)] = p.cnt["sV"]
            sV_am = p.cnt["sV"]

            for par in range(2):
                pb = psY2[par]
                bp = 64 * par

                @blk.tensor
                def _(e, gc=gc, bi=bi, par=par, pb=pb, bp=bp, sV_am=sV_am):
                    p.wait(e, "sV", sV_am)
                    p.wait(e, "sV", ybank_done[gc - 1])
                    for hh in range(8):
                        h = 2 * hh + par
                        e.matmul(pb[:, 64 * hh:64 * hh + 64], amb[bi][:, 128 * h:128 * h + 128],
                                 vqc[bi][:, 64 * h:64 * h + 64], start=True, stop=False)
                        ii = e.matmul(pb[:, 64 * hh:64 * hh + 64],
                                      qtc[bi][bp:bp + 64, h // 2, :],
                                      scg[bi][bp:bp + 64, h // 2, :], start=False, stop=True)
                    p.inc(ii, "sT")
            sT_y = p.cnt["sT"]
            ymm_sT[gc] = sT_y

            @blk.vector
            def _(e, gc=gc, bi=bi, sT_y=sT_y):
                p.wait(e, "sT", sT_y)
                p.wait(e, "sA")
                for h in range(16):
                    ii = e.scalar_tensor_tensor(out=ytc[bi][:, 64 * h:64 * h + 64],
                                                in0=vqc[bi][:, 64 * h:64 * h + 64],
                                                scalar=bTc[bi][:, h:h + 1],
                                                in1=psY2[h % 2][:, 64 * (h // 2):64 * (h // 2) + 64],
                                                op0=ALU.mult, op1=ALU.add)
                p.inc(ii, "sV")
            sV_y = p.cnt["sV"]
            ybank_done[gc] = sV_y

            @blk.vector
            def _(e, bi=bi, sV_y=sV_y):
                p.wait(e, "sV", sV_y)
                p.inc(e.tensor_tensor(out=ysq[bi][:], in0=ytc[bi][:], in1=ytc[bi][:], op=ALU.mult), "sV")
                p.wait(e, "sV")
                e.tensor_reduce(out=gnt[bi][:, 0:16], in_=ysq[bi].rearrange("p (h j) -> p h j", j=64),
                                axis=AX.X, op=ALU.add)
                p.inc(e.tensor_reduce(out=gnt[bi][:, 16:32], in_=ytc[bi].rearrange("p (h j) -> p h j", j=64),
                                      axis=AX.X, op=ALU.add), "sV")
                p.wait(e, "sV")
                e.tensor_scalar(out=gnt[bi][:, 32:48], in0=gnt[bi][:, 16:32],
                                scalar1=1.0 / 64.0, scalar2=None, op0=ALU.mult)
                p.inc(e.tensor_scalar(out=gnt[bi][:, 48:64], in0=gnt[bi][:, 0:16],
                                      scalar1=1.0 / 64.0, scalar2=None, op0=ALU.mult), "sV")
                p.wait(e, "sV")
                p.inc(e.tensor_tensor(out=gnt[bi][:, 0:16], in0=gnt[bi][:, 32:48], in1=gnt[bi][:, 32:48],
                                      op=ALU.mult), "sV")
                p.wait(e, "sV")
                p.inc(e.tensor_tensor(out=gnt[bi][:, 48:64], in0=gnt[bi][:, 48:64], in1=gnt[bi][:, 0:16],
                                      op=ALU.subtract), "sV")
            sV_gn = p.cnt["sV"]

            @blk.scalar
            def _(e, bi=bi, gc=gc, sV_gn=sV_gn):
                p.wait(e, "sV", sV_gn)
                p.wait(e, "sV", apply_done.get(gc - 4, 0))
                p.inc(e.activation(out=gnt[bi][:, 0:16], in_=gnt[bi][:, 48:64], func=AF.Sqrt,
                                   bias=epsg_col, scale=1.0), "sA")
            sA_sq = p.cnt["sA"]

            @blk.vector
            def _(e, bi=bi, sA_sq=sA_sq):
                p.wait(e, "sA", sA_sq)
                p.inc(e.reciprocal(out=gnt[bi][:, 64:80], in_=gnt[bi][:, 0:16]), "sV")
            sV_rs = p.cnt["sV"]
            s1info[gc] = (bi, qc, qld, sV_rs)

        def emit_s2(gc):
            bi, qc, qld, sV_rs = s1info[gc]
            t0 = gc * L

            for hh in range(2):
                pb = psG[hh]
                gi2 = (gc % 2) * 2 + hh

                @blk.tensor
                def _(e, gc=gc, bi=bi, hh=hh, pb=pb, qc=qc, qld=qld):
                    e.wait_ge(p.sems[qc], qld)
                    p.wait(e, "sA", qP_c)
                    p.wait(e, "sV", gproj_done[gc - 1][1])
                    for kc in range(8):
                        ii = e.matmul(pb[:, 0:512], xgc[bi][:, kc, :],
                                      wg[:, kc, 512 * hh:512 * hh + 512],
                                      start=(kc == 0), stop=(kc == 7))
                    p.inc(ii, "sT")
                sT_g = p.cnt["sT"]

                @blk.scalar
                def _(e, gc=gc, hh=hh, pb=pb, gi2=gi2, sT_g=sT_g):
                    p.wait(e, "sT", sT_g)
                    p.wait(e, "sV", gproj_done[gc - 2][1])
                    p.inc(e.activation(out=gth[gi2][:], in_=pb[:, 0:512],
                                       func=AF.Tanh, scale=0.5), "sA")
                sA_gth = p.cnt["sA"]

                @blk.vector
                def _(e, gc=gc, bi=bi, hh=hh, pb=pb, gi2=gi2, sA_gth=sA_gth):
                    p.wait(e, "sA", sA_gth)
                    p.inc(e.tensor_scalar(out=gth1[gi2][:], in0=gth[gi2][:], scalar1=1.0,
                                          scalar2=None, op0=ALU.add), "sV")
                    p.wait(e, "sV")
                    p.inc(e.scalar_tensor_tensor(out=gtc[bi][:, 512 * hh:512 * hh + 512],
                                                 in0=pb[:, 0:512], scalar=0.5, in1=gth1[gi2][:],
                                                 op0=ALU.mult, op1=ALU.mult), "sV")
            sA_g = p.cnt["sA"]
            sV_gd = p.cnt["sV"]
            gproj_done[gc] = (sA_g, sV_gd)

            @blk.vector
            def _(e, gc=gc, bi=bi, sV_rs=sV_rs, sA_g=sA_g):
                p.wait(e, "sV", sV_rs)
                p.wait(e, "sA", sA_g)
                e.wait_ge(p.sems[ygb_free[gc - 4][0]], ygb_free[gc - 4][1])
                p.inc(e.tensor_tensor(out=ysq[bi].rearrange("p (h j) -> p h j", j=64),
                                      in0=ytc[bi].rearrange("p (h j) -> p h j", j=64),
                                      in1=gnt[bi][:, 32:48].rearrange("p (h o) -> p h o", o=1).broadcast_to([128, 16, 64]),
                                      op=ALU.subtract), "sV")
                p.wait(e, "sV")
                p.inc(e.tensor_tensor(out=ysq[bi].rearrange("p (h j) -> p h j", j=64),
                                      in0=ysq[bi].rearrange("p (h j) -> p h j", j=64),
                                      in1=gnt[bi][:, 64:80].rearrange("p (h o) -> p h o", o=1).broadcast_to([128, 16, 64]),
                                      op=ALU.mult), "sV")
                p.wait(e, "sV")
                p.inc(e.tensor_tensor(out=ysq[bi][:], in0=ysq[bi][:], in1=gex[:], op=ALU.mult), "sV")
                p.wait(e, "sV")
                p.inc(e.tensor_tensor(out=ysq[bi][:], in0=ysq[bi][:], in1=bex[:], op=ALU.add), "sV")
                p.wait(e, "sV")
                p.inc(e.tensor_tensor(out=ygb[bi][:], in0=ysq[bi][:], in1=gtc[bi][:], op=ALU.mult), "sV")
            sV_yg = p.cnt["sV"]
            apply_done[gc] = sV_yg

            qtp = "qT%d" % (gc % 2)

            @blk.scalar
            def _(e, gc=gc, bi=bi, sV_yg=sV_yg, qtp=qtp):
                p.wait(e, "sV", sV_yg)
                p.wait(e, "sT", womm_sT.get(gc - 4, 0))
                e.wait_ge(p.sems[qtp], p.cnt[qtp])
                for cb in range(8):
                    p.inc(e.dma_start(out=ygf[bi][:, cb, :], in_=ygb[bi][:, 128 * cb:128 * cb + 128],
                                      transpose=True), qtp, 16)
            qT_y = p.cnt[qtp]
            ygb_free[gc] = (qtp, qT_y)

            for ct in range(8):
                wid = gc * 8 + ct
                pb = psWo[ct % 2]
                qos = "qO%d" % (ct % 2)

                @blk.tensor
                def _(e, gc=gc, ct=ct, pb=pb, qT_y=qT_y, qtp=qtp, wid=wid):
                    e.wait_ge(p.sems[qtp], qT_y)
                    p.wait(e, "sA", afm_done[wid - 2])
                    for kc in range(8):
                        ii = e.matmul(pb[:, 0:128], wo[:, kc, 128 * ct:128 * ct + 128],
                                      ygf[gc % 4][:, kc, :], start=(kc == 0), stop=(kc == 7))
                    p.inc(ii, "sT")
                sT_w = p.cnt["sT"]

                @blk.scalar
                def _(e, gc=gc, ct=ct, pb=pb, sT_w=sT_w, wid=wid, qos=qos):
                    p.wait(e, "sT", sT_w)
                    e.wait_ge(p.sems[woship[wid - 2][0]], woship[wid - 2][1])
                    p.inc(e.activation(out=afm[ct % 2][:], in_=pb[:, 0:128], func=AF.Identity), "sA")
                sA_w = p.cnt["sA"]
                afm_done[wid] = sA_w

                @blk.sync
                def _(e, gc=gc, ct=ct, t0=t0, sA_w=sA_w, qos=qos):
                    p.wait(e, "sA", sA_w)
                    p.inc(e.dma_start(out=attn_scr[128 * ct:128 * (ct + 1), t0:t0 + L],
                                      in_=afm[ct % 2][:]), qos, 16)
                woship[wid] = (qos, p.cnt[qos])
            womm_sT[gc] = p.cnt["sT"]
            chunk_done[gc] = (p.cnt["sV"], p.cnt["sA"], p.cnt["sT"])

        for it in range(NCH + 1):
            if it < NCH:
                emit_s1(it)
            if it >= 1:
                emit_s2(it - 1)

        sub_ps.close()
        subA.close()

        subB = ExitStack()

        def sbb(name, shape, dtype=F32):
            return subB.enter_context(nc.sbuf_tensor(name, list(shape), dtype))

        wtile = [sbb(f"dwt{i}", [128, FFI]) for i in range(2)]
        wq2 = [sbb(f"dwq{i}", [128, FFI], BF16) for i in range(2)]
        babs = [sbb(f"dbabs{i}", [128, FFI // BLK]) for i in range(2)]
        sblk = [sbb(f"dsblk{i}", [128, FFI // BLK]) for i in range(2)]
        rblk = [sbb(f"drblk{i}", [128, FFI // BLK]) for i in range(2)]
        wdT = sbb("wdT", [128, 24, C], BF16)
        ui = [sbb(f"ui{i}", [128, 512], I16) for i in range(2)]
        gi = [sbb(f"gi{i}", [128, 512], I16) for i in range(2)]
        gqf = sbb("gqf", [128, 512])
        sigt = sbb("sigt", [128, 512])
        sigq = sbb("sigq", [128, 512])
        g2t = sbb("g2t", [128, 512])
        g2q = sbb("g2q", [128, 512])
        otl = sbb("otl", [128, 512])
        oqb = sbb("oqb", [128, 512], BF16)
        xres = sbb("xres", [128, 8, 512])
        ares = sbb("ares", [128, 8, 512])
        outt = sbb("outt", [128, 8, 512])

        sub_ps2 = ExitStack()
        psD = [sub_ps2.enter_context(nc.psum_tensor(f"psDn{i}", [128, 512], F32)) for i in range(8)]

        qO_attn = p.cnt["qO"]
        for dev in (blk.sync, blk.scalar, blk.vector, blk.gpsimd, blk.tensor):
            @dev
            def _(e):
                e.wait_ge(p.sems["qO"], qO_attn)
                for qq in ("qC0", "qC1", "qS", "qP", "qO0", "qO1", "qT0", "qT1"):
                    e.wait_ge(p.sems[qq], p.cnt[qq])

        emit_dequant(p, blk, w_down, wdqd_scr, C, (wtile, wq2, babs, sblk, rblk), rnd_col, "qD")

        @blk.scalar
        def _(e):
            for qq in ("qP", "qP0", "qP1"):
                e.wait_ge(p.sems[qq], p.cnt[qq])
            for it in range(24):
                p.inc(e.dma_start(out=wdT[:, it, :], in_=wdqd_scr[0:C, 128 * it:128 * (it + 1)],
                                  transpose=True), "qW", 16)
        qW_d = p.cnt["qW"]

        for tq in range(NQ):
            tt0 = tq * 512
            for it in range(24):
                bi = it % 2
                ql = "qL%d" % bi

                @blk.sync
                def _(e, tq=tq, it=it, bi=bi, ql=ql, tt0=tt0):
                    if tq + it > 0:
                        p.wait(e, "sV")
                    p.inc(e.dma_start(out=ui[bi][:], in_=upq[128 * it:128 * (it + 1), tt0:tt0 + 512]), ql, 16)
                    p.inc(e.dma_start(out=gi[bi][:], in_=gq[128 * it:128 * (it + 1), tt0:tt0 + 512]), ql, 16)
                qld = p.cnt[ql]

                @blk.vector
                def _(e, bi=bi, qld=qld, ql=ql):
                    e.wait_ge(p.sems[ql], qld)
                    p.inc(e.tensor_scalar(out=gqf[:], in0=gi[bi][:], scalar1=sgug[:, 5:6],
                                          scalar2=None, op0=ALU.mult), "sV")
                sV_g = p.cnt["sV"]

                @blk.scalar
                def _(e, sV_g=sV_g):
                    p.wait(e, "sV", sV_g)
                    p.inc(e.activation(out=sigt[:], in_=gqf[:], func=AF.Sigmoid), "sA")
                sA_s = p.cnt["sA"]

                @blk.vector
                def _(e, bi=bi, sA_s=sA_s):
                    p.wait(e, "sA", sA_s)
                    p.inc(e.tensor_scalar(out=sigq[:], in0=sigt[:], scalar1=65536.0, scalar2=RND,
                                          op0=ALU.mult, op1=ALU.add), "sV")
                    p.wait(e, "sV")
                    p.inc(e.tensor_scalar(out=sigq[:], in0=sigq[:], scalar1=RND + 65535.0,
                                          scalar2=RND, op0=ALU.min, op1=ALU.subtract), "sV")
                    p.wait(e, "sV")
                    p.inc(e.scalar_tensor_tensor(out=g2t[:], in0=sigq[:], scalar=1.0 / 65536.0,
                                                 in1=gqf[:], op0=ALU.mult, op1=ALU.mult), "sV")
                    p.wait(e, "sV")
                    p.inc(e.tensor_scalar(out=g2q[:], in0=g2t[:], scalar1=sg3[:, 2:3],
                                          scalar2=RND, op0=ALU.mult, op1=ALU.add), "sV")
                    p.wait(e, "sV")
                    p.inc(e.tensor_scalar(out=g2q[:], in0=g2q[:], scalar1=RND,
                                          scalar2=sg3[:, 1:2], op0=ALU.subtract, op1=ALU.mult), "sV")
                    p.wait(e, "sV")
                    p.inc(e.scalar_tensor_tensor(out=otl[:], in0=ui[bi][:], scalar=sgug[:, 1:2],
                                                 in1=g2q[:], op0=ALU.mult, op1=ALU.mult), "sV")
                    p.wait(e, "sV")
                    p.inc(e.tensor_scalar(out=g2q[:], in0=otl[:], scalar1=sg4[:, 2:3],
                                          scalar2=RND, op0=ALU.mult, op1=ALU.add), "sV")
                    p.wait(e, "sV")
                    p.wait(e, "sT")
                    p.inc(e.tensor_scalar(out=oqb[:], in0=g2q[:], scalar1=RND,
                                          scalar2=sg4[:, 1:2], op0=ALU.subtract, op1=ALU.mult), "sV")
                sV_oq = p.cnt["sV"]

                @blk.tensor
                def _(e, tq=tq, it=it, sV_oq=sV_oq):
                    e.wait_ge(p.sems["qW"], qW_d)
                    p.wait(e, "sV", sV_oq)
                    if it == 0:
                        p.wait(e, "sA")
                    for ct in range(8):
                        ii = e.matmul(psD[ct][:, :], wdT[:, it, 128 * ct:128 * ct + 128],
                                      oqb[:, :], start=(it == 0), stop=(it == 23))
                    p.inc(ii, "sT")
            sT_dn = p.cnt["sT"]

            @blk.sync
            def _(e, tq=tq, tt0=tt0):
                for qq in ("qO", "qO0", "qO1"):
                    e.wait_ge(p.sems[qq], p.cnt[qq])
                p.inc(e.dma_start(out=xres[:], in_=fm3(x_fm)[:, :, tt0 + 1:tt0 + 513]), "qS", 16)
                p.inc(e.dma_start(out=ares[:], in_=fm3(attn_scr)[:, :, tt0:tt0 + 512]), "qS", 16)
            qS_r = p.cnt["qS"]

            @blk.vector
            def _(e, tq=tq, qS_r=qS_r, sT_dn=sT_dn):
                e.wait_ge(p.sems["qS"], qS_r)
                p.wait(e, "sT", sT_dn)
                p.inc(e.tensor_tensor(out=outt[:], in0=xres[:], in1=ares[:], op=ALU.add), "sV")
                p.wait(e, "sV")
                for ct in range(8):
                    ii = e.tensor_tensor(out=outt[:, ct, :], in0=outt[:, ct, :],
                                         in1=psD[ct][:, :], op=ALU.add)
                p.inc(ii, "sV")
            sV_out = p.cnt["sV"]

            @blk.scalar
            def _(e, tq=tq, sV_out=sV_out):
                p.wait(e, "sV", sV_out)
                p.inc(e.nop(), "sA")

            @blk.sync
            def _(e, tq=tq, tt0=tt0, sV_out=sV_out):
                p.wait(e, "sV", sV_out)
                e.wait_ge(p.sems["qO"], p.cnt["qO"])
                p.inc(e.dma_start(out=fm3(out_o)[:, :, tt0:tt0 + 512], in_=outt[:]), "qO", 16)

        @blk.sync
        def _(e):
            for q in ("qS", "qG", "qG2", "qG3", "qO", "qO0", "qO1", "qP", "qP0", "qP1",
                      "qW", "qD", "qL0", "qL1", "qC0", "qC1", "qT0", "qT1"):
                e.wait_ge(p.sems[q], p.cnt[q])
        sub_ps2.close()
        subB.close()
    return p


from concourse.bass_utils import run_bass_kernel_spmd

TCK = 2048
TSL1 = 256
TSL2 = 512
_PROGS = {}


def _progs():
    if not _PROGS:
        _PROGS["L1"] = build_L1(TC=TCK, TSL=TSL1)
        _PROGS["L2"] = build_L2(TC=TCK, TSL=TSL2)
        _PROGS["L3"] = build_L3(TC=TCK, TSL=TSL2)
        _PROGS["L4"] = build_L4(TC=TCK)
        _PROGS["L5"] = build_L5(TC=TCK)
    return _PROGS


def kernel(L0_attn_x, L0_attn_miu_x, L0_attn_lambda_, L0_attn_A_lora, L0_attn_B_lora,
           L0_attn_td_miu, L0_attn_td_A, L0_attn_td_B, L0_attn_u, L0_attn_Wr,
           L0_attn_Wk, L0_attn_Wv, L0_attn_Wg, L0_attn_Wo, L0_attn_gn_gamma,
           L0_attn_gn_beta, L0_ffn_w_gate, L0_ffn_w_up, L0_ffn_w_down):
    x = np.asarray(L0_attn_x, np.float32)
    B, T, _ = x.shape
    NC = 8
    cores = list(range(NC))
    TTP = ceil_div(TCK + 1, 128) * 128

    lam = np.asarray(L0_attn_lambda_, np.float32).reshape(5, C)
    lam_col = np.zeros((128, 5 * 8), np.float32)
    for f in range(5):
        for t in range(8):
            lam_col[:, f * 8 + t] = lam[f][128 * t:128 * (t + 1)]
    miu_x = np.asarray(L0_attn_miu_x, np.float32).reshape(C)
    mx_col = np.ascontiguousarray(miu_x.reshape(8, 128).T)
    tdm = np.asarray(L0_attn_td_miu, np.float32).reshape(C)
    tdm_col = np.ascontiguousarray(tdm.reshape(8, 128).T)
    u = np.asarray(L0_attn_u, np.float32).reshape(16, 64)
    u16 = np.zeros((128, 8 * 16), np.float32)
    for h in range(16):
        u16[64 * (h % 2):64 * (h % 2) + 64, (h // 2) * 16 + h] = u[h]
    id128 = np.eye(128, dtype=np.float32)
    id2 = np.concatenate([np.eye(64), np.eye(64)], 0).astype(np.float32)
    tri = np.triu(np.ones((128, 128), np.float32), 1)
    WkT = np.ascontiguousarray(np.asarray(L0_attn_Wk, np.float32).T)
    WvT = np.ascontiguousarray(np.asarray(L0_attn_Wv, np.float32).T)
    WrT = np.ascontiguousarray(np.asarray(L0_attn_Wr, np.float32).T)
    WgT = np.ascontiguousarray(np.asarray(L0_attn_Wg, np.float32).T)
    WoT = np.ascontiguousarray(np.asarray(L0_attn_Wo, np.float32).T)
    A_lora = np.ascontiguousarray(np.asarray(L0_attn_A_lora, np.float32))
    B_lora = np.ascontiguousarray(np.asarray(L0_attn_B_lora, np.float32))
    td_A = np.ascontiguousarray(np.asarray(L0_attn_td_A, np.float32))
    td_B = np.ascontiguousarray(np.asarray(L0_attn_td_B, np.float32))
    w_up = np.ascontiguousarray(np.asarray(L0_ffn_w_up, np.float32))
    w_gate = np.ascontiguousarray(np.asarray(L0_ffn_w_gate, np.float32))
    w_down = np.ascontiguousarray(np.asarray(L0_ffn_w_down, np.float32))
    gam_row = np.ascontiguousarray(np.asarray(L0_attn_gn_gamma, np.float32).reshape(1, C))
    bet_row = np.ascontiguousarray(np.asarray(L0_attn_gn_beta, np.float32).reshape(1, C))

    xrows_l, xfm_l, xtm_l = [], [], []
    for c in cores:
        b, hf = c // 2, c % 2
        xr = np.zeros((TCK + 1, C), np.float32)
        xr[1:] = x[b, TCK * hf:TCK * (hf + 1)]
        if hf == 1:
            xr[0] = x[b, TCK - 1]
        xrows_l.append(xr)
        xfm_l.append(np.ascontiguousarray(xr.T))
        xtmp = np.zeros((TTP, C), np.float32)
        xtmp[:TCK + 1] = xr
        xtm_l.append(xtmp)

    pr = _progs()

    im1 = [{"x_tm": xtm_l[c], "x_fm": xfm_l[c], "WkT": WkT, "WvT": WvT, "WrT": WrT,
            "A_lora": A_lora, "B_lora": B_lora, "td_A": td_A, "td_B": td_B,
            "lam_col": lam_col, "mx_col": mx_col, "tdmiu_col": tdm_col,
            "u16": u16, "ident": id128, "ident2": id2} for c in cores]
    r1 = run_bass_kernel_spmd(pr["L1"].nc, im1, core_ids=cores).results

    pxn_row = np.concatenate([r1[c]["pxn"][:, 0] for c in cores]).reshape(1, 1024)
    pxn_row = np.ascontiguousarray(pxn_row)

    im2 = [{"x_fm": xfm_l[c], "murstd": r1[c]["murstd"], "pxn_row": pxn_row,
            "w_up": w_up, "w_gate": w_gate} for c in cores]
    r2 = run_bass_kernel_spmd(pr["L2"].nc, im2, core_ids=cores).results

    pxug = np.zeros((1, 2048), np.float32)
    pxug[0, 0:1024] = np.concatenate([r2[c]["pxu"][:, 0] for c in cores])
    pxug[0, 1024:2048] = np.concatenate([r2[c]["pxg"][:, 0] for c in cores])

    im3 = [{"x_fm": xfm_l[c], "murstd": r1[c]["murstd"], "pxn_row": pxn_row,
            "pxug_row": pxug, "wdq_up": r2[c]["wdq_up"], "wdq_gate": r2[c]["wdq_gate"]}
           for c in cores]
    r3 = run_bass_kernel_spmd(pr["L3"].nc, im3, core_ids=cores).results

    pxg2 = np.zeros((1, 1024), np.float32)
    pxg2[0, 0:1024] = np.concatenate([r3[c]["pxg2"][:, 0] for c in cores])

    im4 = [{"upq": r3[c]["upq"], "gq": r3[c]["gq"], "pxug_row": pxug,
            "pxg2_row": pxg2} for c in cores]
    r4 = run_bass_kernel_spmd(pr["L4"].nc, im4, core_ids=cores).results

    pxo = np.zeros((1, 1024), np.float32)
    pxo[0, 0:1024] = np.concatenate([r4[c]["pxo"][:, 0] for c in cores])

    zstate = np.zeros((128, 512), np.float32)
    im5 = []
    for c in cores:
        b, hf = c // 2, c % 2
        utotP = np.ascontiguousarray(r1[c - 1]["utot"]) if hf == 1 else zstate
        im5.append({"x_fm": xfm_l[c], "qt": r1[c]["qt"], "kt": r1[c]["kt"],
                    "vq": r1[c]["vq"], "xg": r1[c]["xg"],
                    "b_rowT": np.ascontiguousarray(np.asarray(r1[c]["b_row"]).T),
                    "utotP": utotP, "traj": r1[c]["traj"], "dpre": r1[c]["dpre"],
                    "WgT": WgT, "WoT": WoT, "w_down": w_down,
                    "upq": r3[c]["upq"], "gq": r3[c]["gq"], "pxug_row": pxug,
                    "pxg2_row": pxg2, "pxo_row": pxo, "gam_row": gam_row,
                    "bet_row": bet_row, "tri": tri, "ident": id128})
    r5 = run_bass_kernel_spmd(pr["L5"].nc, im5, core_ids=cores).results

    out = np.empty((B, T, C), np.float32)
    for c in cores:
        b, hf = c // 2, c % 2
        out[b, TCK * hf:TCK * (hf + 1)] = np.asarray(r5[c]["out_fm"], np.float32).T
    return out


# version 13
# speedup vs baseline: 1.0413x; 1.0413x over previous
import numpy as np
from contextlib import ExitStack
import concourse.bass as bass
import concourse.mybir as mybir

F32 = mybir.dt.float32
BF16 = mybir.dt.bfloat16
I16 = mybir.dt.int16
AF = mybir.ActivationFunctionType
ALU = mybir.AluOpType
AX = mybir.AxisListType

C = 1024
H = 16
HS = 64
L = 128
FFI = 3072
RND = 12582912.0
QMAX = 32767.0
WQMAX = 7.0
BLK = 32


def ceil_div(a, b):
    return (a + b - 1) // b


class Prog:
    def __init__(self, name="prog"):
        self.nc = bass.Bass("TRN2", target_bir_lowering=False)
        self.ctx = ExitStack()
        self.sems = {}
        self.cnt = {}

    def sb(self, name, shape, dtype=F32):
        return self.ctx.enter_context(self.nc.sbuf_tensor(name, list(shape), dtype))

    def ps(self, name, shape, dtype=F32):
        return self.ctx.enter_context(self.nc.psum_tensor(name, list(shape), dtype))

    def sem(self, name):
        if name not in self.sems:
            self.sems[name] = self.ctx.enter_context(self.nc.semaphore(name))
            self.cnt[name] = 0
        return self.sems[name]

    def inc(self, instr, name, n=1):
        instr.then_inc(self.sems[name], n)
        self.cnt[name] += n
        return self.cnt[name]

    def wait(self, eng, name, v=None):
        eng.wait_ge(self.sems[name], self.cnt[name] if v is None else v)

    def constcol(self, value):
        if not hasattr(self, "consts"):
            self.consts = {}
        if value not in self.consts:
            t = self.sb(f"cc_{len(self.consts)}", [128, 1], F32)
            self.consts[value] = t
        return self.consts[value][:, 0:1]

    def emit_const_memsets(self, eng, semname):
        if not hasattr(self, "consts"):
            self.consts = {}
        ii = None
        for v, t in self.consts.items():
            ii = eng.memset(t[:], float(v))
        if ii is not None:
            self.inc(ii, semname)

    def din(self, name, shape, dtype=F32):
        return self.nc.dram_tensor(name, list(shape), dtype, kind="ExternalInput").ap()

    def dout(self, name, shape, dtype=F32):
        return self.nc.dram_tensor(name, list(shape), dtype, kind="ExternalOutput").ap()


def fm3(ap, p=128):
    return ap.rearrange("(t p) x -> p t x", p=p)


def build_L1(TC=2048, TSL=512):
    NCH = TC // L
    NSL = TC // TSL
    CPS = TSL // L
    TTP = ceil_div(TC + 1, 128) * 128
    NTT = TTP // 128
    p = Prog("L1")
    nc = p.nc

    x_tm = p.din("x_tm", [TTP, C])
    x_fm = p.din("x_fm", [C, TC + 1])
    WkT = p.din("WkT", [C, C])
    WvT = p.din("WvT", [C, C])
    WrT = p.din("WrT", [C, C])
    A_lora = p.din("A_lora", [C, 160])
    B_lora = p.din("B_lora", [5, 32, C])
    td_A = p.din("td_A", [C, 64])
    td_B = p.din("td_B", [64, C])
    lam_col = p.din("lam_col", [128, 5 * 8])
    mx_col = p.din("mx_col", [128, 8])
    tdmiu_col = p.din("tdmiu_col", [128, 8])
    u16 = p.din("u16", [128, 8 * 16])
    ident = p.din("ident", [128, 128])
    ident2 = p.din("ident2", [128, 64])

    murstd = p.dout("murstd", [2, TTP])
    pxn_o = p.dout("pxn", [128, 1])
    qt_o = p.dout("qt", [C, TC], BF16)
    kt_o = p.dout("kt", [C, TC], BF16)
    vq_o = p.dout("vq", [TC, C], BF16)
    xg_o = p.dout("xg", [C, TC], BF16)
    b_o = p.dout("b_row", [16, TC])
    utot_o = p.dout("utot", [128, 512])
    traj_o = p.dout("traj", [128, NCH * 512])
    dpre_o = p.dout("dpre", [128, NCH * 8])

    wk = p.sb("wk", [128, 8, C], BF16)
    wv = p.sb("wv", [128, 8, C], BF16)
    wr = p.sb("wr", [128, 8, C], BF16)
    al = p.sb("al", [128, 8, 160], BF16)
    bl = p.sb("bl", [128, 5, C], BF16)
    tda = p.sb("tda", [128, 8, 64], BF16)
    tdb = p.sb("tdb", [64, C], BF16)
    lam = p.sb("lam", [128, 5, 8])
    mxc = p.sb("mxc", [128, 8])
    tdm = p.sb("tdm", [128, 8])
    u16s = p.sb("u16s", [128, 8, 16], BF16)
    idn = p.sb("idn", [128, 128])
    idnb = p.sb("idnb", [128, 64], BF16)

    xtm = [p.sb(f"xtm{i}", [128, C]) for i in range(2)]
    trash = p.sb("trash", [128, C], BF16)
    muA = p.sb("muA", [128, NTT])
    rsA = p.sb("rsA", [128, NTT])
    mrP = p.sb("mrP", [128, 2 * NTT])

    xfs = p.sb("xfs", [128, 8, TSL + 1])
    mue = p.sb("mue", [128, TSL + 1])
    rse = p.sb("rse", [128, TSL + 1])
    xns = p.sb("xns", [128, 8, TSL + 1])
    dxa = p.sb("dxa", [128, 8, TSL])
    xm = p.sb("xm", [128, 8, TSL], BF16)
    lhA = p.sb("lhA", [96, TSL], BF16)
    lh2 = p.sb("lh2", [64, TSL], BF16)
    tmpm = p.sb("tmpm", [128, 8, TSL])
    xxb = [p.sb(f"xx{i}", [128, 8, TSL], BF16) for i in range(2)]
    th = p.sb("th", [64, TSL], BF16)
    expw = p.sb("expw", [128, 8, TSL])
    spr = p.sb("spr", [128, 8, TSL])
    epos = p.sb("epos", [128, 8, TSL])
    dtot = p.sb("dtot", [128, 8, CPS])
    rf = p.sb("rf", [128, 8, TSL])
    ktb = p.sb("ktb", [128, 8, TSL], BF16)
    khb = p.sb("khb", [128, 8, TSL], BF16)
    qtb = p.sb("qtb", [128, 8, TSL], BF16)
    vtb = p.sb("vtb", [128, CPS, C], BF16)
    khT = p.sb("khT", [128, 8, 128], BF16)
    bsl = p.sb("bsl", [16, TSL])
    sown = [p.sb(f"sown{i}", [128, 8, 64]) for i in range(2)]
    dprer = [p.sb(f"dprer{i}", [128, 8]) for i in range(2)]
    pxn_t = p.sb("pxn_t", [128, NSL + 1])
    zer = p.sb("zer", [128, L])

    psA = p.ps("psA", [128, 512])
    psX = p.ps("psX", [64, 512])
    PSW = max(TSL, 512)
    psP = [p.ps(f"psP{i}", [128, PSW]) for i in range(2)]
    psU = p.ps("psU", [128, 512])

    eps_col = p.constcol(1e-5)

    for s in ("qS", "qA", "qP", "qO", "qB0", "qB1", "qT0", "qT1", "sV", "sA", "sP", "sT"):
        p.sem(s)

    with nc.Block() as blk:
        @blk.sync
        def _(e):
            p.inc(e.dma_start(out=lam[:], in_=lam_col.rearrange("p (f t) -> p f t", f=5)), "qS", 16)
            p.inc(e.dma_start(out=mxc[:], in_=mx_col[:, :]), "qS", 16)
            p.inc(e.dma_start(out=tdm[:], in_=tdmiu_col[:, :]), "qS", 16)
            p.inc(e.dma_start(out=idn[:], in_=ident[:, :]), "qS", 16)
        qS_const = p.cnt["qS"]

        @blk.gpsimd
        def _(e):
            p.inc(e.dma_start(out=wk[:], in_=fm3(WkT)), "qP", 16)
            p.inc(e.dma_start(out=wv[:], in_=fm3(WvT)), "qP", 16)
            p.inc(e.dma_start(out=wr[:], in_=fm3(WrT)), "qP", 16)
            p.inc(e.dma_start(out=al[:], in_=fm3(A_lora)), "qP", 16)
            p.inc(e.dma_start(out=bl[0:32], in_=B_lora.rearrange("f i c -> i f c")), "qP", 16)
            p.inc(e.dma_start(out=bl[32:64], in_=B_lora.rearrange("f i c -> i f c")), "qP", 16)
            p.inc(e.dma_start(out=bl[64:96], in_=B_lora.rearrange("f i c -> i f c")), "qP", 16)
            p.inc(e.dma_start(out=tda[:], in_=fm3(td_A)), "qP", 16)
            p.inc(e.dma_start(out=tdb[:], in_=td_B[:, :]), "qP", 16)
            p.inc(e.dma_start(out=u16s[:], in_=u16.rearrange("p (t h) -> p t h", t=8)), "qP", 16)
            p.inc(e.dma_start(out=idnb[:], in_=ident2[:, :]), "qP", 16)
        qP_const = p.cnt["qP"]

        @blk.vector
        def _(e):
            e.memset(zer[:], 0.0)
            e.memset(sown[0][:], 0.0)
            e.memset(dprer[0][:], 1.0)
            p.emit_const_memsets(e, "sV")
            p.inc(e.memset(pxn_t[:], 0.0), "sV")
        sV_init = p.cnt["sV"]

        sV_b0 = p.cnt["sV"]
        sA_b0 = p.cnt["sA"]

        @blk.scalar
        def _(e):
            for j in range(NTT):
                if j >= 2:
                    p.wait(e, "sV", sV_b0 + (j - 1))
                    p.wait(e, "sA", sA_b0 + (j - 1))
                p.inc(e.dma_start(out=xtm[j % 2][:], in_=x_tm[128 * j:128 * (j + 1), :]), "qA", 16)
                e.wait_ge(p.sems["qA"], 16 * (j + 1))
                p.inc(e.activation(out=trash[:], in_=xtm[j % 2][:], func=AF.Square,
                                   accum_out=rsA[:, j:j + 1]), "sA")
        sA_sq = p.cnt["sA"]

        @blk.vector
        def _(e):
            for j in range(NTT):
                e.wait_ge(p.sems["qA"], 16 * (j + 1))
                p.inc(e.tensor_reduce(out=muA[:, j:j + 1], in_=xtm[j % 2][:], axis=AX.X,
                                      op=ALU.add), "sV")
        sV_sums = p.cnt["sV"]

        sc1 = p.sb("sc1", [128, NTT])
        sc2 = p.sb("sc2", [128, NTT])

        @blk.vector
        def _(e):
            p.wait(e, "sA", sA_sq)
            p.wait(e, "sV", sV_sums)
            e.tensor_scalar(out=mrP[:, 0:NTT], in0=muA[:], scalar1=1.0 / 1024.0, scalar2=None,
                            op0=ALU.mult)
            p.inc(e.tensor_scalar(out=sc1[:], in0=rsA[:], scalar1=1.0 / 1024.0, scalar2=None,
                                  op0=ALU.mult), "sV")
            p.wait(e, "sV")
            p.inc(e.tensor_tensor(out=sc2[:], in0=mrP[:, 0:NTT], in1=mrP[:, 0:NTT], op=ALU.mult), "sV")
            p.wait(e, "sV")
            p.inc(e.tensor_tensor(out=rsA[:], in0=sc1[:], in1=sc2[:], op=ALU.subtract), "sV")
        sV_var = p.cnt["sV"]

        @blk.scalar
        def _(e):
            p.wait(e, "sV", sV_var)
            p.inc(e.activation(out=sc1[:], in_=rsA[:], func=AF.Sqrt, bias=eps_col, scale=1.0), "sA")
        sA_sqrt = p.cnt["sA"]

        @blk.vector
        def _(e):
            p.wait(e, "sA", sA_sqrt)
            p.inc(e.reciprocal(out=mrP[:, NTT:2 * NTT], in_=sc1[:]), "sV")
        sV_mr = p.cnt["sV"]

        @blk.sync
        def _(e):
            p.wait(e, "sV", sV_mr)
            e.wait_ge(p.sems["qS"], qS_const)
            for j in range(NTT):
                p.inc(e.dma_start(out=murstd[0:1, 128 * j:128 * (j + 1)], in_=mrP[:, j:j + 1]), "qS", 16)
                p.inc(e.dma_start(out=murstd[1:2, 128 * j:128 * (j + 1)], in_=mrP[:, NTT + j:NTT + j + 1]), "qS", 16)
        qS_mr = p.cnt["qS"]

        for s in range(NSL):
            c0 = s * TSL
            sl1 = slice(c0, c0 + TSL + 1)
            slt = slice(c0, c0 + TSL)

            @blk.sync
            def _(e):
                if s > 0:
                    p.wait(e, "sV")
                    p.wait(e, "sA")
                    p.wait(e, "sT")
                    p.wait(e, "sP")
                e.wait_ge(p.sems["qS"], qS_mr)
                p.inc(e.dma_start(out=xfs[:], in_=fm3(x_fm)[:, :, sl1]), "qS", 16)
                p.inc(e.dma_start(out=mue[:], in_=murstd[0:1, c0:c0 + TSL + 1].broadcast_to([128, TSL + 1])), "qS", 16)
                p.inc(e.dma_start(out=rse[:], in_=murstd[1:2, c0:c0 + TSL + 1].broadcast_to([128, TSL + 1])), "qS", 16)
            qS_ld = p.cnt["qS"]

            @blk.vector
            def _(e):
                e.wait_ge(p.sems["qS"], qS_ld)
                p.inc(e.tensor_tensor(out=xns[:], in0=xfs[:],
                                in1=mue.rearrange("p (o c) -> p o c", o=1).broadcast_to([128, 8, TSL + 1]),
                                op=ALU.subtract), "sV")
                p.wait(e, "sV")
                p.inc(e.tensor_tensor(out=xns[:], in0=xns[:],
                                in1=rse.rearrange("p (o c) -> p o c", o=1).broadcast_to([128, 8, TSL + 1]),
                                op=ALU.mult), "sV")
                p.wait(e, "sV")
                e.tensor_reduce(out=pxn_t[:, s:s + 1], in_=xns[:, :, 1:TSL + 1], axis=AX.XY,
                                op=ALU.max, apply_absolute_value=True)
                p.inc(e.tensor_tensor(out=dxa[:], in0=xns[:, :, 0:TSL], in1=xns[:, :, 1:TSL + 1],
                                op=ALU.subtract), "sV")
                p.wait(e, "sV")
                for t in range(8):
                    ii = e.scalar_tensor_tensor(out=xm[:, t, :], in0=dxa[:, t, :],
                                                scalar=mxc[:, t:t + 1], in1=xns[:, t, 1:TSL + 1],
                                                op0=ALU.mult, op1=ALU.add)
                p.inc(ii, "sV")
            sV_xm = p.cnt["sV"]

            @blk.tensor
            def _(e):
                p.wait(e, "sV", sV_xm)
                p.wait(e, "sV")
                e.wait_ge(p.sems["qP"], qP_const)
                for kc in range(8):
                    e.matmul(psA[0:96, 0:TSL], al[:, kc, 0:96], xm[:, kc, :],
                             start=(kc == 0), stop=(kc == 7))
                for kc in range(8):
                    ii = e.matmul(psX[0:64, 0:TSL], al[:, kc, 96:160], xm[:, kc, :],
                                  start=(kc == 0), stop=(kc == 7))
                p.inc(ii, "sT")
            sT_lora = p.cnt["sT"]

            @blk.scalar
            def _(e):
                p.wait(e, "sT", sT_lora)
                e.activation(out=lhA[0:96, :], in_=psA[0:96, 0:TSL], func=AF.Tanh)
                p.inc(e.activation(out=lh2[:], in_=psX[0:64, 0:TSL], func=AF.Tanh), "sA")
            sA_lh = p.cnt["sA"]

            def emit_xx(fi, slot, eng_name, prev_tmpm=None):
                ev = {-2: p.cnt["sA"], -1: p.cnt["sA"]}
                if fi < 3:
                    rhs_ap = lhA[32 * fi:32 * fi + 32, :]
                    lbase = 32 * fi
                else:
                    rhs_ap = lh2[32 * (fi - 3):32 * (fi - 3) + 32, :]
                    lbase = 32 * (fi - 3)
                for tt in range(8):
                    pp = psP[tt % 2]

                    @blk.tensor
                    def _(e, fi=fi, tt=tt, pp=pp, rhs_ap=rhs_ap, lbase=lbase):
                        p.wait(e, "sA", sA_lh)
                        p.wait(e, "sA", ev[tt - 2])
                        if tt == 0:
                            p.wait(e, "sV")
                        p.inc(e.matmul(pp[:, 0:TSL], bl[lbase:lbase + 32, fi, 128 * tt:128 * tt + 128],
                                       rhs_ap, start=True, stop=True), "sT")

                    @blk.scalar
                    def _(e, fi=fi, tt=tt, pp=pp, prev_tmpm=prev_tmpm):
                        p.wait(e, "sT")
                        if tt == 0 and prev_tmpm is not None:
                            p.wait(e, prev_tmpm[0], prev_tmpm[1])
                        p.inc(e.activation(out=tmpm[:, tt, :], in_=pp[:, 0:TSL], func=AF.Identity,
                                           bias=lam[:, fi, tt:tt + 1], scale=1.0), "sA")
                    ev[tt] = p.cnt["sA"]
                dev = blk.vector if eng_name == "sV" else blk.gpsimd

                mid = {}

                @dev
                def _(e, slot=slot, eng_name=eng_name):
                    p.wait(e, "sA")
                    p.wait(e, "sT")
                    e.wait_ge(p.sems["qO"], p.cnt["qO"])
                    p.inc(e.tensor_tensor(out=tmpm[:], in0=tmpm[:], in1=dxa[:], op=ALU.mult), eng_name)
                    mid["v"] = p.cnt[eng_name]
                    p.wait(e, eng_name)
                    p.inc(e.tensor_tensor(out=xxb[slot][:], in0=tmpm[:], in1=xns[:, :, 1:TSL + 1],
                                          op=ALU.add), eng_name)
                return (eng_name, p.cnt[eng_name], mid["v"])

            dep_w = emit_xx(0, 0, "sV")

            @blk.tensor
            def _(e):
                p.wait(e, dep_w[0], dep_w[1])
                p.wait(e, "sA")
                for kc in range(8):
                    ii = e.matmul(psA[0:64, 0:TSL], tda[:, kc, :], xxb[0][:, kc, :],
                                  start=(kc == 0), stop=(kc == 7))
                p.inc(ii, "sT")

            @blk.scalar
            def _(e):
                p.wait(e, "sT")
                p.inc(e.activation(out=th[:], in_=psA[0:64, 0:TSL], func=AF.Tanh), "sA")

            evw = {-2: p.cnt["sA"], -1: p.cnt["sA"]}
            for tt in range(8):
                pp = psP[tt % 2]

                @blk.tensor
                def _(e, tt=tt, pp=pp):
                    p.wait(e, "sA", evw[tt - 2])
                    if tt == 0:
                        p.wait(e, "sA")
                        p.wait(e, "sV")
                    p.inc(e.matmul(pp[:, 0:TSL], tdb[:, 128 * tt:128 * tt + 128], th[:],
                                   start=True, stop=True), "sT")

                @blk.scalar
                def _(e, tt=tt, pp=pp):
                    p.wait(e, "sT")
                    p.inc(e.activation(out=expw[:, tt, :], in_=pp[:, 0:TSL], func=AF.Exp,
                                       bias=tdm[:, tt:tt + 1], scale=1.0), "sA")
                evw[tt] = p.cnt["sA"]
            sA_expw = p.cnt["sA"]

            @blk.vector
            def _(e):
                p.wait(e, "sA", sA_expw)
                for tt in range(8):
                    for cc in range(CPS):
                        ii = e.tensor_tensor_scan(out=spr[:, tt, L * cc:L * cc + L],
                                                  data0=expw[:, tt, L * cc:L * cc + L],
                                                  data1=zer[:, 0:L], initial=0.0,
                                                  op0=ALU.add, op1=ALU.add)
                p.inc(ii, "sV")
            sV_scan = p.cnt["sV"]

            @blk.scalar
            def _(e):
                p.wait(e, "sV", sV_scan)
                for tt in range(8):
                    e.activation(out=epos[:, tt, 1:TSL], in_=spr[:, tt, 0:TSL - 1], func=AF.Exp, scale=-1.0)
                    e.activation(out=dtot[:, tt, :], in_=spr[:, tt, L - 1:TSL:L], func=AF.Exp, scale=-1.0)
                    p.inc(e.activation(out=expw[:, tt, :], in_=spr[:, tt, :], func=AF.Exp, scale=1.0), "sA")
            sA_dec = p.cnt["sA"]
            eneg = expw

            @blk.vector
            def _(e):
                p.wait(e, "sA", sA_dec)
                for tt in range(8):
                    e.memset(epos[:, tt, 0:TSL:L], 1.0)
                p.inc(e.tensor_tensor(
                    out=spr.rearrange("p t (c l) -> p t c l", l=L),
                    in0=eneg.rearrange("p t (c l) -> p t c l", l=L),
                    in1=dtot.rearrange("p t (c o) -> p t c o", o=1).broadcast_to([128, 8, CPS, L]),
                    op=ALU.mult), "sV")
            sV_fk = p.cnt["sV"]
            fk = spr

            dep_g = emit_xx(4, 1, "sP", prev_tmpm=(dep_w[0], dep_w[1]))

            @blk.sync
            def _(e):
                p.wait(e, dep_g[0], dep_g[1])
                e.wait_ge(p.sems["qO"], p.cnt["qO"])
                p.inc(e.dma_start(out=fm3(xg_o)[:, :, slt], in_=xxb[1][:]), "qO", 16)

            dep_r = emit_xx(3, 0, "sP", prev_tmpm=(dep_g[0], dep_g[1]))
            evr = {-2: (p.cnt["sA"], p.cnt["sV"]), -1: (p.cnt["sA"], p.cnt["sV"])}
            for tt in range(8):
                pp = psP[tt % 2]

                @blk.tensor
                def _(e, tt=tt, pp=pp):
                    p.wait(e, dep_r[0], dep_r[1])
                    p.wait(e, "sA", evr[tt - 2][0])
                    p.wait(e, "sV", evr[tt - 2][1])
                    if tt == 0:
                        p.wait(e, "sA")
                        p.wait(e, "sV")
                    for kc in range(8):
                        ii = e.matmul(pp[:, 0:TSL], wr[:, kc, 128 * tt:128 * tt + 128],
                                      xxb[0][:, kc, :], start=(kc == 0), stop=(kc == 7))
                    p.inc(ii, "sT")

                @blk.scalar
                def _(e, tt=tt, pp=pp):
                    p.wait(e, "sT")
                    p.inc(e.activation(out=rf[:, tt, :], in_=pp[:, 0:TSL], func=AF.Identity), "sA")

                @blk.vector
                def _(e, tt=tt, pp=pp):
                    p.wait(e, "sT")
                    if tt == 0:
                        p.wait(e, "sV", sV_fk)
                    p.inc(e.scalar_tensor_tensor(out=qtb[:, tt, :], in0=pp[:, 0:TSL], scalar=1.0,
                                                 in1=epos[:, tt, :], op0=ALU.mult, op1=ALU.mult), "sV")
                evr[tt] = (p.cnt["sA"], p.cnt["sV"])
            sA_r = p.cnt["sA"]
            sV_q = p.cnt["sV"]

            dep_k = emit_xx(1, 1, "sP", prev_tmpm=(dep_r[0], dep_r[1]))
            rkb = xm
            evk = {-2: p.cnt["sV"], -1: p.cnt["sV"]}
            for tt in range(8):
                pp = psP[tt % 2]

                @blk.tensor
                def _(e, tt=tt, pp=pp):
                    p.wait(e, dep_k[0], dep_k[1])
                    p.wait(e, "sV", evk[tt - 2])
                    if tt == 0:
                        p.wait(e, "sA")
                        p.wait(e, "sV")
                    for kc in range(8):
                        ii = e.matmul(pp[:, 0:TSL], wk[:, kc, 128 * tt:128 * tt + 128],
                                      xxb[1][:, kc, :], start=(kc == 0), stop=(kc == 7))
                    p.inc(ii, "sT")

                @blk.vector
                def _(e, tt=tt, pp=pp):
                    p.wait(e, "sT")
                    if tt == 0:
                        p.wait(e, "sA", sA_r)
                    e.scalar_tensor_tensor(out=ktb[:, tt, :], in0=pp[:, 0:TSL], scalar=1.0,
                                           in1=eneg[:, tt, :], op0=ALU.mult, op1=ALU.mult)
                    e.scalar_tensor_tensor(out=khb[:, tt, :], in0=pp[:, 0:TSL], scalar=1.0,
                                           in1=fk[:, tt, :], op0=ALU.mult, op1=ALU.mult)
                    p.inc(e.scalar_tensor_tensor(out=rkb[:, tt, :], in0=pp[:, 0:TSL], scalar=1.0,
                                                 in1=rf[:, tt, :], op0=ALU.mult, op1=ALU.mult), "sV")
                evk[tt] = p.cnt["sV"]
            sV_k = p.cnt["sV"]

            @blk.sync
            def _(e):
                p.wait(e, "sV", sV_k)
                e.wait_ge(p.sems["qO"], p.cnt["qO"])
                p.inc(e.dma_start(out=fm3(qt_o)[:, :, slt], in_=qtb[:]), "qO", 16)
                p.inc(e.dma_start(out=fm3(kt_o)[:, :, slt], in_=ktb[:]), "qO", 16)

            dep_v = emit_xx(2, 0, "sP", prev_tmpm=(dep_k[0], dep_k[1]))
            for st in range(CPS):
                for hh in range(2):
                    pp = psP[(st * 2 + hh) % 2]

                    @blk.tensor
                    def _(e, st=st, hh=hh, pp=pp):
                        p.wait(e, dep_v[0], dep_v[1])
                        p.wait(e, "sA")
                        p.wait(e, "sV")
                        for kc in range(8):
                            ii = e.matmul(pp[:, 0:512], xxb[0][:, kc, L * st:L * st + L],
                                          wv[:, kc, 512 * hh:512 * hh + 512],
                                          start=(kc == 0), stop=(kc == 7))
                        p.inc(ii, "sT")

                    @blk.scalar
                    def _(e, st=st, hh=hh, pp=pp):
                        p.wait(e, "sT")
                        p.inc(e.activation(out=vtb[:, st, 512 * hh:512 * hh + 512],
                                           in_=pp[:, 0:512], func=AF.Identity), "sA")
            sA_v = p.cnt["sA"]

            @blk.sync
            def _(e):
                p.wait(e, "sA", sA_v)
                e.wait_ge(p.sems["qO"], p.cnt["qO"])
                for st in range(CPS):
                    p.inc(e.dma_start(out=vq_o[c0 + L * st:c0 + L * st + L, :], in_=vtb[:, st, :]), "qO", 16)

            @blk.tensor
            def _(e):
                p.wait(e, "sV", sV_k)
                for tt in range(8):
                    ii = e.matmul(psX[0:16, 0:TSL], u16s[:, tt, :], rkb[:, tt, :],
                                  start=(tt == 0), stop=(tt == 7))
                p.inc(ii, "sT")

            @blk.scalar
            def _(e):
                p.wait(e, "sT")
                p.inc(e.activation(out=bsl[:], in_=psX[0:16, 0:TSL], func=AF.Identity), "sA")
            sA_b = p.cnt["sA"]

            @blk.sync
            def _(e):
                p.wait(e, "sA", sA_b)
                e.wait_ge(p.sems["qO"], p.cnt["qO"])
                p.inc(e.dma_start(out=b_o[:, slt], in_=bsl[:]), "qO", 16)

            for cc in range(CPS):
                gc = s * CPS + cc
                cur = gc % 2
                nxt = (gc + 1) % 2

                qtp = "qT%d" % (cc % 2)

                @blk.scalar
                def _(e, cc=cc, qtp=qtp):
                    p.wait(e, "sV", sV_k)
                    p.wait(e, "sT")
                    e.wait_ge(p.sems[qtp], p.cnt[qtp])
                    for g in range(8):
                        p.inc(e.dma_start(out=khT[:, g, :],
                                          in_=khb[:, g, L * cc:L * cc + L],
                                          transpose=True), qtp, 16)
                qT_k = p.cnt[qtp]

                @blk.tensor
                def _(e, cc=cc, qT_k=qT_k, qtp=qtp):
                    e.wait_ge(p.sems[qtp], qT_k)
                    p.wait(e, "sV")
                    for h in range(16):
                        ii = e.matmul(psU[64 * (h % 2):64 * (h % 2) + 64, 64 * (h // 2):64 * (h // 2) + 64],
                                      khT[:, h // 2, 64 * (h % 2):64 * (h % 2) + 64],
                                      vtb[:, cc, 64 * h:64 * h + 64], start=True, stop=True)
                    p.inc(ii, "sT")
                sT_U = p.cnt["sT"]

                @blk.sync
                def _(e, gc=gc, cur=cur):
                    p.wait(e, "sV")
                    e.wait_ge(p.sems["qO"], p.cnt["qO"])
                    p.inc(e.dma_start(out=traj_o[:, 512 * gc:512 * gc + 512],
                                      in_=sown[cur].rearrange("p g j -> p (g j)")), "qO", 16)
                    p.inc(e.dma_start(out=dpre_o[:, 8 * gc:8 * gc + 8], in_=dprer[cur][:]), "qO", 16)
                qS_st = p.cnt["qO"]

                @blk.vector
                def _(e, cc=cc, cur=cur, nxt=nxt, sT_U=sT_U, qS_st=qS_st):
                    p.wait(e, "sT", sT_U)
                    e.wait_ge(p.sems["qO"], qS_st)
                    for g in range(8):
                        e.scalar_tensor_tensor(out=sown[nxt][:, g, :], in0=sown[cur][:, g, :],
                                               scalar=dtot[:, g, cc:cc + 1],
                                               in1=psU[:, 64 * g:64 * g + 64],
                                               op0=ALU.mult, op1=ALU.add)
                    p.inc(e.tensor_tensor(out=dprer[nxt][:], in0=dprer[cur][:],
                                          in1=dtot[:, :, cc], op=ALU.mult), "sV")

        fin = NCH % 2

        @blk.vector
        def _(e):
            p.inc(e.tensor_reduce(out=pxn_t[:, NSL:NSL + 1], in_=pxn_t[:, 0:NSL],
                                  axis=AX.X, op=ALU.max), "sV")
        sV_pxn = p.cnt["sV"]

        @blk.sync
        def _(e):
            p.wait(e, "sV", sV_pxn)
            p.inc(e.dma_start(out=pxn_o[:, :], in_=pxn_t[:, NSL:NSL + 1]), "qO", 16)
            p.inc(e.dma_start(out=utot_o[:, :], in_=sown[fin].rearrange("p g j -> p (g j)")), "qO", 16)
            for q in ("qS", "qA", "qP", "qO", "qB0", "qB1", "qT0", "qT1"):
                e.wait_ge(p.sems[q], p.cnt[q])

    return p


def _sigma_cols(p, blk, maxrow_ap, nmax, NV, scr_dram, out_sb, qname="qA", bufs=None):
    nc = p.nc
    if bufs is None:
        mrow = p.sb(f"sgm_{id(maxrow_ap) % 997}", [1, nmax * NV])
        mred = p.sb(f"sgr_{id(maxrow_ap) % 997}", [1, 4 * nmax])
    else:
        mrow, mred = bufs

    @blk.scalar
    def _(e):
        p.inc(e.dma_start(out=mrow[:], in_=maxrow_ap[:, :]), qname, 16)
    qv0 = p.cnt[qname]

    @blk.vector
    def _(e):
        e.wait_ge(p.sems[qname], qv0)
        for i in range(nmax):
            p.inc(e.tensor_reduce(out=mred[:, 4 * i:4 * i + 1], in_=mrow[:, NV * i:NV * (i + 1)],
                                  axis=AX.X, op=ALU.max), "sV")
            p.wait(e, "sV")
            p.inc(e.tensor_scalar(out=mred[:, 4 * i + 1:4 * i + 2], in0=mred[:, 4 * i:4 * i + 1],
                                  scalar1=1.0 / QMAX, scalar2=1e-12, op0=ALU.mult, op1=ALU.max), "sV")
            p.wait(e, "sV")
            p.inc(e.reciprocal(out=mred[:, 4 * i + 2:4 * i + 3], in_=mred[:, 4 * i + 1:4 * i + 2]), "sV")
            p.wait(e, "sV")
            p.inc(e.tensor_scalar(out=mred[:, 4 * i + 3:4 * i + 4], in0=mred[:, 4 * i + 1:4 * i + 2],
                                  scalar1=1.0 / 65536.0, scalar2=None, op0=ALU.mult), "sV")
            p.wait(e, "sV")

    @blk.scalar
    def _(e):
        p.wait(e, "sV")
        p.inc(e.dma_start(out=scr_dram[0:1, :], in_=mred[:]), qname, 16)
        e.wait_ge(p.sems[qname], p.cnt[qname])
        p.inc(e.dma_start(out=out_sb[:], in_=scr_dram[0:1, :].broadcast_to([128, 4 * nmax])), qname, 16)
    return p.cnt[qname]


def emit_xq(p, blk, TC, TSL, x_fm, murstd, sgc, xq, bufs, sg_ready=None):
    NSL = TC // TSL
    xfs, mue, rse, xns, tq = bufs["xfs"], bufs["mue"], bufs["rse"], bufs["xns"], bufs["tq"]
    for s in range(NSL):
        c0 = s * TSL

        @blk.sync
        def _(e, s=s, c0=c0):
            if s > 0:
                p.wait(e, "sV")
            p.inc(e.dma_start(out=xfs[:, :, 0:TSL], in_=fm3(x_fm)[:, :, c0 + 1:c0 + TSL + 1]), "qS", 16)
            p.inc(e.dma_start(out=mue[:, 0:TSL], in_=murstd[0:1, c0 + 1:c0 + TSL + 1].broadcast_to([128, TSL])), "qS", 16)
            p.inc(e.dma_start(out=rse[:, 0:TSL], in_=murstd[1:2, c0 + 1:c0 + TSL + 1].broadcast_to([128, TSL])), "qS", 16)
        qS_ld = p.cnt["qS"]

        @blk.vector
        def _(e, s=s, c0=c0, qS_ld=qS_ld):
            e.wait_ge(p.sems["qS"], qS_ld)
            if sg_ready is not None:
                e.wait_ge(p.sems[sg_ready[0]], sg_ready[1])
            p.inc(e.tensor_tensor(out=xns[:, :, 0:TSL], in0=xfs[:, :, 0:TSL],
                                  in1=mue[:, 0:TSL].rearrange("p (o c) -> p o c", o=1).broadcast_to([128, 8, TSL]),
                                  op=ALU.subtract), "sV")
            p.wait(e, "sV")
            p.inc(e.tensor_tensor(out=xns[:, :, 0:TSL], in0=xns[:, :, 0:TSL],
                                  in1=rse[:, 0:TSL].rearrange("p (o c) -> p o c", o=1).broadcast_to([128, 8, TSL]),
                                  op=ALU.mult), "sV")
            p.wait(e, "sV")
            p.inc(e.tensor_scalar(out=tq[:, :, 0:TSL], in0=xns[:, :, 0:TSL], scalar1=sgc[:, 2:3],
                                  scalar2=RND, op0=ALU.mult, op1=ALU.add), "sV")
            p.wait(e, "sV")
            p.inc(e.tensor_scalar(out=xq[:, :, c0:c0 + TSL], in0=tq[:, :, 0:TSL], scalar1=RND,
                                  scalar2=sgc[:, 1:2], op0=ALU.subtract, op1=ALU.mult), "sV")


def emit_dequant(p, blk, W_dram, wdq_dram, O, tiles_sb, rnd_col, qd="qA"):
    CW = W_dram.shape[1]
    NB = CW // BLK
    NT = O // 128
    wtile, wq, babs, sblk, rblk = tiles_sb
    qp_mark = {}
    q_mark = {}
    m_mark = {}

    def stageA(j):
        bi = j % 2

        @blk.scalar
        def _(e, j=j, bi=bi):
            if j >= 2:
                p.wait(e, "sV", q_mark[j - 2])
            e.wait_ge(p.sems[qd], p.cnt[qd])
            p.inc(e.dma_start(out=wtile[bi][:], in_=W_dram[128 * j:128 * (j + 1), :]), qd, 16)
        qv = p.cnt[qd]

        @blk.vector
        def _(e, j=j, bi=bi, qv=qv):
            e.wait_ge(p.sems[qd], qv)
            if j >= 2:
                p.wait(e, "sV", q_mark[j - 2])
            p.inc(e.tensor_reduce(out=babs[bi][:], in_=wtile[bi].rearrange("p (b k) -> p b k", k=BLK),
                                  axis=AX.X, op=ALU.max, apply_absolute_value=True), "sV")
            p.wait(e, "sV")
            p.inc(e.tensor_scalar(out=sblk[bi][:], in0=babs[bi][:], scalar1=1.0 / WQMAX, scalar2=1e-12,
                                  op0=ALU.mult, op1=ALU.max), "sV")
            p.wait(e, "sV")
            p.inc(e.reciprocal(out=rblk[bi][:], in_=sblk[bi][:]), "sV")
            p.wait(e, "sV")
            p.inc(e.tensor_tensor(out=wtile[bi].rearrange("p (b k) -> p b k", k=BLK),
                                  in0=wtile[bi].rearrange("p (b k) -> p b k", k=BLK),
                                  in1=rblk[bi].rearrange("p (b o) -> p b o", o=1).broadcast_to([128, NB, BLK]),
                                  op=ALU.mult), "sV")
        m_mark[j] = p.cnt["sV"]

    def stageB(j):
        bi = j % 2
        sV_m = m_mark[j]

        @blk.scalar
        def _(e, j=j, bi=bi, sV_m=sV_m):
            p.wait(e, "sV", sV_m)
            p.inc(e.activation(out=wtile[bi][:], in_=wtile[bi][:], func=AF.Identity,
                               bias=rnd_col, scale=1.0), "sA")
        sA_m = p.cnt["sA"]

        @blk.vector
        def _(e, j=j, bi=bi, sA_m=sA_m):
            p.wait(e, "sA", sA_m)
            if j >= 2:
                e.wait_ge(p.sems[qp_mark[j - 2][0]], qp_mark[j - 2][1])
            p.inc(e.scalar_tensor_tensor(out=wq[bi].rearrange("p (b k) -> p b k", k=BLK),
                                         in0=wtile[bi].rearrange("p (b k) -> p b k", k=BLK),
                                         scalar=RND, in1=sblk[bi].rearrange("p (b o) -> p b o", o=1).broadcast_to([128, NB, BLK]),
                                         op0=ALU.subtract, op1=ALU.mult), "sV")
        q_mark[j] = p.cnt["sV"]

        @blk.gpsimd
        def _(e, j=j, bi=bi, sV_q=q_mark[j]):
            p.wait(e, "sV", sV_q)
            qps = "qP%d" % bi
            p.inc(e.dma_start(out=wdq_dram[128 * j:128 * (j + 1), :], in_=wq[bi][:]), qps, 16)
        qp_mark[j] = ("qP%d" % bi, p.cnt["qP%d" % bi])

    for it in range(NT + 1):
        if it < NT:
            stageA(it)
        if it >= 1:
            stageB(it - 1)
    return p.cnt["qP"]


def emit_upgate_mm(p, blk, wdq_dram, wT, xq, TC, psD, utile, pxacc, col0,
                   i16_out=None, qi16=None, sgu=None, sig_chain=None, qw="qA",
                   sg_ready=None):
    NT = FFI // 128
    NTC = TC // 512
    for kc in range(8):
        @blk.scalar
        def _(e, kc=kc):
            if kc == 0:
                p.wait(e, "sT")
            for qq in ("qP", "qP0", "qP1"):
                e.wait_ge(p.sems[qq], p.cnt[qq])
            p.inc(e.dma_start(out=wT[:, kc, :], in_=wdq_dram[0:FFI, 128 * kc:128 * (kc + 1)],
                              transpose=True), qw, 16)
    qv_w = p.cnt[qw]
    sV_xq = p.cnt["sV"]
    evac_done = {-2: p.cnt["sA"], -1: p.cnt["sA"]}
    cons_done = {-2: p.cnt["sV"], -1: p.cnt["sV"]}
    ship_done = {-2: ("qO0", p.cnt.get("qO0", 0)), -1: ("qO1", p.cnt.get("qO1", 0))}
    for ot in range(NT):
        for tch in range(NTC):
            idx = ot * NTC + tch
            pp = psD[idx % 2]

            @blk.tensor
            def _(e, ot=ot, tch=tch, pp=pp, idx=idx):
                e.wait_ge(p.sems[qw], qv_w)
                p.wait(e, "sV", sV_xq)
                p.wait(e, "sA", evac_done[idx - 2])
                for kc in range(8):
                    ii = e.matmul(pp[:, 0:512], wT[:, kc, 128 * ot:128 * ot + 128],
                                  xq[:, kc, 512 * tch:512 * tch + 512],
                                  start=(kc == 0), stop=(kc == 7))
                p.inc(ii, "sT")

            @blk.scalar
            def _(e, ot=ot, tch=tch, pp=pp, idx=idx):
                p.wait(e, "sT")
                p.wait(e, "sV", cons_done[idx - 2])
                if i16_out is None:
                    p.inc(e.activation(out=utile[idx % 2][:], in_=pp[:, 0:512], func=AF.Identity), "sA")
                else:
                    if sg_ready is not None:
                        e.wait_ge(p.sems[sg_ready[0]], sg_ready[1])
                    p.inc(e.activation(out=utile[idx % 2][:], in_=pp[:, 0:512], func=AF.Identity,
                                       bias=p.constcol(RND), scale=sgu[:, 2:3]), "sA")
            evac_done[idx] = p.cnt["sA"]
            sA_e = p.cnt["sA"]

            @blk.vector
            def _(e, ot=ot, tch=tch, idx=idx, sA_e=sA_e):
                p.wait(e, "sA", sA_e)
                ut = utile[idx % 2]
                if i16_out is None:
                    p.inc(e.tensor_reduce(out=pxacc[:, col0 + idx:col0 + idx + 1], in_=ut[:],
                                          axis=AX.X, op=ALU.max, apply_absolute_value=True), "sV")
                else:
                    e.wait_ge(p.sems[ship_done[idx - 2][0]], ship_done[idx - 2][1])
                    p.inc(e.tensor_scalar(out=qi16[idx % 2][:], in0=ut[:], scalar1=RND, scalar2=None,
                                          op0=ALU.subtract), "sV")
            if i16_out is not None:
                sV_q = p.cnt["sV"]

                qos = "qO%d" % (idx % 2)

                @blk.sync
                def _(e, ot=ot, tch=tch, idx=idx, sV_q=sV_q, qos=qos):
                    p.wait(e, "sV", sV_q)
                    p.inc(e.dma_start(out=i16_out[128 * ot:128 * (ot + 1), 512 * tch:512 * (tch + 1)],
                                      in_=qi16[idx % 2][:]), qos, 16)
                ship_done[idx] = (qos, p.cnt[qos])
                if sig_chain is not None:
                    gqf, sigt, sigq, g2t = sig_chain
                    bi = idx % 2

                    @blk.vector
                    def _(e, idx=idx, bi=bi, sV_q=sV_q):
                        p.wait(e, "sV", sV_q)
                        p.inc(e.tensor_scalar(out=gqf[bi][:], in0=qi16[bi][:], scalar1=sgu[:, 1:2],
                                              scalar2=None, op0=ALU.mult), "sV")
                    sV_g = p.cnt["sV"]

                    @blk.scalar
                    def _(e, idx=idx, bi=bi, sV_g=sV_g):
                        p.wait(e, "sV", sV_g)
                        p.inc(e.activation(out=sigt[bi][:], in_=gqf[bi][:], func=AF.Tanh, scale=0.5), "sA")
                    sA_s = p.cnt["sA"]

                    @blk.vector
                    def _(e, idx=idx, bi=bi, sA_s=sA_s):
                        p.wait(e, "sA", sA_s)
                        p.inc(e.tensor_scalar(out=sigq[bi][:], in0=sigt[bi][:], scalar1=32768.0, scalar2=32768.0 + RND,
                                              op0=ALU.mult, op1=ALU.add), "sV")
                        p.wait(e, "sV")
                        p.inc(e.tensor_scalar(out=sigq[bi][:], in0=sigq[bi][:], scalar1=RND + 65535.0,
                                              scalar2=RND, op0=ALU.min, op1=ALU.subtract), "sV")
                        p.wait(e, "sV")
                        p.inc(e.scalar_tensor_tensor(out=g2t[bi][:], in0=sigq[bi][:], scalar=1.0 / 65536.0,
                                                     in1=gqf[bi][:], op0=ALU.mult, op1=ALU.mult), "sV")
                        p.wait(e, "sV")
                        p.inc(e.tensor_reduce(out=pxacc[:, col0 + idx:col0 + idx + 1], in_=g2t[bi][:],
                                              axis=AX.X, op=ALU.max, apply_absolute_value=True), "sV")
            cons_done[idx] = p.cnt["sV"]
    return


def build_L2(TC=2048, TSL=512):
    TTP = ceil_div(TC + 1, 128) * 128
    NTC = TC // 512
    p = Prog("L2")
    nc = p.nc
    x_fm = p.din("x_fm", [C, TC + 1])
    murstd = p.din("murstd", [2, TTP])
    pxn_row = p.din("pxn_row", [1, 1024])
    w_up = p.din("w_up", [FFI, C])
    w_gate = p.din("w_gate", [FFI, C])
    pxu_o = p.dout("pxu", [128, 1])
    pxg_o = p.dout("pxg", [128, 1])
    wdqu_o = p.dout("wdq_up", [FFI, C], BF16)
    wdqg_o = p.dout("wdq_gate", [FFI, C], BF16)
    sgscr = nc.dram_tensor("sgscr", [1, 4], F32, kind="Internal").ap()

    sgx = p.sb("sgx", [128, 4])
    xq = p.sb("xq", [128, 8, TC], BF16)
    wT = p.sb("wT", [128, 8, FFI], BF16)
    wtile = [p.sb(f"wt{i}", [128, C]) for i in range(2)]
    wq2 = [p.sb(f"wq{i}", [128, C], BF16) for i in range(2)]
    babs = [p.sb(f"babs{i}", [128, C // BLK]) for i in range(2)]
    sblk = [p.sb(f"sblk{i}", [128, C // BLK]) for i in range(2)]
    rblk = [p.sb(f"rblk{i}", [128, C // BLK]) for i in range(2)]
    utile = [p.sb(f"ut{i}", [128, 512]) for i in range(2)]
    pxacc = p.sb("pxacc", [128, 2 * 24 * NTC + 2])
    bufs = dict(xfs=p.sb("xfs", [128, 8, TSL]), mue=p.sb("mue", [128, TSL]),
                rse=p.sb("rse", [128, TSL]), xns=p.sb("xns", [128, 8, TSL]),
                tq=p.sb("tq", [128, 8, TSL]))
    rnd_col = p.constcol(RND)
    psD = [p.ps(f"psD{i}", [128, 512]) for i in range(2)]

    for s in ("qS", "qA", "qP", "qP0", "qP1", "qG", "qD", "qW", "qO", "sV", "sA", "sP", "sT"):
        p.sem(s)

    with nc.Block() as blk:
        @blk.vector
        def _(e):
            p.emit_const_memsets(e, "sV")
        qsg = _sigma_cols(p, blk, pxn_row, 1, 1024, sgscr, sgx, "qG")
        emit_xq(p, blk, TC, TSL, x_fm, murstd, sgx, xq, bufs, sg_ready=("qG", qsg))
        emit_dequant(p, blk, w_up, wdqu_o, FFI, (wtile, wq2, babs, sblk, rblk), rnd_col, "qD")
        emit_upgate_mm(p, blk, wdqu_o, wT, xq, TC, psD, utile, pxacc, 0, qw="qW")
        emit_dequant(p, blk, w_gate, wdqg_o, FFI, (wtile, wq2, babs, sblk, rblk), rnd_col, "qD")
        emit_upgate_mm(p, blk, wdqg_o, wT, xq, TC, psD, utile, pxacc, 24 * NTC, qw="qW")

        NCOL = 24 * NTC

        @blk.vector
        def _(e):
            p.wait(e, "sA")
            p.wait(e, "sV")
            p.inc(e.tensor_reduce(out=pxacc[:, 2 * NCOL:2 * NCOL + 1], in_=pxacc[:, 0:NCOL],
                                  axis=AX.X, op=ALU.max), "sV")
            p.wait(e, "sV")
            p.inc(e.tensor_reduce(out=pxacc[:, 2 * NCOL + 1:2 * NCOL + 2], in_=pxacc[:, NCOL:2 * NCOL],
                                  axis=AX.X, op=ALU.max), "sV")
        sV_fin = p.cnt["sV"]

        @blk.sync
        def _(e):
            p.wait(e, "sV", sV_fin)
            p.inc(e.dma_start(out=pxu_o[:, :], in_=pxacc[:, 2 * NCOL:2 * NCOL + 1]), "qS", 16)
            p.inc(e.dma_start(out=pxg_o[:, :], in_=pxacc[:, 2 * NCOL + 1:2 * NCOL + 2]), "qS", 16)
            for q in ("qS", "qG", "qD", "qW", "qP", "qP0", "qP1", "qO"):
                e.wait_ge(p.sems[q], p.cnt[q])
    return p


def build_L3(TC=2048, TSL=512):
    TTP = ceil_div(TC + 1, 128) * 128
    NTC = TC // 512
    p = Prog("L3")
    nc = p.nc
    x_fm = p.din("x_fm", [C, TC + 1])
    murstd = p.din("murstd", [2, TTP])
    pxn_row = p.din("pxn_row", [1, 1024])
    pxug_row = p.din("pxug_row", [1, 2048])
    wdqu = p.din("wdq_up", [FFI, C], BF16)
    wdqg = p.din("wdq_gate", [FFI, C], BF16)
    upq_o = p.dout("upq", [FFI, TC], I16)
    gq_o = p.dout("gq", [FFI, TC], I16)
    pxg2_o = p.dout("pxg2", [128, 1])
    sgscr = nc.dram_tensor("sgscr", [1, 4], F32, kind="Internal").ap()
    sgscr2 = nc.dram_tensor("sgscr2", [1, 8], F32, kind="Internal").ap()

    sgx = p.sb("sgx", [128, 4])
    sgug = p.sb("sgug", [128, 8])
    xq = p.sb("xq", [128, 8, TC], BF16)
    wT = p.sb("wT", [128, 8, FFI], BF16)
    utile = [p.sb(f"ut{i}", [128, 512]) for i in range(2)]
    qi16 = [p.sb(f"qi{i}", [128, 512], I16) for i in range(2)]
    gqf = [p.sb(f"gqf{i}", [128, 512]) for i in range(2)]
    sigt = [p.sb(f"sigt{i}", [128, 512]) for i in range(2)]
    sigq = [p.sb(f"sigq{i}", [128, 512]) for i in range(2)]
    g2t = [p.sb(f"g2t{i}", [128, 512]) for i in range(2)]
    pxacc = p.sb("pxacc", [128, 24 * NTC + 1])
    bufs = dict(xfs=p.sb("xfs", [128, 8, TSL]), mue=p.sb("mue", [128, TSL]),
                rse=p.sb("rse", [128, TSL]), xns=p.sb("xns", [128, 8, TSL]),
                tq=p.sb("tq", [128, 8, TSL]))
    rnd_col = p.constcol(RND)
    psD = [p.ps(f"psD{i}", [128, 512]) for i in range(2)]

    for s in ("qS", "qA", "qP", "qP0", "qP1", "qG", "qG2", "qW", "qO", "qO0", "qO1",
              "sV", "sA", "sP", "sT"):
        p.sem(s)

    with nc.Block() as blk:
        @blk.vector
        def _(e):
            p.emit_const_memsets(e, "sV")
        qs1 = _sigma_cols(p, blk, pxn_row, 1, 1024, sgscr, sgx, "qG")
        qsg = _sigma_cols(p, blk, pxug_row, 2, 1024, sgscr2, sgug, "qG2")
        emit_xq(p, blk, TC, TSL, x_fm, murstd, sgx, xq, bufs, sg_ready=("qG", qs1))
        emit_upgate_mm(p, blk, wdqu, wT, xq, TC, psD, utile, pxacc, 0,
                       i16_out=upq_o, qi16=qi16, sgu=sgug[:, 0:4], qw="qW",
                       sg_ready=("qG2", qsg))
        emit_upgate_mm(p, blk, wdqg, wT, xq, TC, psD, utile, pxacc, 0,
                       i16_out=gq_o, qi16=qi16, sgu=sgug[:, 4:8],
                       sig_chain=(gqf, sigt, sigq, g2t), qw="qW",
                       sg_ready=("qG2", qsg))
        NCOL = 24 * NTC

        @blk.vector
        def _(e):
            p.wait(e, "sA")
            p.wait(e, "sV")
            p.inc(e.tensor_reduce(out=pxacc[:, NCOL:NCOL + 1], in_=pxacc[:, 0:NCOL],
                                  axis=AX.X, op=ALU.max), "sV")
        sV_fin = p.cnt["sV"]

        @blk.sync
        def _(e):
            p.wait(e, "sV", sV_fin)
            p.inc(e.dma_start(out=pxg2_o[:, :], in_=pxacc[:, NCOL:NCOL + 1]), "qS", 16)
            for q in ("qS", "qG", "qG2", "qW", "qO", "qO0", "qO1", "qP0", "qP1"):
                e.wait_ge(p.sems[q], p.cnt[q])
    return p


def build_L4(TC=2048):
    NT = FFI // 128
    NTC = TC // 512
    p = Prog("L4")
    nc = p.nc
    upq = p.din("upq", [FFI, TC], I16)
    gq = p.din("gq", [FFI, TC], I16)
    pxug_row = p.din("pxug_row", [1, 2048])
    pxg2_row = p.din("pxg2_row", [1, 1024])
    pxo_o = p.dout("pxo", [128, 1])
    sgscr2 = nc.dram_tensor("sgscr2", [1, 8], F32, kind="Internal").ap()
    sgscr3 = nc.dram_tensor("sgscr3", [1, 4], F32, kind="Internal").ap()

    sgug = p.sb("sgug", [128, 8])
    sg3 = p.sb("sg3", [128, 4])
    ui = [p.sb(f"ui{i}", [128, 512], I16) for i in range(2)]
    gi = [p.sb(f"gi{i}", [128, 512], I16) for i in range(2)]
    gqf = [p.sb(f"gqf{i}", [128, 512]) for i in range(2)]
    sigt = [p.sb(f"sigt{i}", [128, 512]) for i in range(2)]
    sigq = [p.sb(f"sigq{i}", [128, 512]) for i in range(2)]
    g2t = [p.sb(f"g2t{i}", [128, 512]) for i in range(2)]
    g2q = [p.sb(f"g2q{i}", [128, 512]) for i in range(2)]
    ot = [p.sb(f"ot{i}", [128, 512]) for i in range(2)]
    NTL = NT * NTC
    pxacc = p.sb("pxacc", [128, NTL + 1])
    rnd_col = p.constcol(RND)

    for s in ("qS", "qG", "qG2", "qL0", "qL1", "sV", "sA", "sP", "sT"):
        p.sem(s)

    with nc.Block() as blk:
        @blk.vector
        def _(e):
            p.emit_const_memsets(e, "sV")
        qg1 = _sigma_cols(p, blk, pxug_row, 2, 1024, sgscr2, sgug, "qG")
        qg2 = _sigma_cols(p, blk, pxg2_row, 1, 1024, sgscr3, sg3, "qG2")

        chain_end = {}
        for j in range(NT):
            for tc4 in range(NTC):
                idx = j * NTC + tc4
                bi = idx % 2
                ql = "qL%d" % bi

                @blk.sync
                def _(e, j=j, tc4=tc4, bi=bi, idx=idx, ql=ql):
                    if idx >= 2:
                        p.wait(e, "sV", chain_end[idx - 2])
                    p.inc(e.dma_start(out=ui[bi][:], in_=upq[128 * j:128 * (j + 1), 512 * tc4:512 * (tc4 + 1)]), ql, 16)
                    p.inc(e.dma_start(out=gi[bi][:], in_=gq[128 * j:128 * (j + 1), 512 * tc4:512 * (tc4 + 1)]), ql, 16)
                qld = p.cnt[ql]

                @blk.vector
                def _(e, bi=bi, idx=idx, qld=qld, ql=ql):
                    e.wait_ge(p.sems[ql], qld)
                    if idx == 0:
                        e.wait_ge(p.sems["qG"], qg1)
                        e.wait_ge(p.sems["qG2"], qg2)
                    if idx >= 1:
                        p.wait(e, "sV", chain_end.get(idx - 1, 0))
                    p.inc(e.tensor_scalar(out=gqf[bi][:], in0=gi[bi][:], scalar1=sgug[:, 5:6],
                                          scalar2=None, op0=ALU.mult), "sV")
                sV_g = p.cnt["sV"]

                @blk.scalar
                def _(e, bi=bi, sV_g=sV_g):
                    p.wait(e, "sV", sV_g)
                    p.inc(e.activation(out=sigt[bi][:], in_=gqf[bi][:], func=AF.Tanh, scale=0.5), "sA")
                sA_s = p.cnt["sA"]

                @blk.vector
                def _(e, bi=bi, idx=idx, sA_s=sA_s):
                    p.wait(e, "sA", sA_s)
                    p.inc(e.tensor_scalar(out=sigq[bi][:], in0=sigt[bi][:], scalar1=32768.0, scalar2=32768.0 + RND,
                                          op0=ALU.mult, op1=ALU.add), "sV")
                    p.wait(e, "sV")
                    p.inc(e.tensor_scalar(out=sigq[bi][:], in0=sigq[bi][:], scalar1=RND + 65535.0,
                                          scalar2=RND, op0=ALU.min, op1=ALU.subtract), "sV")
                    p.wait(e, "sV")
                    p.inc(e.scalar_tensor_tensor(out=g2t[bi][:], in0=sigq[bi][:], scalar=1.0 / 65536.0,
                                                 in1=gqf[bi][:], op0=ALU.mult, op1=ALU.mult), "sV")
                    p.wait(e, "sV")
                    p.inc(e.tensor_scalar(out=g2q[bi][:], in0=g2t[bi][:], scalar1=sg3[:, 2:3],
                                          scalar2=RND, op0=ALU.mult, op1=ALU.add), "sV")
                    p.wait(e, "sV")
                    p.inc(e.tensor_scalar(out=g2q[bi][:], in0=g2q[bi][:], scalar1=RND,
                                          scalar2=sg3[:, 1:2], op0=ALU.subtract, op1=ALU.mult), "sV")
                    p.wait(e, "sV")
                    p.inc(e.scalar_tensor_tensor(out=ot[bi][:], in0=ui[bi][:], scalar=sgug[:, 1:2],
                                                 in1=g2q[bi][:], op0=ALU.mult, op1=ALU.mult), "sV")
                    p.wait(e, "sV")
                    p.inc(e.tensor_reduce(out=pxacc[:, idx:idx + 1], in_=ot[bi][:],
                                          axis=AX.X, op=ALU.max, apply_absolute_value=True), "sV")
                chain_end[idx] = p.cnt["sV"]

        @blk.vector
        def _(e):
            p.wait(e, "sV")
            p.wait(e, "sA")
            p.inc(e.tensor_reduce(out=pxacc[:, NTL:NTL + 1], in_=pxacc[:, 0:NTL],
                                  axis=AX.X, op=ALU.max), "sV")
        sV_fin = p.cnt["sV"]

        @blk.sync
        def _(e):
            p.wait(e, "sV", sV_fin)
            p.inc(e.dma_start(out=pxo_o[:, :], in_=pxacc[:, NTL:NTL + 1]), "qS", 16)
            for q in ("qS", "qG", "qG2", "qL0", "qL1"):
                e.wait_ge(p.sems[q], p.cnt[q])
    return p


def build_L5(TC=2048):
    NCH = TC // L
    NQ = TC // 512
    p = Prog("L5")
    nc = p.nc
    x_fm = p.din("x_fm", [C, TC + 1])
    qt = p.din("qt", [C, TC], BF16)
    kt = p.din("kt", [C, TC], BF16)
    vq = p.din("vq", [TC, C], BF16)
    xg = p.din("xg", [C, TC], BF16)
    b_rowT = p.din("b_rowT", [TC, 16])
    utotP = p.din("utotP", [128, 512])
    traj = p.din("traj", [128, NCH * 512])
    dpre = p.din("dpre", [128, NCH * 8])
    WgT = p.din("WgT", [C, C])
    WoT = p.din("WoT", [C, C])
    w_down = p.din("w_down", [C, FFI])
    upq = p.din("upq", [FFI, TC], I16)
    gq = p.din("gq", [FFI, TC], I16)
    pxug_row = p.din("pxug_row", [1, 2048])
    pxg2_row = p.din("pxg2_row", [1, 1024])
    pxo_row = p.din("pxo_row", [1, 1024])
    gam_row = p.din("gam_row", [1, C])
    bet_row = p.din("bet_row", [1, C])
    tri = p.din("tri", [128, 128])
    ident = p.din("ident", [128, 128])
    out_o = p.dout("out_fm", [C, TC])
    attn_scr = nc.dram_tensor("attn_scr", [C, TC], F32, kind="Internal").ap()
    wdqd_scr = nc.dram_tensor("wdqd_scr", [C, FFI], BF16, kind="Internal").ap()
    sgscr2 = nc.dram_tensor("sgscr2", [1, 8], F32, kind="Internal").ap()
    sgscr3 = nc.dram_tensor("sgscr3", [1, 4], F32, kind="Internal").ap()
    sgscr4 = nc.dram_tensor("sgscr4", [1, 4], F32, kind="Internal").ap()

    sgug = p.sb("sgug", [128, 8])
    sg3 = p.sb("sg3", [128, 4])
    sg4 = p.sb("sg4", [128, 4])
    sgb1 = (p.sb("sgb1a", [1, 2048]), p.sb("sgb1b", [1, 8]))
    sgb2 = (p.sb("sgb2a", [1, 1024]), p.sb("sgb2b", [1, 4]))
    sgb3 = (p.sb("sgb3a", [1, 1024]), p.sb("sgb3b", [1, 4]))
    rnd_col = p.constcol(RND)
    epsg_col = p.constcol(16e-5)

    for s in ("qS", "qA", "qP", "qP0", "qP1", "qG", "qG2", "qG3", "qO", "qO0", "qO1",
              "qL0", "qL1", "qC0", "qC1", "qW", "qD", "qT0", "qT1", "sV", "sA", "sP", "sT"):
        p.sem(s)

    from contextlib import ExitStack
    subA = ExitStack()

    def sba(name, shape, dtype=F32):
        return subA.enter_context(nc.sbuf_tensor(name, list(shape), dtype))

    wg = sba("wg", [128, 8, C], BF16)
    wo = sba("wo", [128, 8, C], BF16)
    u0s = sba("u0s", [128, 8, 64])
    gex = sba("gex", [128, C])
    bex = sba("bex", [128, C])
    trim = sba("trim", [128, 128])
    idn = sba("idn", [128, 128])
    stgw = sba("stgw", [128, 2, C])
    qtc = [sba(f"qtc{i}", [128, 8, 128], BF16) for i in range(4)]
    ktc = [sba(f"ktc{i}", [128, 8, 128], BF16) for i in range(4)]
    vqc = [sba(f"vqc{i}", [128, C], BF16) for i in range(4)]
    xgc = [sba(f"xgc{i}", [128, 8, 128], BF16) for i in range(4)]
    trajc = [sba(f"trajc{i}", [128, 512]) for i in range(4)]
    dpc = [sba(f"dpc{i}", [128, 8]) for i in range(4)]
    bTc = [sba(f"bTc{i}", [128, 16]) for i in range(4)]
    scg = [sba(f"scg{i}", [128, 8, 64], BF16) for i in range(4)]
    amb = [sba(f"amb{i}", [128, 16 * 128], BF16) for i in range(4)]
    ytc = [sba(f"ytc{i}", [128, C]) for i in range(4)]
    ysq = [sba(f"ysq{i}", [128, C]) for i in range(4)]
    gnt = [sba(f"gnt{i}", [128, 80]) for i in range(4)]
    gtc = [sba(f"gtc{i}", [128, C]) for i in range(4)]
    gth = [sba(f"gth{i}", [128, 512], BF16) for i in range(4)]
    gth1 = [sba(f"gth1{i}", [128, 512]) for i in range(4)]
    ygb = [sba(f"ygb{i}", [128, C], BF16) for i in range(4)]
    ygf = [sba(f"ygf{i}", [128, 8, 128], BF16) for i in range(4)]
    afm = [sba(f"afm{i}", [128, 128]) for i in range(2)]

    sub_ps = ExitStack()
    psA2 = [sub_ps.enter_context(nc.psum_tensor(f"psA{i}", [128, 512], F32)) for i in range(2)]
    psY2 = [sub_ps.enter_context(nc.psum_tensor(f"psY{i}", [128, 512], F32)) for i in range(2)]
    psWo = [sub_ps.enter_context(nc.psum_tensor(f"psWo{i}", [128, 512], F32)) for i in range(2)]
    psG = [sub_ps.enter_context(nc.psum_tensor(f"psG{i}", [128, 512], F32)) for i in range(2)]

    with nc.Block() as blk:
        @blk.vector
        def _(e):
            p.emit_const_memsets(e, "sV")
        qg1 = _sigma_cols(p, blk, pxug_row, 2, 1024, sgscr2, sgug, "qG", bufs=sgb1)
        qg2 = _sigma_cols(p, blk, pxg2_row, 1, 1024, sgscr3, sg3, "qG2", bufs=sgb2)
        qg3 = _sigma_cols(p, blk, pxo_row, 1, 1024, sgscr4, sg4, "qG3", bufs=sgb3)

        @blk.sync
        def _(e):
            p.inc(e.dma_start(out=u0s[:], in_=utotP.rearrange("p (g j) -> p g j", j=64)), "qS", 16)
            p.inc(e.dma_start(out=gex[:], in_=gam_row[0:1, :].broadcast_to([128, C])), "qS", 16)
            p.inc(e.dma_start(out=bex[:], in_=bet_row[0:1, :].broadcast_to([128, C])), "qS", 16)
            p.inc(e.dma_start(out=trim[:], in_=tri[:, :]), "qS", 16)
            p.inc(e.dma_start(out=idn[:], in_=ident[:, :]), "qS", 16)
        qS_c = p.cnt["qS"]

        for wsb, whbm in ((wg, WgT), (wo, WoT)):
            for t2 in range(0, 8, 2):
                @blk.scalar
                def _(e, wsb=wsb, whbm=whbm, t2=t2):
                    p.wait(e, "sA")
                    p.inc(e.dma_start(out=stgw[:], in_=fm3(whbm)[:, t2:t2 + 2, :]), "qP", 16)
                    e.wait_ge(p.sems["qP"], p.cnt["qP"])
                    p.inc(e.activation(out=wsb[:, t2:t2 + 2, :], in_=stgw[:], func=AF.Identity), "sA")
        qP_c = p.cnt["sA"]

        chunk_done = {}
        for _i in range(-4, 0):
            chunk_done[_i] = (0, 0, 0)
        amb_done = {}
        ybank_done = {-2: p.cnt["sV"], -1: p.cnt["sV"]}
        ymm_sT = {-4: 0, -3: 0, -2: 0, -1: 0}
        ygb_free = {-1: ("qT0", 0), -2: ("qT0", 0), -3: ("qT0", 0), -4: ("qT0", 0)}
        apply_done = {}
        womm_sT = {-4: 0, -3: 0, -2: 0, -1: 0}
        gproj_done = {-2: (0, 0), -1: (p.cnt["sA"], p.cnt["sV"])}
        afm_done = {-2: p.cnt["sA"], -1: p.cnt["sA"]}
        woship = {-2: ("qO", 0), -1: ("qO", 0)}
        s1info = {}

        def emit_s1(gc):
            bi = gc % 4
            qc = "qC%d" % (gc % 2)
            t0 = gc * L

            @blk.sync
            def _(e, gc=gc, bi=bi, qc=qc, t0=t0):
                e.wait_ge(p.sems["qS"], qS_c)
                if gc >= 4:
                    cd = chunk_done[gc - 4]
                    p.wait(e, "sV", cd[0])
                    p.wait(e, "sA", cd[1])
                    p.wait(e, "sT", cd[2])
                p.inc(e.dma_start(out=qtc[bi][:], in_=fm3(qt)[:, :, t0:t0 + L]), qc, 16)
                p.inc(e.dma_start(out=ktc[bi][:], in_=fm3(kt)[:, :, t0:t0 + L]), qc, 16)
                p.inc(e.dma_start(out=vqc[bi][:], in_=vq[t0:t0 + L, :]), qc, 16)
                p.inc(e.dma_start(out=xgc[bi][:], in_=fm3(xg)[:, :, t0:t0 + L]), qc, 16)
                p.inc(e.dma_start(out=trajc[bi][:], in_=traj[:, 512 * gc:512 * gc + 512]), qc, 16)
                p.inc(e.dma_start(out=dpc[bi][:], in_=dpre[:, 8 * gc:8 * gc + 8]), qc, 16)
                p.inc(e.dma_start(out=bTc[bi][:], in_=b_rowT[t0:t0 + L, :]), qc, 16)
            qld = p.cnt[qc]

            @blk.vector
            def _(e, gc=gc, bi=bi, qld=qld, qc=qc):
                e.wait_ge(p.sems[qc], qld)
                e.wait_ge(p.sems["qG"], qg1)
                e.wait_ge(p.sems["qG2"], qg2)
                e.wait_ge(p.sems["qG3"], qg3)
                p.wait(e, "sT", chunk_done[gc - 4][2] if gc >= 4 else 0)
                for g in range(8):
                    ii = e.scalar_tensor_tensor(out=scg[bi][:, g, :], in0=u0s[:, g, :],
                                                scalar=dpc[bi][:, g:g + 1], in1=trajc[bi][:, 64 * g:64 * g + 64],
                                                op0=ALU.mult, op1=ALU.add)
                p.inc(ii, "sV")

            for half in range(2):
                for par in range(2):
                    pb = psA2[par]
                    bp = 64 * par

                    @blk.tensor
                    def _(e, gc=gc, bi=bi, half=half, par=par, pb=pb, bp=bp, qld=qld, qc=qc):
                        e.wait_ge(p.sems[qc], qld)
                        pk = (gc, half - 1, par) if half > 0 else (gc - 1, 1, par)
                        if pk in amb_done:
                            p.wait(e, "sV", amb_done[pk])
                        for hh in range(4):
                            h = half * 8 + 2 * hh + par
                            ii = e.matmul(pb[:, 128 * hh:128 * hh + 128],
                                          ktc[bi][bp:bp + 64, h // 2, :],
                                          qtc[bi][bp:bp + 64, h // 2, :], start=True, stop=True)
                        p.inc(ii, "sT")
                    sT_a = p.cnt["sT"]

                    @blk.vector
                    def _(e, gc=gc, half=half, par=par, pb=pb, sT_a=sT_a):
                        p.wait(e, "sT", sT_a)
                        if half == 0:
                            p.wait(e, "sT", ymm_sT.get(gc - 4, 0))
                        oview = amb[gc % 4].rearrange("p (u v) -> p u v", v=128)[:, 8 * half + par:8 * half + 8:2, :]
                        p.inc(e.tensor_tensor(
                            out=oview,
                            in0=pb[:, :].rearrange("p (u v) -> p u v", v=128),
                            in1=trim.rearrange("p (o v) -> p o v", o=1).broadcast_to([128, 4, 128]),
                            op=ALU.mult), "sV")
                    amb_done[(gc, half, par)] = p.cnt["sV"]
            sV_am = p.cnt["sV"]

            for par in range(2):
                pb = psY2[par]
                bp = 64 * par

                @blk.tensor
                def _(e, gc=gc, bi=bi, par=par, pb=pb, bp=bp, sV_am=sV_am):
                    p.wait(e, "sV", sV_am)
                    p.wait(e, "sV", ybank_done[gc - 1])
                    for hh in range(8):
                        h = 2 * hh + par
                        e.matmul(pb[:, 64 * hh:64 * hh + 64], amb[bi][:, 128 * h:128 * h + 128],
                                 vqc[bi][:, 64 * h:64 * h + 64], start=True, stop=False)
                        ii = e.matmul(pb[:, 64 * hh:64 * hh + 64],
                                      qtc[bi][bp:bp + 64, h // 2, :],
                                      scg[bi][bp:bp + 64, h // 2, :], start=False, stop=True)
                    p.inc(ii, "sT")
            sT_y = p.cnt["sT"]
            ymm_sT[gc] = sT_y

            @blk.vector
            def _(e, gc=gc, bi=bi, sT_y=sT_y):
                p.wait(e, "sT", sT_y)
                p.wait(e, "sA")
                for h in range(16):
                    ii = e.scalar_tensor_tensor(out=ytc[bi][:, 64 * h:64 * h + 64],
                                                in0=vqc[bi][:, 64 * h:64 * h + 64],
                                                scalar=bTc[bi][:, h:h + 1],
                                                in1=psY2[h % 2][:, 64 * (h // 2):64 * (h // 2) + 64],
                                                op0=ALU.mult, op1=ALU.add)
                p.inc(ii, "sV")
            sV_y = p.cnt["sV"]
            ybank_done[gc] = sV_y

            @blk.vector
            def _(e, bi=bi, sV_y=sV_y):
                p.wait(e, "sV", sV_y)
                p.inc(e.tensor_tensor(out=ysq[bi][:], in0=ytc[bi][:], in1=ytc[bi][:], op=ALU.mult), "sV")
                p.wait(e, "sV")
                e.tensor_reduce(out=gnt[bi][:, 0:16], in_=ysq[bi].rearrange("p (h j) -> p h j", j=64),
                                axis=AX.X, op=ALU.add)
                p.inc(e.tensor_reduce(out=gnt[bi][:, 16:32], in_=ytc[bi].rearrange("p (h j) -> p h j", j=64),
                                      axis=AX.X, op=ALU.add), "sV")
                p.wait(e, "sV")
                e.tensor_scalar(out=gnt[bi][:, 32:48], in0=gnt[bi][:, 16:32],
                                scalar1=1.0 / 64.0, scalar2=None, op0=ALU.mult)
                p.inc(e.tensor_scalar(out=gnt[bi][:, 48:64], in0=gnt[bi][:, 0:16],
                                      scalar1=1.0 / 64.0, scalar2=None, op0=ALU.mult), "sV")
                p.wait(e, "sV")
                p.inc(e.tensor_tensor(out=gnt[bi][:, 0:16], in0=gnt[bi][:, 32:48], in1=gnt[bi][:, 32:48],
                                      op=ALU.mult), "sV")
                p.wait(e, "sV")
                p.inc(e.tensor_tensor(out=gnt[bi][:, 48:64], in0=gnt[bi][:, 48:64], in1=gnt[bi][:, 0:16],
                                      op=ALU.subtract), "sV")
            sV_gn = p.cnt["sV"]

            @blk.scalar
            def _(e, bi=bi, gc=gc, sV_gn=sV_gn):
                p.wait(e, "sV", sV_gn)
                p.wait(e, "sV", apply_done.get(gc - 4, 0))
                p.inc(e.activation(out=gnt[bi][:, 0:16], in_=gnt[bi][:, 48:64], func=AF.Sqrt,
                                   bias=epsg_col, scale=1.0), "sA")
            sA_sq = p.cnt["sA"]

            @blk.vector
            def _(e, bi=bi, sA_sq=sA_sq):
                p.wait(e, "sA", sA_sq)
                p.inc(e.reciprocal(out=gnt[bi][:, 64:80], in_=gnt[bi][:, 0:16]), "sV")
            sV_rs = p.cnt["sV"]
            s1info[gc] = (bi, qc, qld, sV_rs)

        def emit_s2(gc):
            bi, qc, qld, sV_rs = s1info[gc]
            t0 = gc * L

            for hh in range(2):
                pb = psG[hh]
                gi2 = (gc % 2) * 2 + hh

                @blk.tensor
                def _(e, gc=gc, bi=bi, hh=hh, pb=pb, qc=qc, qld=qld):
                    e.wait_ge(p.sems[qc], qld)
                    p.wait(e, "sA", qP_c)
                    p.wait(e, "sV", gproj_done[gc - 1][1])
                    for kc in range(8):
                        ii = e.matmul(pb[:, 0:512], xgc[bi][:, kc, :],
                                      wg[:, kc, 512 * hh:512 * hh + 512],
                                      start=(kc == 0), stop=(kc == 7))
                    p.inc(ii, "sT")
                sT_g = p.cnt["sT"]

                @blk.scalar
                def _(e, gc=gc, hh=hh, pb=pb, gi2=gi2, sT_g=sT_g):
                    p.wait(e, "sT", sT_g)
                    p.wait(e, "sV", gproj_done[gc - 2][1])
                    p.inc(e.activation(out=gth[gi2][:], in_=pb[:, 0:512],
                                       func=AF.Tanh, scale=0.5), "sA")
                sA_gth = p.cnt["sA"]

                @blk.vector
                def _(e, gc=gc, bi=bi, hh=hh, pb=pb, gi2=gi2, sA_gth=sA_gth):
                    p.wait(e, "sA", sA_gth)
                    p.inc(e.tensor_scalar(out=gth1[gi2][:], in0=gth[gi2][:], scalar1=1.0,
                                          scalar2=None, op0=ALU.add), "sV")
                    p.wait(e, "sV")
                    p.inc(e.scalar_tensor_tensor(out=gtc[bi][:, 512 * hh:512 * hh + 512],
                                                 in0=pb[:, 0:512], scalar=0.5, in1=gth1[gi2][:],
                                                 op0=ALU.mult, op1=ALU.mult), "sV")
            sA_g = p.cnt["sA"]
            sV_gd = p.cnt["sV"]
            gproj_done[gc] = (sA_g, sV_gd)

            @blk.vector
            def _(e, gc=gc, bi=bi, sV_rs=sV_rs, sA_g=sA_g):
                p.wait(e, "sV", sV_rs)
                p.wait(e, "sA", sA_g)
                e.wait_ge(p.sems[ygb_free[gc - 4][0]], ygb_free[gc - 4][1])
                p.inc(e.tensor_tensor(out=ysq[bi].rearrange("p (h j) -> p h j", j=64),
                                      in0=ytc[bi].rearrange("p (h j) -> p h j", j=64),
                                      in1=gnt[bi][:, 32:48].rearrange("p (h o) -> p h o", o=1).broadcast_to([128, 16, 64]),
                                      op=ALU.subtract), "sV")
                p.wait(e, "sV")
                p.inc(e.tensor_tensor(out=ysq[bi].rearrange("p (h j) -> p h j", j=64),
                                      in0=ysq[bi].rearrange("p (h j) -> p h j", j=64),
                                      in1=gnt[bi][:, 64:80].rearrange("p (h o) -> p h o", o=1).broadcast_to([128, 16, 64]),
                                      op=ALU.mult), "sV")
                p.wait(e, "sV")
                p.inc(e.tensor_tensor(out=ysq[bi][:], in0=ysq[bi][:], in1=gex[:], op=ALU.mult), "sV")
                p.wait(e, "sV")
                p.inc(e.tensor_tensor(out=ysq[bi][:], in0=ysq[bi][:], in1=bex[:], op=ALU.add), "sV")
                p.wait(e, "sV")
                p.inc(e.tensor_tensor(out=ygb[bi][:], in0=ysq[bi][:], in1=gtc[bi][:], op=ALU.mult), "sV")
            sV_yg = p.cnt["sV"]
            apply_done[gc] = sV_yg

            qtp = "qT%d" % (gc % 2)

            @blk.scalar
            def _(e, gc=gc, bi=bi, sV_yg=sV_yg, qtp=qtp):
                p.wait(e, "sV", sV_yg)
                p.wait(e, "sT", womm_sT.get(gc - 4, 0))
                e.wait_ge(p.sems[qtp], p.cnt[qtp])
                for cb in range(8):
                    p.inc(e.dma_start(out=ygf[bi][:, cb, :], in_=ygb[bi][:, 128 * cb:128 * cb + 128],
                                      transpose=True), qtp, 16)
            qT_y = p.cnt[qtp]
            ygb_free[gc] = (qtp, qT_y)

            for ct in range(8):
                wid = gc * 8 + ct
                pb = psWo[ct % 2]
                qos = "qO%d" % (ct % 2)

                @blk.tensor
                def _(e, gc=gc, ct=ct, pb=pb, qT_y=qT_y, qtp=qtp, wid=wid):
                    e.wait_ge(p.sems[qtp], qT_y)
                    p.wait(e, "sA", afm_done[wid - 2])
                    for kc in range(8):
                        ii = e.matmul(pb[:, 0:128], wo[:, kc, 128 * ct:128 * ct + 128],
                                      ygf[gc % 4][:, kc, :], start=(kc == 0), stop=(kc == 7))
                    p.inc(ii, "sT")
                sT_w = p.cnt["sT"]

                @blk.scalar
                def _(e, gc=gc, ct=ct, pb=pb, sT_w=sT_w, wid=wid, qos=qos):
                    p.wait(e, "sT", sT_w)
                    e.wait_ge(p.sems[woship[wid - 2][0]], woship[wid - 2][1])
                    p.inc(e.activation(out=afm[ct % 2][:], in_=pb[:, 0:128], func=AF.Identity), "sA")
                sA_w = p.cnt["sA"]
                afm_done[wid] = sA_w

                @blk.sync
                def _(e, gc=gc, ct=ct, t0=t0, sA_w=sA_w, qos=qos):
                    p.wait(e, "sA", sA_w)
                    p.inc(e.dma_start(out=attn_scr[128 * ct:128 * (ct + 1), t0:t0 + L],
                                      in_=afm[ct % 2][:]), qos, 16)
                woship[wid] = (qos, p.cnt[qos])
            womm_sT[gc] = p.cnt["sT"]
            chunk_done[gc] = (p.cnt["sV"], p.cnt["sA"], p.cnt["sT"])

        for it in range(NCH + 1):
            if it < NCH:
                emit_s1(it)
            if it >= 1:
                emit_s2(it - 1)

        sub_ps.close()
        subA.close()

        subB = ExitStack()

        def sbb(name, shape, dtype=F32):
            return subB.enter_context(nc.sbuf_tensor(name, list(shape), dtype))

        wtile = [sbb(f"dwt{i}", [128, FFI]) for i in range(2)]
        wq2 = [sbb(f"dwq{i}", [128, FFI], BF16) for i in range(2)]
        babs = [sbb(f"dbabs{i}", [128, FFI // BLK]) for i in range(2)]
        sblk = [sbb(f"dsblk{i}", [128, FFI // BLK]) for i in range(2)]
        rblk = [sbb(f"drblk{i}", [128, FFI // BLK]) for i in range(2)]
        wdT = sbb("wdT", [128, 24, C], BF16)
        ui = [sbb(f"ui{i}", [128, 512], I16) for i in range(2)]
        gi = [sbb(f"gi{i}", [128, 512], I16) for i in range(2)]
        gqf = [sbb(f"gqf{i}", [128, 512]) for i in range(2)]
        sigt = [sbb(f"sigt{i}", [128, 512]) for i in range(2)]
        sigq = sbb("sigq", [128, 512])
        g2t = sbb("g2t", [128, 512])
        g2q = sbb("g2q", [128, 512])
        otl = sbb("otl", [128, 512])
        oqb = sbb("oqb", [128, 512], BF16)
        xres = sbb("xres", [128, 8, 512])
        ares = sbb("ares", [128, 8, 512])
        outt = sbb("outt", [128, 8, 512])

        sub_ps2 = ExitStack()
        psD = [sub_ps2.enter_context(nc.psum_tensor(f"psDn{i}", [128, 512], F32)) for i in range(8)]

        qO_attn = p.cnt["qO"]
        for dev in (blk.sync, blk.scalar, blk.vector, blk.gpsimd, blk.tensor):
            @dev
            def _(e):
                e.wait_ge(p.sems["qO"], qO_attn)
                for qq in ("qC0", "qC1", "qS", "qP", "qO0", "qO1", "qT0", "qT1"):
                    e.wait_ge(p.sems[qq], p.cnt[qq])

        emit_dequant(p, blk, w_down, wdqd_scr, C, (wtile, wq2, babs, sblk, rblk), rnd_col, "qD")

        @blk.scalar
        def _(e):
            for qq in ("qP", "qP0", "qP1"):
                e.wait_ge(p.sems[qq], p.cnt[qq])
            for it in range(24):
                p.inc(e.dma_start(out=wdT[:, it, :], in_=wdqd_scr[0:C, 128 * it:128 * (it + 1)],
                                  transpose=True), "qW", 16)
        qW_d = p.cnt["qW"]

        chain_mark = {}

        def dn_s1(tq, it):
            bi = it % 2
            ql = "qL%d" % bi
            tt0 = tq * 512

            @blk.sync
            def _(e, tq=tq, it=it, bi=bi, ql=ql, tt0=tt0):
                if tq + it > 0:
                    p.wait(e, "sV")
                p.inc(e.dma_start(out=ui[bi][:], in_=upq[128 * it:128 * (it + 1), tt0:tt0 + 512]), ql, 16)
                p.inc(e.dma_start(out=gi[bi][:], in_=gq[128 * it:128 * (it + 1), tt0:tt0 + 512]), ql, 16)
            qld = p.cnt[ql]

            @blk.vector
            def _(e, bi=bi, qld=qld, ql=ql):
                e.wait_ge(p.sems[ql], qld)
                p.inc(e.tensor_scalar(out=gqf[bi][:], in0=gi[bi][:], scalar1=sgug[:, 5:6],
                                      scalar2=None, op0=ALU.mult), "sV")
            sV_g = p.cnt["sV"]

            @blk.scalar
            def _(e, bi=bi, it=it, sV_g=sV_g):
                p.wait(e, "sV", sV_g)
                p.wait(e, "sV", chain_mark.get(it - 2, 0))
                p.inc(e.activation(out=sigt[bi][:], in_=gqf[bi][:], func=AF.Sigmoid), "sA")
            return p.cnt["sA"]

        def dn_s2(tq, it, sA_s):
            bi = it % 2

            @blk.vector
            def _(e, bi=bi, sA_s=sA_s):
                p.wait(e, "sA", sA_s)
                p.inc(e.tensor_scalar(out=sigq[:], in0=sigt[bi][:], scalar1=65536.0, scalar2=RND,
                                      op0=ALU.mult, op1=ALU.add), "sV")
                p.wait(e, "sV")
                p.inc(e.tensor_scalar(out=sigq[:], in0=sigq[:], scalar1=RND + 65535.0,
                                      scalar2=RND, op0=ALU.min, op1=ALU.subtract), "sV")
                p.wait(e, "sV")
                p.inc(e.scalar_tensor_tensor(out=g2t[:], in0=sigq[:], scalar=1.0 / 65536.0,
                                             in1=gqf[bi][:], op0=ALU.mult, op1=ALU.mult), "sV")
                p.wait(e, "sV")
                p.inc(e.tensor_scalar(out=g2q[:], in0=g2t[:], scalar1=sg3[:, 2:3],
                                      scalar2=RND, op0=ALU.mult, op1=ALU.add), "sV")
                p.wait(e, "sV")
                p.inc(e.tensor_scalar(out=g2q[:], in0=g2q[:], scalar1=RND,
                                      scalar2=sg3[:, 1:2], op0=ALU.subtract, op1=ALU.mult), "sV")
                p.wait(e, "sV")
                p.inc(e.scalar_tensor_tensor(out=otl[:], in0=ui[bi][:], scalar=sgug[:, 1:2],
                                             in1=g2q[:], op0=ALU.mult, op1=ALU.mult), "sV")
                p.wait(e, "sV")
                p.inc(e.tensor_scalar(out=g2q[:], in0=otl[:], scalar1=sg4[:, 2:3],
                                      scalar2=RND, op0=ALU.mult, op1=ALU.add), "sV")
                p.wait(e, "sV")
                p.wait(e, "sT")
                p.inc(e.tensor_scalar(out=oqb[:], in0=g2q[:], scalar1=RND,
                                      scalar2=sg4[:, 1:2], op0=ALU.subtract, op1=ALU.mult), "sV")
            sV_oq = p.cnt["sV"]
            chain_mark[it if tq == 0 else it + 24 * tq] = sV_oq
            chain_mark[it - 2] = chain_mark.get(it - 2, 0)

            @blk.tensor
            def _(e, tq=tq, it=it, sV_oq=sV_oq):
                e.wait_ge(p.sems["qW"], qW_d)
                p.wait(e, "sV", sV_oq)
                if it == 0:
                    p.wait(e, "sA")
                for ct in range(8):
                    ii = e.matmul(psD[ct][:, :], wdT[:, it, 128 * ct:128 * ct + 128],
                                  oqb[:, :], start=(it == 0), stop=(it == 23))
                p.inc(ii, "sT")

        def dn_tail(tq):
            tt0 = tq * 512
            sT_dn = p.cnt["sT"]

            @blk.sync
            def _(e, tq=tq, tt0=tt0):
                for qq in ("qO", "qO0", "qO1"):
                    e.wait_ge(p.sems[qq], p.cnt[qq])
                p.inc(e.dma_start(out=xres[:], in_=fm3(x_fm)[:, :, tt0 + 1:tt0 + 513]), "qS", 16)
                p.inc(e.dma_start(out=ares[:], in_=fm3(attn_scr)[:, :, tt0:tt0 + 512]), "qS", 16)
            qS_r = p.cnt["qS"]

            @blk.vector
            def _(e, tq=tq, qS_r=qS_r, sT_dn=sT_dn):
                e.wait_ge(p.sems["qS"], qS_r)
                p.wait(e, "sT", sT_dn)
                p.inc(e.tensor_tensor(out=outt[:], in0=xres[:], in1=ares[:], op=ALU.add), "sV")
                p.wait(e, "sV")
                for ct in range(8):
                    ii = e.tensor_tensor(out=outt[:, ct, :], in0=outt[:, ct, :],
                                         in1=psD[ct][:, :], op=ALU.add)
                p.inc(ii, "sV")
            sV_out = p.cnt["sV"]

            @blk.scalar
            def _(e, tq=tq, sV_out=sV_out):
                p.wait(e, "sV", sV_out)
                p.inc(e.nop(), "sA")

            @blk.sync
            def _(e, tq=tq, tt0=tt0, sV_out=sV_out):
                p.wait(e, "sV", sV_out)
                e.wait_ge(p.sems["qO"], p.cnt["qO"])
                p.inc(e.dma_start(out=fm3(out_o)[:, :, tt0:tt0 + 512], in_=outt[:]), "qO", 16)
            return sV_out, qS_r

        items = [(tq, it) for tq in range(NQ) for it in range(24)]
        sA_marks = {}
        tails = []
        for k in range(len(items) + 1):
            if k < len(items):
                sA_marks[k] = dn_s1(*items[k])
            if k >= 1:
                tq, it = items[k - 1]
                dn_s2(tq, it, sA_marks[k - 1])
                if it == 23:
                    tails.append(dn_tail(tq))

        @blk.sync
        def _(e):
            for q in ("qS", "qG", "qG2", "qG3", "qO", "qO0", "qO1", "qP", "qP0", "qP1",
                      "qW", "qD", "qL0", "qL1", "qC0", "qC1", "qT0", "qT1"):
                e.wait_ge(p.sems[q], p.cnt[q])
        sub_ps2.close()
        subB.close()
    return p


from concourse.bass_utils import run_bass_kernel_spmd

TCK = 2048
TSL1 = 256
TSL2 = 512
_PROGS = {}


def _progs():
    if not _PROGS:
        _PROGS["L1"] = build_L1(TC=TCK, TSL=TSL1)
        _PROGS["L2"] = build_L2(TC=TCK, TSL=TSL2)
        _PROGS["L3"] = build_L3(TC=TCK, TSL=TSL2)
        _PROGS["L4"] = build_L4(TC=TCK)
        _PROGS["L5"] = build_L5(TC=TCK)
    return _PROGS


def kernel(L0_attn_x, L0_attn_miu_x, L0_attn_lambda_, L0_attn_A_lora, L0_attn_B_lora,
           L0_attn_td_miu, L0_attn_td_A, L0_attn_td_B, L0_attn_u, L0_attn_Wr,
           L0_attn_Wk, L0_attn_Wv, L0_attn_Wg, L0_attn_Wo, L0_attn_gn_gamma,
           L0_attn_gn_beta, L0_ffn_w_gate, L0_ffn_w_up, L0_ffn_w_down):
    x = np.asarray(L0_attn_x, np.float32)
    B, T, _ = x.shape
    NC = 8
    cores = list(range(NC))
    TTP = ceil_div(TCK + 1, 128) * 128

    lam = np.asarray(L0_attn_lambda_, np.float32).reshape(5, C)
    lam_col = np.zeros((128, 5 * 8), np.float32)
    for f in range(5):
        for t in range(8):
            lam_col[:, f * 8 + t] = lam[f][128 * t:128 * (t + 1)]
    miu_x = np.asarray(L0_attn_miu_x, np.float32).reshape(C)
    mx_col = np.ascontiguousarray(miu_x.reshape(8, 128).T)
    tdm = np.asarray(L0_attn_td_miu, np.float32).reshape(C)
    tdm_col = np.ascontiguousarray(tdm.reshape(8, 128).T)
    u = np.asarray(L0_attn_u, np.float32).reshape(16, 64)
    u16 = np.zeros((128, 8 * 16), np.float32)
    for h in range(16):
        u16[64 * (h % 2):64 * (h % 2) + 64, (h // 2) * 16 + h] = u[h]
    id128 = np.eye(128, dtype=np.float32)
    id2 = np.concatenate([np.eye(64), np.eye(64)], 0).astype(np.float32)
    tri = np.triu(np.ones((128, 128), np.float32), 1)
    WkT = np.ascontiguousarray(np.asarray(L0_attn_Wk, np.float32).T)
    WvT = np.ascontiguousarray(np.asarray(L0_attn_Wv, np.float32).T)
    WrT = np.ascontiguousarray(np.asarray(L0_attn_Wr, np.float32).T)
    WgT = np.ascontiguousarray(np.asarray(L0_attn_Wg, np.float32).T)
    WoT = np.ascontiguousarray(np.asarray(L0_attn_Wo, np.float32).T)
    A_lora = np.ascontiguousarray(np.asarray(L0_attn_A_lora, np.float32))
    B_lora = np.ascontiguousarray(np.asarray(L0_attn_B_lora, np.float32))
    td_A = np.ascontiguousarray(np.asarray(L0_attn_td_A, np.float32))
    td_B = np.ascontiguousarray(np.asarray(L0_attn_td_B, np.float32))
    w_up = np.ascontiguousarray(np.asarray(L0_ffn_w_up, np.float32))
    w_gate = np.ascontiguousarray(np.asarray(L0_ffn_w_gate, np.float32))
    w_down = np.ascontiguousarray(np.asarray(L0_ffn_w_down, np.float32))
    gam_row = np.ascontiguousarray(np.asarray(L0_attn_gn_gamma, np.float32).reshape(1, C))
    bet_row = np.ascontiguousarray(np.asarray(L0_attn_gn_beta, np.float32).reshape(1, C))

    xrows_l, xfm_l, xtm_l = [], [], []
    for c in cores:
        b, hf = c // 2, c % 2
        xr = np.zeros((TCK + 1, C), np.float32)
        xr[1:] = x[b, TCK * hf:TCK * (hf + 1)]
        if hf == 1:
            xr[0] = x[b, TCK - 1]
        xrows_l.append(xr)
        xfm_l.append(np.ascontiguousarray(xr.T))
        xtmp = np.zeros((TTP, C), np.float32)
        xtmp[:TCK + 1] = xr
        xtm_l.append(xtmp)

    pr = _progs()

    im1 = [{"x_tm": xtm_l[c], "x_fm": xfm_l[c], "WkT": WkT, "WvT": WvT, "WrT": WrT,
            "A_lora": A_lora, "B_lora": B_lora, "td_A": td_A, "td_B": td_B,
            "lam_col": lam_col, "mx_col": mx_col, "tdmiu_col": tdm_col,
            "u16": u16, "ident": id128, "ident2": id2} for c in cores]
    r1 = run_bass_kernel_spmd(pr["L1"].nc, im1, core_ids=cores).results

    pxn_row = np.concatenate([r1[c]["pxn"][:, 0] for c in cores]).reshape(1, 1024)
    pxn_row = np.ascontiguousarray(pxn_row)

    im2 = [{"x_fm": xfm_l[c], "murstd": r1[c]["murstd"], "pxn_row": pxn_row,
            "w_up": w_up, "w_gate": w_gate} for c in cores]
    r2 = run_bass_kernel_spmd(pr["L2"].nc, im2, core_ids=cores).results

    pxug = np.zeros((1, 2048), np.float32)
    pxug[0, 0:1024] = np.concatenate([r2[c]["pxu"][:, 0] for c in cores])
    pxug[0, 1024:2048] = np.concatenate([r2[c]["pxg"][:, 0] for c in cores])

    im3 = [{"x_fm": xfm_l[c], "murstd": r1[c]["murstd"], "pxn_row": pxn_row,
            "pxug_row": pxug, "wdq_up": r2[c]["wdq_up"], "wdq_gate": r2[c]["wdq_gate"]}
           for c in cores]
    r3 = run_bass_kernel_spmd(pr["L3"].nc, im3, core_ids=cores).results

    pxg2 = np.zeros((1, 1024), np.float32)
    pxg2[0, 0:1024] = np.concatenate([r3[c]["pxg2"][:, 0] for c in cores])

    im4 = [{"upq": r3[c]["upq"], "gq": r3[c]["gq"], "pxug_row": pxug,
            "pxg2_row": pxg2} for c in cores]
    r4 = run_bass_kernel_spmd(pr["L4"].nc, im4, core_ids=cores).results

    pxo = np.zeros((1, 1024), np.float32)
    pxo[0, 0:1024] = np.concatenate([r4[c]["pxo"][:, 0] for c in cores])

    zstate = np.zeros((128, 512), np.float32)
    im5 = []
    for c in cores:
        b, hf = c // 2, c % 2
        utotP = np.ascontiguousarray(r1[c - 1]["utot"]) if hf == 1 else zstate
        im5.append({"x_fm": xfm_l[c], "qt": r1[c]["qt"], "kt": r1[c]["kt"],
                    "vq": r1[c]["vq"], "xg": r1[c]["xg"],
                    "b_rowT": np.ascontiguousarray(np.asarray(r1[c]["b_row"]).T),
                    "utotP": utotP, "traj": r1[c]["traj"], "dpre": r1[c]["dpre"],
                    "WgT": WgT, "WoT": WoT, "w_down": w_down,
                    "upq": r3[c]["upq"], "gq": r3[c]["gq"], "pxug_row": pxug,
                    "pxg2_row": pxg2, "pxo_row": pxo, "gam_row": gam_row,
                    "bet_row": bet_row, "tri": tri, "ident": id128})
    r5 = run_bass_kernel_spmd(pr["L5"].nc, im5, core_ids=cores).results

    out = np.empty((B, T, C), np.float32)
    for c in cores:
        b, hf = c // 2, c % 2
        out[b, TCK * hf:TCK * (hf + 1)] = np.asarray(r5[c]["out_fm"], np.float32).T
    return out


# version 14
# speedup vs baseline: 1.0880x; 1.0449x over previous
import numpy as np
from contextlib import ExitStack
import concourse.bass as bass
import concourse.mybir as mybir

F32 = mybir.dt.float32
BF16 = mybir.dt.bfloat16
I16 = mybir.dt.int16
AF = mybir.ActivationFunctionType
ALU = mybir.AluOpType
AX = mybir.AxisListType

C = 1024
H = 16
HS = 64
L = 128
FFI = 3072
RND = 12582912.0
QMAX = 32767.0
WQMAX = 7.0
BLK = 32


def ceil_div(a, b):
    return (a + b - 1) // b


class Prog:
    def __init__(self, name="prog"):
        self.nc = bass.Bass("TRN2", target_bir_lowering=False)
        self.ctx = ExitStack()
        self.sems = {}
        self.cnt = {}

    def sb(self, name, shape, dtype=F32):
        return self.ctx.enter_context(self.nc.sbuf_tensor(name, list(shape), dtype))

    def ps(self, name, shape, dtype=F32):
        return self.ctx.enter_context(self.nc.psum_tensor(name, list(shape), dtype))

    def sem(self, name):
        if name not in self.sems:
            self.sems[name] = self.ctx.enter_context(self.nc.semaphore(name))
            self.cnt[name] = 0
        return self.sems[name]

    def inc(self, instr, name, n=1):
        instr.then_inc(self.sems[name], n)
        self.cnt[name] += n
        return self.cnt[name]

    def wait(self, eng, name, v=None):
        eng.wait_ge(self.sems[name], self.cnt[name] if v is None else v)

    def constcol(self, value):
        if not hasattr(self, "consts"):
            self.consts = {}
        if value not in self.consts:
            t = self.sb(f"cc_{len(self.consts)}", [128, 1], F32)
            self.consts[value] = t
        return self.consts[value][:, 0:1]

    def emit_const_memsets(self, eng, semname):
        if not hasattr(self, "consts"):
            self.consts = {}
        ii = None
        for v, t in self.consts.items():
            ii = eng.memset(t[:], float(v))
        if ii is not None:
            self.inc(ii, semname)

    def din(self, name, shape, dtype=F32):
        return self.nc.dram_tensor(name, list(shape), dtype, kind="ExternalInput").ap()

    def dout(self, name, shape, dtype=F32):
        return self.nc.dram_tensor(name, list(shape), dtype, kind="ExternalOutput").ap()


def fm3(ap, p=128):
    return ap.rearrange("(t p) x -> p t x", p=p)


def build_L1(TC=2048, TSL=512):
    NCH = TC // L
    NSL = TC // TSL
    CPS = TSL // L
    TTP = ceil_div(TC + 1, 128) * 128
    NTT = TTP // 128
    p = Prog("L1")
    nc = p.nc

    x_tm = p.din("x_tm", [TTP, C])
    x_fm = p.din("x_fm", [C, TC + 1])
    WkT = p.din("WkT", [C, C])
    WvT = p.din("WvT", [C, C])
    WrT = p.din("WrT", [C, C])
    A_lora = p.din("A_lora", [C, 160])
    B_lora = p.din("B_lora", [5, 32, C])
    td_A = p.din("td_A", [C, 64])
    td_B = p.din("td_B", [64, C])
    lam_col = p.din("lam_col", [128, 5 * 8])
    mx_col = p.din("mx_col", [128, 8])
    tdmiu_col = p.din("tdmiu_col", [128, 8])
    u16 = p.din("u16", [128, 8 * 16])
    ident = p.din("ident", [128, 128])
    ident2 = p.din("ident2", [128, 64])

    murstd = p.dout("murstd", [2, TTP])
    pxn_o = p.dout("pxn", [128, 1])
    qt_o = p.dout("qt", [C, TC], BF16)
    kt_o = p.dout("kt", [C, TC], BF16)
    vq_o = p.dout("vq", [TC, C], BF16)
    xg_o = p.dout("xg", [C, TC], BF16)
    b_o = p.dout("b_row", [16, TC])
    utot_o = p.dout("utot", [128, 512])
    traj_o = p.dout("traj", [128, NCH * 512])
    dpre_o = p.dout("dpre", [128, NCH * 8])

    wk = p.sb("wk", [128, 8, C], BF16)
    wv = p.sb("wv", [128, 8, C], BF16)
    wr = p.sb("wr", [128, 8, C], BF16)
    al = p.sb("al", [128, 8, 160], BF16)
    bl = p.sb("bl", [128, 5, C], BF16)
    tda = p.sb("tda", [128, 8, 64], BF16)
    tdb = p.sb("tdb", [64, C], BF16)
    lam = p.sb("lam", [128, 5, 8])
    mxc = p.sb("mxc", [128, 8])
    tdm = p.sb("tdm", [128, 8])
    u16s = p.sb("u16s", [128, 8, 16], BF16)
    idn = p.sb("idn", [128, 128])
    idnb = p.sb("idnb", [128, 64], BF16)

    xtm = [p.sb(f"xtm{i}", [128, C]) for i in range(2)]
    trash = p.sb("trash", [128, C], BF16)
    muA = p.sb("muA", [128, NTT])
    rsA = p.sb("rsA", [128, NTT])
    mrP = p.sb("mrP", [128, 2 * NTT])

    xfs = p.sb("xfs", [128, 8, TSL + 1])
    mue = p.sb("mue", [128, TSL + 1])
    rse = p.sb("rse", [128, TSL + 1])
    xns = p.sb("xns", [128, 8, TSL + 1])
    dxa = p.sb("dxa", [128, 8, TSL])
    xm = p.sb("xm", [128, 8, TSL], BF16)
    lhA = p.sb("lhA", [96, TSL], BF16)
    lh2 = p.sb("lh2", [64, TSL], BF16)
    tmpm = p.sb("tmpm", [128, 8, TSL])
    xxb = [p.sb(f"xx{i}", [128, 8, TSL], BF16) for i in range(2)]
    th = p.sb("th", [64, TSL], BF16)
    expw = p.sb("expw", [128, 8, TSL])
    spr = p.sb("spr", [128, 8, TSL])
    epos = p.sb("epos", [128, 8, TSL])
    dtot = p.sb("dtot", [128, 8, CPS])
    rf = p.sb("rf", [128, 8, TSL])
    ktb = p.sb("ktb", [128, 8, TSL], BF16)
    khb = p.sb("khb", [128, 8, TSL], BF16)
    qtb = p.sb("qtb", [128, 8, TSL], BF16)
    vtb = p.sb("vtb", [128, CPS, C], BF16)
    khT = p.sb("khT", [128, 8, 128], BF16)
    bsl = p.sb("bsl", [16, TSL])
    sown = [p.sb(f"sown{i}", [128, 8, 64]) for i in range(2)]
    dprer = [p.sb(f"dprer{i}", [128, 8]) for i in range(2)]
    pxn_t = p.sb("pxn_t", [128, NSL + 1])
    zer = p.sb("zer", [128, L])

    psA = p.ps("psA", [128, 512])
    psX = p.ps("psX", [64, 512])
    PSW = max(TSL, 512)
    psP = [p.ps(f"psP{i}", [128, PSW]) for i in range(2)]
    psU = p.ps("psU", [128, 512])

    eps_col = p.constcol(1e-5)

    for s in ("qS", "qA", "qP", "qO", "qB0", "qB1", "qT0", "qT1", "sV", "sA", "sP", "sT"):
        p.sem(s)

    with nc.Block() as blk:
        @blk.sync
        def _(e):
            p.inc(e.dma_start(out=lam[:], in_=lam_col.rearrange("p (f t) -> p f t", f=5)), "qS", 16)
            p.inc(e.dma_start(out=mxc[:], in_=mx_col[:, :]), "qS", 16)
            p.inc(e.dma_start(out=tdm[:], in_=tdmiu_col[:, :]), "qS", 16)
            p.inc(e.dma_start(out=idn[:], in_=ident[:, :]), "qS", 16)
        qS_const = p.cnt["qS"]

        @blk.gpsimd
        def _(e):
            p.inc(e.dma_start(out=wk[:], in_=fm3(WkT)), "qP", 16)
            p.inc(e.dma_start(out=wv[:], in_=fm3(WvT)), "qP", 16)
            p.inc(e.dma_start(out=wr[:], in_=fm3(WrT)), "qP", 16)
            p.inc(e.dma_start(out=al[:], in_=fm3(A_lora)), "qP", 16)
            p.inc(e.dma_start(out=bl[0:32], in_=B_lora.rearrange("f i c -> i f c")), "qP", 16)
            p.inc(e.dma_start(out=bl[32:64], in_=B_lora.rearrange("f i c -> i f c")), "qP", 16)
            p.inc(e.dma_start(out=bl[64:96], in_=B_lora.rearrange("f i c -> i f c")), "qP", 16)
            p.inc(e.dma_start(out=tda[:], in_=fm3(td_A)), "qP", 16)
            p.inc(e.dma_start(out=tdb[:], in_=td_B[:, :]), "qP", 16)
            p.inc(e.dma_start(out=u16s[:], in_=u16.rearrange("p (t h) -> p t h", t=8)), "qP", 16)
            p.inc(e.dma_start(out=idnb[:], in_=ident2[:, :]), "qP", 16)
        qP_const = p.cnt["qP"]

        @blk.vector
        def _(e):
            e.memset(zer[:], 0.0)
            e.memset(sown[0][:], 0.0)
            e.memset(dprer[0][:], 1.0)
            p.emit_const_memsets(e, "sV")
            p.inc(e.memset(pxn_t[:], 0.0), "sV")
        sV_init = p.cnt["sV"]

        sV_b0 = p.cnt["sV"]
        sA_b0 = p.cnt["sA"]

        @blk.scalar
        def _(e):
            for j in range(NTT):
                if j >= 2:
                    p.wait(e, "sV", sV_b0 + (j - 1))
                    p.wait(e, "sA", sA_b0 + (j - 1))
                p.inc(e.dma_start(out=xtm[j % 2][:], in_=x_tm[128 * j:128 * (j + 1), :]), "qA", 16)
                e.wait_ge(p.sems["qA"], 16 * (j + 1))
                p.inc(e.activation(out=trash[:], in_=xtm[j % 2][:], func=AF.Square,
                                   accum_out=rsA[:, j:j + 1]), "sA")
        sA_sq = p.cnt["sA"]

        @blk.vector
        def _(e):
            for j in range(NTT):
                e.wait_ge(p.sems["qA"], 16 * (j + 1))
                p.inc(e.tensor_reduce(out=muA[:, j:j + 1], in_=xtm[j % 2][:], axis=AX.X,
                                      op=ALU.add), "sV")
        sV_sums = p.cnt["sV"]

        sc1 = p.sb("sc1", [128, NTT])
        sc2 = p.sb("sc2", [128, NTT])

        @blk.vector
        def _(e):
            p.wait(e, "sA", sA_sq)
            p.wait(e, "sV", sV_sums)
            e.tensor_scalar(out=mrP[:, 0:NTT], in0=muA[:], scalar1=1.0 / 1024.0, scalar2=None,
                            op0=ALU.mult)
            p.inc(e.tensor_scalar(out=sc1[:], in0=rsA[:], scalar1=1.0 / 1024.0, scalar2=None,
                                  op0=ALU.mult), "sV")
            p.wait(e, "sV")
            p.inc(e.tensor_tensor(out=sc2[:], in0=mrP[:, 0:NTT], in1=mrP[:, 0:NTT], op=ALU.mult), "sV")
            p.wait(e, "sV")
            p.inc(e.tensor_tensor(out=rsA[:], in0=sc1[:], in1=sc2[:], op=ALU.subtract), "sV")
        sV_var = p.cnt["sV"]

        @blk.scalar
        def _(e):
            p.wait(e, "sV", sV_var)
            p.inc(e.activation(out=sc1[:], in_=rsA[:], func=AF.Sqrt, bias=eps_col, scale=1.0), "sA")
        sA_sqrt = p.cnt["sA"]

        @blk.vector
        def _(e):
            p.wait(e, "sA", sA_sqrt)
            p.inc(e.reciprocal(out=mrP[:, NTT:2 * NTT], in_=sc1[:]), "sV")
        sV_mr = p.cnt["sV"]

        @blk.sync
        def _(e):
            p.wait(e, "sV", sV_mr)
            e.wait_ge(p.sems["qS"], qS_const)
            for j in range(NTT):
                p.inc(e.dma_start(out=murstd[0:1, 128 * j:128 * (j + 1)], in_=mrP[:, j:j + 1]), "qS", 16)
                p.inc(e.dma_start(out=murstd[1:2, 128 * j:128 * (j + 1)], in_=mrP[:, NTT + j:NTT + j + 1]), "qS", 16)
        qS_mr = p.cnt["qS"]

        for s in range(NSL):
            c0 = s * TSL
            sl1 = slice(c0, c0 + TSL + 1)
            slt = slice(c0, c0 + TSL)

            @blk.sync
            def _(e):
                if s > 0:
                    p.wait(e, "sV")
                    p.wait(e, "sA")
                    p.wait(e, "sT")
                    p.wait(e, "sP")
                e.wait_ge(p.sems["qS"], qS_mr)
                p.inc(e.dma_start(out=xfs[:], in_=fm3(x_fm)[:, :, sl1]), "qS", 16)
                p.inc(e.dma_start(out=mue[:], in_=murstd[0:1, c0:c0 + TSL + 1].broadcast_to([128, TSL + 1])), "qS", 16)
                p.inc(e.dma_start(out=rse[:], in_=murstd[1:2, c0:c0 + TSL + 1].broadcast_to([128, TSL + 1])), "qS", 16)
            qS_ld = p.cnt["qS"]

            @blk.vector
            def _(e):
                e.wait_ge(p.sems["qS"], qS_ld)
                p.inc(e.tensor_tensor(out=xns[:], in0=xfs[:],
                                in1=mue.rearrange("p (o c) -> p o c", o=1).broadcast_to([128, 8, TSL + 1]),
                                op=ALU.subtract), "sV")
                p.wait(e, "sV")
                p.inc(e.tensor_tensor(out=xns[:], in0=xns[:],
                                in1=rse.rearrange("p (o c) -> p o c", o=1).broadcast_to([128, 8, TSL + 1]),
                                op=ALU.mult), "sV")
                p.wait(e, "sV")
                e.tensor_reduce(out=pxn_t[:, s:s + 1], in_=xns[:, :, 1:TSL + 1], axis=AX.XY,
                                op=ALU.max, apply_absolute_value=True)
                p.inc(e.tensor_tensor(out=dxa[:], in0=xns[:, :, 0:TSL], in1=xns[:, :, 1:TSL + 1],
                                op=ALU.subtract), "sV")
                p.wait(e, "sV")
                for t in range(8):
                    ii = e.scalar_tensor_tensor(out=xm[:, t, :], in0=dxa[:, t, :],
                                                scalar=mxc[:, t:t + 1], in1=xns[:, t, 1:TSL + 1],
                                                op0=ALU.mult, op1=ALU.add)
                p.inc(ii, "sV")
            sV_xm = p.cnt["sV"]

            @blk.tensor
            def _(e):
                p.wait(e, "sV", sV_xm)
                p.wait(e, "sV")
                e.wait_ge(p.sems["qP"], qP_const)
                for kc in range(8):
                    e.matmul(psA[0:96, 0:TSL], al[:, kc, 0:96], xm[:, kc, :],
                             start=(kc == 0), stop=(kc == 7))
                for kc in range(8):
                    ii = e.matmul(psX[0:64, 0:TSL], al[:, kc, 96:160], xm[:, kc, :],
                                  start=(kc == 0), stop=(kc == 7))
                p.inc(ii, "sT")
            sT_lora = p.cnt["sT"]

            @blk.scalar
            def _(e):
                p.wait(e, "sT", sT_lora)
                e.activation(out=lhA[0:96, :], in_=psA[0:96, 0:TSL], func=AF.Tanh)
                p.inc(e.activation(out=lh2[:], in_=psX[0:64, 0:TSL], func=AF.Tanh), "sA")
            sA_lh = p.cnt["sA"]

            def emit_xx(fi, slot, eng_name, prev_tmpm=None):
                ev = {-2: p.cnt["sV"], -1: p.cnt["sV"]}
                if fi < 3:
                    rhs_ap = lhA[32 * fi:32 * fi + 32, :]
                    lbase = 32 * fi
                else:
                    rhs_ap = lh2[32 * (fi - 3):32 * (fi - 3) + 32, :]
                    lbase = 32 * (fi - 3)
                for tt in range(8):
                    pp = psP[tt % 2]

                    @blk.tensor
                    def _(e, fi=fi, tt=tt, pp=pp, rhs_ap=rhs_ap, lbase=lbase):
                        p.wait(e, "sA", sA_lh)
                        p.wait(e, "sV", ev[tt - 2])
                        if tt < 2:
                            p.wait(e, "sV")
                            p.wait(e, "sA")
                        p.inc(e.matmul(pp[:, 0:TSL], bl[lbase:lbase + 32, fi, 128 * tt:128 * tt + 128],
                                       rhs_ap, start=True, stop=True), "sT")

                    @blk.vector
                    def _(e, fi=fi, tt=tt, pp=pp, prev_tmpm=prev_tmpm):
                        p.wait(e, "sT")
                        if tt == 0 and prev_tmpm is not None:
                            p.wait(e, prev_tmpm[0], prev_tmpm[1])
                        p.inc(e.tensor_scalar(out=tmpm[:, tt, :], in0=pp[:, 0:TSL],
                                              scalar1=lam[:, fi, tt:tt + 1], scalar2=None,
                                              op0=ALU.add), "sV")
                    ev[tt] = p.cnt["sV"]
                dev = blk.vector if eng_name == "sV" else blk.gpsimd

                mid = {}

                @dev
                def _(e, slot=slot, eng_name=eng_name, ev7=ev[7]):
                    p.wait(e, "sV", ev7)
                    p.wait(e, "sT")
                    e.wait_ge(p.sems["qO"], p.cnt["qO"])
                    p.inc(e.tensor_tensor(out=tmpm[:], in0=tmpm[:], in1=dxa[:], op=ALU.mult), eng_name)
                    mid["v"] = p.cnt[eng_name]
                    p.wait(e, eng_name)
                    p.inc(e.tensor_tensor(out=xxb[slot][:], in0=tmpm[:], in1=xns[:, :, 1:TSL + 1],
                                          op=ALU.add), eng_name)
                return (eng_name, p.cnt[eng_name], mid["v"])

            dep_w = emit_xx(0, 0, "sV")

            @blk.tensor
            def _(e):
                p.wait(e, dep_w[0], dep_w[1])
                p.wait(e, "sA")
                for kc in range(8):
                    ii = e.matmul(psA[0:64, 0:TSL], tda[:, kc, :], xxb[0][:, kc, :],
                                  start=(kc == 0), stop=(kc == 7))
                p.inc(ii, "sT")

            @blk.scalar
            def _(e):
                p.wait(e, "sT")
                p.inc(e.activation(out=th[:], in_=psA[0:64, 0:TSL], func=AF.Tanh), "sA")

            evw = {-2: p.cnt["sA"], -1: p.cnt["sA"]}
            for tt in range(8):
                pp = psP[tt % 2]

                @blk.tensor
                def _(e, tt=tt, pp=pp):
                    p.wait(e, "sA", evw[tt - 2])
                    if tt == 0:
                        p.wait(e, "sA")
                        p.wait(e, "sV")
                    p.inc(e.matmul(pp[:, 0:TSL], tdb[:, 128 * tt:128 * tt + 128], th[:],
                                   start=True, stop=True), "sT")

                @blk.scalar
                def _(e, tt=tt, pp=pp):
                    p.wait(e, "sT")
                    p.inc(e.activation(out=expw[:, tt, :], in_=pp[:, 0:TSL], func=AF.Exp,
                                       bias=tdm[:, tt:tt + 1], scale=1.0), "sA")
                evw[tt] = p.cnt["sA"]
            sA_expw = p.cnt["sA"]

            @blk.vector
            def _(e):
                p.wait(e, "sA", sA_expw)
                for tt in range(8):
                    for cc in range(CPS):
                        ii = e.tensor_tensor_scan(out=spr[:, tt, L * cc:L * cc + L],
                                                  data0=expw[:, tt, L * cc:L * cc + L],
                                                  data1=zer[:, 0:L], initial=0.0,
                                                  op0=ALU.add, op1=ALU.add)
                p.inc(ii, "sV")
            sV_scan = p.cnt["sV"]

            @blk.scalar
            def _(e):
                p.wait(e, "sV", sV_scan)
                for tt in range(8):
                    e.activation(out=epos[:, tt, 1:TSL], in_=spr[:, tt, 0:TSL - 1], func=AF.Exp, scale=-1.0)
                    e.activation(out=dtot[:, tt, :], in_=spr[:, tt, L - 1:TSL:L], func=AF.Exp, scale=-1.0)
                    p.inc(e.activation(out=expw[:, tt, :], in_=spr[:, tt, :], func=AF.Exp, scale=1.0), "sA")
            sA_dec = p.cnt["sA"]
            eneg = expw

            @blk.vector
            def _(e):
                p.wait(e, "sA", sA_dec)
                for tt in range(8):
                    e.memset(epos[:, tt, 0:TSL:L], 1.0)
                p.inc(e.tensor_tensor(
                    out=spr.rearrange("p t (c l) -> p t c l", l=L),
                    in0=eneg.rearrange("p t (c l) -> p t c l", l=L),
                    in1=dtot.rearrange("p t (c o) -> p t c o", o=1).broadcast_to([128, 8, CPS, L]),
                    op=ALU.mult), "sV")
            sV_fk = p.cnt["sV"]
            fk = spr

            dep_g = emit_xx(4, 1, "sP", prev_tmpm=(dep_w[0], dep_w[1]))

            @blk.sync
            def _(e):
                p.wait(e, dep_g[0], dep_g[1])
                e.wait_ge(p.sems["qO"], p.cnt["qO"])
                p.inc(e.dma_start(out=fm3(xg_o)[:, :, slt], in_=xxb[1][:]), "qO", 16)

            dep_r = emit_xx(3, 0, "sP", prev_tmpm=(dep_g[0], dep_g[1]))
            evr = {-2: (p.cnt["sA"], p.cnt["sV"]), -1: (p.cnt["sA"], p.cnt["sV"])}
            for tt in range(8):
                pp = psP[tt % 2]

                @blk.tensor
                def _(e, tt=tt, pp=pp):
                    p.wait(e, dep_r[0], dep_r[1])
                    p.wait(e, "sA", evr[tt - 2][0])
                    p.wait(e, "sV", evr[tt - 2][1])
                    if tt == 0:
                        p.wait(e, "sA")
                        p.wait(e, "sV")
                    for kc in range(8):
                        ii = e.matmul(pp[:, 0:TSL], wr[:, kc, 128 * tt:128 * tt + 128],
                                      xxb[0][:, kc, :], start=(kc == 0), stop=(kc == 7))
                    p.inc(ii, "sT")

                @blk.scalar
                def _(e, tt=tt, pp=pp):
                    p.wait(e, "sT")
                    p.inc(e.activation(out=rf[:, tt, :], in_=pp[:, 0:TSL], func=AF.Identity), "sA")

                @blk.vector
                def _(e, tt=tt, pp=pp):
                    p.wait(e, "sT")
                    if tt == 0:
                        p.wait(e, "sV", sV_fk)
                    p.inc(e.scalar_tensor_tensor(out=qtb[:, tt, :], in0=pp[:, 0:TSL], scalar=1.0,
                                                 in1=epos[:, tt, :], op0=ALU.mult, op1=ALU.mult), "sV")
                evr[tt] = (p.cnt["sA"], p.cnt["sV"])
            sA_r = p.cnt["sA"]
            sV_q = p.cnt["sV"]

            dep_k = emit_xx(1, 1, "sP", prev_tmpm=(dep_r[0], dep_r[1]))
            rkb = xm
            evk = {-2: p.cnt["sV"], -1: p.cnt["sV"]}
            for tt in range(8):
                pp = psP[tt % 2]

                @blk.tensor
                def _(e, tt=tt, pp=pp):
                    p.wait(e, dep_k[0], dep_k[1])
                    p.wait(e, "sV", evk[tt - 2])
                    if tt == 0:
                        p.wait(e, "sA")
                        p.wait(e, "sV")
                    for kc in range(8):
                        ii = e.matmul(pp[:, 0:TSL], wk[:, kc, 128 * tt:128 * tt + 128],
                                      xxb[1][:, kc, :], start=(kc == 0), stop=(kc == 7))
                    p.inc(ii, "sT")

                @blk.vector
                def _(e, tt=tt, pp=pp):
                    p.wait(e, "sT")
                    if tt == 0:
                        p.wait(e, "sA", sA_r)
                    e.scalar_tensor_tensor(out=ktb[:, tt, :], in0=pp[:, 0:TSL], scalar=1.0,
                                           in1=eneg[:, tt, :], op0=ALU.mult, op1=ALU.mult)
                    e.scalar_tensor_tensor(out=khb[:, tt, :], in0=pp[:, 0:TSL], scalar=1.0,
                                           in1=fk[:, tt, :], op0=ALU.mult, op1=ALU.mult)
                    p.inc(e.scalar_tensor_tensor(out=rkb[:, tt, :], in0=pp[:, 0:TSL], scalar=1.0,
                                                 in1=rf[:, tt, :], op0=ALU.mult, op1=ALU.mult), "sV")
                evk[tt] = p.cnt["sV"]
            sV_k = p.cnt["sV"]

            @blk.sync
            def _(e):
                p.wait(e, "sV", sV_k)
                e.wait_ge(p.sems["qO"], p.cnt["qO"])
                p.inc(e.dma_start(out=fm3(qt_o)[:, :, slt], in_=qtb[:]), "qO", 16)
                p.inc(e.dma_start(out=fm3(kt_o)[:, :, slt], in_=ktb[:]), "qO", 16)

            dep_v = emit_xx(2, 0, "sP", prev_tmpm=(dep_k[0], dep_k[1]))
            for st in range(CPS):
                for hh in range(2):
                    pp = psP[(st * 2 + hh) % 2]

                    @blk.tensor
                    def _(e, st=st, hh=hh, pp=pp):
                        p.wait(e, dep_v[0], dep_v[1])
                        p.wait(e, "sA")
                        p.wait(e, "sV")
                        for kc in range(8):
                            ii = e.matmul(pp[:, 0:512], xxb[0][:, kc, L * st:L * st + L],
                                          wv[:, kc, 512 * hh:512 * hh + 512],
                                          start=(kc == 0), stop=(kc == 7))
                        p.inc(ii, "sT")

                    @blk.scalar
                    def _(e, st=st, hh=hh, pp=pp):
                        p.wait(e, "sT")
                        p.inc(e.activation(out=vtb[:, st, 512 * hh:512 * hh + 512],
                                           in_=pp[:, 0:512], func=AF.Identity), "sA")
            sA_v = p.cnt["sA"]

            @blk.sync
            def _(e):
                p.wait(e, "sA", sA_v)
                e.wait_ge(p.sems["qO"], p.cnt["qO"])
                for st in range(CPS):
                    p.inc(e.dma_start(out=vq_o[c0 + L * st:c0 + L * st + L, :], in_=vtb[:, st, :]), "qO", 16)

            @blk.tensor
            def _(e):
                p.wait(e, "sV", sV_k)
                for tt in range(8):
                    ii = e.matmul(psX[0:16, 0:TSL], u16s[:, tt, :], rkb[:, tt, :],
                                  start=(tt == 0), stop=(tt == 7))
                p.inc(ii, "sT")

            @blk.scalar
            def _(e):
                p.wait(e, "sT")
                p.inc(e.activation(out=bsl[:], in_=psX[0:16, 0:TSL], func=AF.Identity), "sA")
            sA_b = p.cnt["sA"]

            @blk.sync
            def _(e):
                p.wait(e, "sA", sA_b)
                e.wait_ge(p.sems["qO"], p.cnt["qO"])
                p.inc(e.dma_start(out=b_o[:, slt], in_=bsl[:]), "qO", 16)

            for cc in range(CPS):
                gc = s * CPS + cc
                cur = gc % 2
                nxt = (gc + 1) % 2

                qtp = "qT%d" % (cc % 2)

                @blk.scalar
                def _(e, cc=cc, qtp=qtp):
                    p.wait(e, "sV", sV_k)
                    p.wait(e, "sT")
                    e.wait_ge(p.sems[qtp], p.cnt[qtp])
                    for g in range(8):
                        p.inc(e.dma_start(out=khT[:, g, :],
                                          in_=khb[:, g, L * cc:L * cc + L],
                                          transpose=True), qtp, 16)
                qT_k = p.cnt[qtp]

                @blk.tensor
                def _(e, cc=cc, qT_k=qT_k, qtp=qtp):
                    e.wait_ge(p.sems[qtp], qT_k)
                    p.wait(e, "sV")
                    for h in range(16):
                        ii = e.matmul(psU[64 * (h % 2):64 * (h % 2) + 64, 64 * (h // 2):64 * (h // 2) + 64],
                                      khT[:, h // 2, 64 * (h % 2):64 * (h % 2) + 64],
                                      vtb[:, cc, 64 * h:64 * h + 64], start=True, stop=True)
                    p.inc(ii, "sT")
                sT_U = p.cnt["sT"]

                @blk.sync
                def _(e, gc=gc, cur=cur):
                    p.wait(e, "sV")
                    e.wait_ge(p.sems["qO"], p.cnt["qO"])
                    p.inc(e.dma_start(out=traj_o[:, 512 * gc:512 * gc + 512],
                                      in_=sown[cur].rearrange("p g j -> p (g j)")), "qO", 16)
                    p.inc(e.dma_start(out=dpre_o[:, 8 * gc:8 * gc + 8], in_=dprer[cur][:]), "qO", 16)
                qS_st = p.cnt["qO"]

                @blk.vector
                def _(e, cc=cc, cur=cur, nxt=nxt, sT_U=sT_U, qS_st=qS_st):
                    p.wait(e, "sT", sT_U)
                    e.wait_ge(p.sems["qO"], qS_st)
                    for g in range(8):
                        e.scalar_tensor_tensor(out=sown[nxt][:, g, :], in0=sown[cur][:, g, :],
                                               scalar=dtot[:, g, cc:cc + 1],
                                               in1=psU[:, 64 * g:64 * g + 64],
                                               op0=ALU.mult, op1=ALU.add)
                    p.inc(e.tensor_tensor(out=dprer[nxt][:], in0=dprer[cur][:],
                                          in1=dtot[:, :, cc], op=ALU.mult), "sV")

        fin = NCH % 2

        @blk.vector
        def _(e):
            p.inc(e.tensor_reduce(out=pxn_t[:, NSL:NSL + 1], in_=pxn_t[:, 0:NSL],
                                  axis=AX.X, op=ALU.max), "sV")
        sV_pxn = p.cnt["sV"]

        @blk.sync
        def _(e):
            p.wait(e, "sV", sV_pxn)
            p.inc(e.dma_start(out=pxn_o[:, :], in_=pxn_t[:, NSL:NSL + 1]), "qO", 16)
            p.inc(e.dma_start(out=utot_o[:, :], in_=sown[fin].rearrange("p g j -> p (g j)")), "qO", 16)
            for q in ("qS", "qA", "qP", "qO", "qB0", "qB1", "qT0", "qT1"):
                e.wait_ge(p.sems[q], p.cnt[q])

    return p


def _sigma_cols(p, blk, maxrow_ap, nmax, NV, scr_dram, out_sb, qname="qA", bufs=None):
    nc = p.nc
    if bufs is None:
        mrow = p.sb(f"sgm_{id(maxrow_ap) % 997}", [1, nmax * NV])
        mred = p.sb(f"sgr_{id(maxrow_ap) % 997}", [1, 4 * nmax])
    else:
        mrow, mred = bufs

    @blk.scalar
    def _(e):
        p.inc(e.dma_start(out=mrow[:], in_=maxrow_ap[:, :]), qname, 16)
    qv0 = p.cnt[qname]

    @blk.vector
    def _(e):
        e.wait_ge(p.sems[qname], qv0)
        for i in range(nmax):
            p.inc(e.tensor_reduce(out=mred[:, 4 * i:4 * i + 1], in_=mrow[:, NV * i:NV * (i + 1)],
                                  axis=AX.X, op=ALU.max), "sV")
            p.wait(e, "sV")
            p.inc(e.tensor_scalar(out=mred[:, 4 * i + 1:4 * i + 2], in0=mred[:, 4 * i:4 * i + 1],
                                  scalar1=1.0 / QMAX, scalar2=1e-12, op0=ALU.mult, op1=ALU.max), "sV")
            p.wait(e, "sV")
            p.inc(e.reciprocal(out=mred[:, 4 * i + 2:4 * i + 3], in_=mred[:, 4 * i + 1:4 * i + 2]), "sV")
            p.wait(e, "sV")
            p.inc(e.tensor_scalar(out=mred[:, 4 * i + 3:4 * i + 4], in0=mred[:, 4 * i + 1:4 * i + 2],
                                  scalar1=1.0 / 65536.0, scalar2=None, op0=ALU.mult), "sV")
            p.wait(e, "sV")

    @blk.scalar
    def _(e):
        p.wait(e, "sV")
        p.inc(e.dma_start(out=scr_dram[0:1, :], in_=mred[:]), qname, 16)
        e.wait_ge(p.sems[qname], p.cnt[qname])
        p.inc(e.dma_start(out=out_sb[:], in_=scr_dram[0:1, :].broadcast_to([128, 4 * nmax])), qname, 16)
    return p.cnt[qname]


def emit_xq(p, blk, TC, TSL, x_fm, murstd, sgc, xq, bufs, sg_ready=None):
    NSL = TC // TSL
    xfs, mue, rse, xns, tq = bufs["xfs"], bufs["mue"], bufs["rse"], bufs["xns"], bufs["tq"]
    for s in range(NSL):
        c0 = s * TSL

        @blk.sync
        def _(e, s=s, c0=c0):
            if s > 0:
                p.wait(e, "sV")
            p.inc(e.dma_start(out=xfs[:, :, 0:TSL], in_=fm3(x_fm)[:, :, c0 + 1:c0 + TSL + 1]), "qS", 16)
            p.inc(e.dma_start(out=mue[:, 0:TSL], in_=murstd[0:1, c0 + 1:c0 + TSL + 1].broadcast_to([128, TSL])), "qS", 16)
            p.inc(e.dma_start(out=rse[:, 0:TSL], in_=murstd[1:2, c0 + 1:c0 + TSL + 1].broadcast_to([128, TSL])), "qS", 16)
        qS_ld = p.cnt["qS"]

        @blk.vector
        def _(e, s=s, c0=c0, qS_ld=qS_ld):
            e.wait_ge(p.sems["qS"], qS_ld)
            if sg_ready is not None:
                e.wait_ge(p.sems[sg_ready[0]], sg_ready[1])
            p.inc(e.tensor_tensor(out=xns[:, :, 0:TSL], in0=xfs[:, :, 0:TSL],
                                  in1=mue[:, 0:TSL].rearrange("p (o c) -> p o c", o=1).broadcast_to([128, 8, TSL]),
                                  op=ALU.subtract), "sV")
            p.wait(e, "sV")
            p.inc(e.tensor_tensor(out=xns[:, :, 0:TSL], in0=xns[:, :, 0:TSL],
                                  in1=rse[:, 0:TSL].rearrange("p (o c) -> p o c", o=1).broadcast_to([128, 8, TSL]),
                                  op=ALU.mult), "sV")
            p.wait(e, "sV")
            p.inc(e.tensor_scalar(out=tq[:, :, 0:TSL], in0=xns[:, :, 0:TSL], scalar1=sgc[:, 2:3],
                                  scalar2=RND, op0=ALU.mult, op1=ALU.add), "sV")
            p.wait(e, "sV")
            p.inc(e.tensor_scalar(out=xq[:, :, c0:c0 + TSL], in0=tq[:, :, 0:TSL], scalar1=RND,
                                  scalar2=sgc[:, 1:2], op0=ALU.subtract, op1=ALU.mult), "sV")


def emit_dequant(p, blk, W_dram, wdq_dram, O, tiles_sb, rnd_col, qd="qA"):
    CW = W_dram.shape[1]
    NB = CW // BLK
    NT = O // 128
    wtile, wq, babs, sblk, rblk = tiles_sb
    qp_mark = {}
    q_mark = {}
    m_mark = {}

    def stageA(j):
        bi = j % 2

        @blk.scalar
        def _(e, j=j, bi=bi):
            if j >= 2:
                p.wait(e, "sV", q_mark[j - 2])
            e.wait_ge(p.sems[qd], p.cnt[qd])
            p.inc(e.dma_start(out=wtile[bi][:], in_=W_dram[128 * j:128 * (j + 1), :]), qd, 16)
        qv = p.cnt[qd]

        @blk.vector
        def _(e, j=j, bi=bi, qv=qv):
            e.wait_ge(p.sems[qd], qv)
            if j >= 2:
                p.wait(e, "sV", q_mark[j - 2])
            p.inc(e.tensor_reduce(out=babs[bi][:], in_=wtile[bi].rearrange("p (b k) -> p b k", k=BLK),
                                  axis=AX.X, op=ALU.max, apply_absolute_value=True), "sV")
            p.wait(e, "sV")
            p.inc(e.tensor_scalar(out=sblk[bi][:], in0=babs[bi][:], scalar1=1.0 / WQMAX, scalar2=1e-12,
                                  op0=ALU.mult, op1=ALU.max), "sV")
            p.wait(e, "sV")
            p.inc(e.reciprocal(out=rblk[bi][:], in_=sblk[bi][:]), "sV")
            p.wait(e, "sV")
            p.inc(e.tensor_tensor(out=wtile[bi].rearrange("p (b k) -> p b k", k=BLK),
                                  in0=wtile[bi].rearrange("p (b k) -> p b k", k=BLK),
                                  in1=rblk[bi].rearrange("p (b o) -> p b o", o=1).broadcast_to([128, NB, BLK]),
                                  op=ALU.mult), "sV")
        m_mark[j] = p.cnt["sV"]

    def stageB(j):
        bi = j % 2
        sV_m = m_mark[j]

        @blk.scalar
        def _(e, j=j, bi=bi, sV_m=sV_m):
            p.wait(e, "sV", sV_m)
            p.inc(e.activation(out=wtile[bi][:], in_=wtile[bi][:], func=AF.Identity,
                               bias=rnd_col, scale=1.0), "sA")
        sA_m = p.cnt["sA"]

        @blk.vector
        def _(e, j=j, bi=bi, sA_m=sA_m):
            p.wait(e, "sA", sA_m)
            if j >= 2:
                e.wait_ge(p.sems[qp_mark[j - 2][0]], qp_mark[j - 2][1])
            p.inc(e.scalar_tensor_tensor(out=wq[bi].rearrange("p (b k) -> p b k", k=BLK),
                                         in0=wtile[bi].rearrange("p (b k) -> p b k", k=BLK),
                                         scalar=RND, in1=sblk[bi].rearrange("p (b o) -> p b o", o=1).broadcast_to([128, NB, BLK]),
                                         op0=ALU.subtract, op1=ALU.mult), "sV")
        q_mark[j] = p.cnt["sV"]

        @blk.gpsimd
        def _(e, j=j, bi=bi, sV_q=q_mark[j]):
            p.wait(e, "sV", sV_q)
            qps = "qP%d" % bi
            p.inc(e.dma_start(out=wdq_dram[128 * j:128 * (j + 1), :], in_=wq[bi][:]), qps, 16)
        qp_mark[j] = ("qP%d" % bi, p.cnt["qP%d" % bi])

    for it in range(NT + 1):
        if it < NT:
            stageA(it)
        if it >= 1:
            stageB(it - 1)
    return p.cnt["qP"]


def emit_upgate_mm(p, blk, wdq_dram, wT, xq, TC, psD, utile, pxacc, col0,
                   i16_out=None, qi16=None, sgu=None, sig_chain=None, qw="qA",
                   sg_ready=None):
    NT = FFI // 128
    NTC = TC // 512
    for kc in range(8):
        @blk.scalar
        def _(e, kc=kc):
            if kc == 0:
                p.wait(e, "sT")
            for qq in ("qP", "qP0", "qP1"):
                e.wait_ge(p.sems[qq], p.cnt[qq])
            p.inc(e.dma_start(out=wT[:, kc, :], in_=wdq_dram[0:FFI, 128 * kc:128 * (kc + 1)],
                              transpose=True), qw, 16)
    qv_w = p.cnt[qw]
    sV_xq = p.cnt["sV"]
    evac_done = {-2: p.cnt["sA"], -1: p.cnt["sA"]}
    cons_done = {-2: p.cnt["sV"], -1: p.cnt["sV"]}
    ship_done = {-2: ("qO0", p.cnt.get("qO0", 0)), -1: ("qO1", p.cnt.get("qO1", 0))}
    for ot in range(NT):
        for tch in range(NTC):
            idx = ot * NTC + tch
            pp = psD[idx % 2]

            @blk.tensor
            def _(e, ot=ot, tch=tch, pp=pp, idx=idx):
                e.wait_ge(p.sems[qw], qv_w)
                p.wait(e, "sV", sV_xq)
                p.wait(e, "sA", evac_done[idx - 2])
                for kc in range(8):
                    ii = e.matmul(pp[:, 0:512], wT[:, kc, 128 * ot:128 * ot + 128],
                                  xq[:, kc, 512 * tch:512 * tch + 512],
                                  start=(kc == 0), stop=(kc == 7))
                p.inc(ii, "sT")

            @blk.scalar
            def _(e, ot=ot, tch=tch, pp=pp, idx=idx):
                p.wait(e, "sT")
                p.wait(e, "sV", cons_done[idx - 2])
                if i16_out is None:
                    p.inc(e.activation(out=utile[idx % 2][:], in_=pp[:, 0:512], func=AF.Identity), "sA")
                else:
                    if sg_ready is not None:
                        e.wait_ge(p.sems[sg_ready[0]], sg_ready[1])
                    p.inc(e.activation(out=utile[idx % 2][:], in_=pp[:, 0:512], func=AF.Identity,
                                       bias=p.constcol(RND), scale=sgu[:, 2:3]), "sA")
            evac_done[idx] = p.cnt["sA"]
            sA_e = p.cnt["sA"]

            @blk.vector
            def _(e, ot=ot, tch=tch, idx=idx, sA_e=sA_e):
                p.wait(e, "sA", sA_e)
                ut = utile[idx % 2]
                if i16_out is None:
                    p.inc(e.tensor_reduce(out=pxacc[:, col0 + idx:col0 + idx + 1], in_=ut[:],
                                          axis=AX.X, op=ALU.max, apply_absolute_value=True), "sV")
                else:
                    e.wait_ge(p.sems[ship_done[idx - 2][0]], ship_done[idx - 2][1])
                    p.inc(e.tensor_scalar(out=qi16[idx % 2][:], in0=ut[:], scalar1=RND, scalar2=None,
                                          op0=ALU.subtract), "sV")
            if i16_out is not None:
                sV_q = p.cnt["sV"]

                qos = "qO%d" % (idx % 2)

                @blk.sync
                def _(e, ot=ot, tch=tch, idx=idx, sV_q=sV_q, qos=qos):
                    p.wait(e, "sV", sV_q)
                    p.inc(e.dma_start(out=i16_out[128 * ot:128 * (ot + 1), 512 * tch:512 * (tch + 1)],
                                      in_=qi16[idx % 2][:]), qos, 16)
                ship_done[idx] = (qos, p.cnt[qos])
                if sig_chain is not None:
                    gqf, sigt, sigq, g2t = sig_chain
                    bi = idx % 2

                    @blk.vector
                    def _(e, idx=idx, bi=bi, sV_q=sV_q):
                        p.wait(e, "sV", sV_q)
                        p.inc(e.tensor_scalar(out=gqf[bi][:], in0=qi16[bi][:], scalar1=sgu[:, 1:2],
                                              scalar2=None, op0=ALU.mult), "sV")
                    sV_g = p.cnt["sV"]

                    @blk.scalar
                    def _(e, idx=idx, bi=bi, sV_g=sV_g):
                        p.wait(e, "sV", sV_g)
                        p.inc(e.activation(out=sigt[bi][:], in_=gqf[bi][:], func=AF.Tanh, scale=0.5), "sA")
                    sA_s = p.cnt["sA"]

                    @blk.vector
                    def _(e, idx=idx, bi=bi, sA_s=sA_s):
                        p.wait(e, "sA", sA_s)
                        p.inc(e.tensor_scalar(out=sigq[bi][:], in0=sigt[bi][:], scalar1=32768.0, scalar2=32768.0 + RND,
                                              op0=ALU.mult, op1=ALU.add), "sV")
                        p.wait(e, "sV")
                        p.inc(e.tensor_scalar(out=sigq[bi][:], in0=sigq[bi][:], scalar1=RND + 65535.0,
                                              scalar2=RND, op0=ALU.min, op1=ALU.subtract), "sV")
                        p.wait(e, "sV")
                        p.inc(e.scalar_tensor_tensor(out=g2t[bi][:], in0=sigq[bi][:], scalar=1.0 / 65536.0,
                                                     in1=gqf[bi][:], op0=ALU.mult, op1=ALU.mult), "sV")
                        p.wait(e, "sV")
                        p.inc(e.tensor_reduce(out=pxacc[:, col0 + idx:col0 + idx + 1], in_=g2t[bi][:],
                                              axis=AX.X, op=ALU.max, apply_absolute_value=True), "sV")
            cons_done[idx] = p.cnt["sV"]
    return


def build_L2(TC=2048, TSL=512):
    TTP = ceil_div(TC + 1, 128) * 128
    NTC = TC // 512
    p = Prog("L2")
    nc = p.nc
    x_fm = p.din("x_fm", [C, TC + 1])
    murstd = p.din("murstd", [2, TTP])
    pxn_row = p.din("pxn_row", [1, 1024])
    w_up = p.din("w_up", [FFI, C])
    w_gate = p.din("w_gate", [FFI, C])
    pxu_o = p.dout("pxu", [128, 1])
    pxg_o = p.dout("pxg", [128, 1])
    wdqu_o = p.dout("wdq_up", [FFI, C], BF16)
    wdqg_o = p.dout("wdq_gate", [FFI, C], BF16)
    sgscr = nc.dram_tensor("sgscr", [1, 4], F32, kind="Internal").ap()

    sgx = p.sb("sgx", [128, 4])
    xq = p.sb("xq", [128, 8, TC], BF16)
    wT = p.sb("wT", [128, 8, FFI], BF16)
    wtile = [p.sb(f"wt{i}", [128, C]) for i in range(2)]
    wq2 = [p.sb(f"wq{i}", [128, C], BF16) for i in range(2)]
    babs = [p.sb(f"babs{i}", [128, C // BLK]) for i in range(2)]
    sblk = [p.sb(f"sblk{i}", [128, C // BLK]) for i in range(2)]
    rblk = [p.sb(f"rblk{i}", [128, C // BLK]) for i in range(2)]
    utile = [p.sb(f"ut{i}", [128, 512]) for i in range(2)]
    pxacc = p.sb("pxacc", [128, 2 * 24 * NTC + 2])
    bufs = dict(xfs=p.sb("xfs", [128, 8, TSL]), mue=p.sb("mue", [128, TSL]),
                rse=p.sb("rse", [128, TSL]), xns=p.sb("xns", [128, 8, TSL]),
                tq=p.sb("tq", [128, 8, TSL]))
    rnd_col = p.constcol(RND)
    psD = [p.ps(f"psD{i}", [128, 512]) for i in range(2)]

    for s in ("qS", "qA", "qP", "qP0", "qP1", "qG", "qD", "qW", "qO", "sV", "sA", "sP", "sT"):
        p.sem(s)

    with nc.Block() as blk:
        @blk.vector
        def _(e):
            p.emit_const_memsets(e, "sV")
        qsg = _sigma_cols(p, blk, pxn_row, 1, 1024, sgscr, sgx, "qG")
        emit_xq(p, blk, TC, TSL, x_fm, murstd, sgx, xq, bufs, sg_ready=("qG", qsg))
        emit_dequant(p, blk, w_up, wdqu_o, FFI, (wtile, wq2, babs, sblk, rblk), rnd_col, "qD")
        emit_upgate_mm(p, blk, wdqu_o, wT, xq, TC, psD, utile, pxacc, 0, qw="qW")
        emit_dequant(p, blk, w_gate, wdqg_o, FFI, (wtile, wq2, babs, sblk, rblk), rnd_col, "qD")
        emit_upgate_mm(p, blk, wdqg_o, wT, xq, TC, psD, utile, pxacc, 24 * NTC, qw="qW")

        NCOL = 24 * NTC

        @blk.vector
        def _(e):
            p.wait(e, "sA")
            p.wait(e, "sV")
            p.inc(e.tensor_reduce(out=pxacc[:, 2 * NCOL:2 * NCOL + 1], in_=pxacc[:, 0:NCOL],
                                  axis=AX.X, op=ALU.max), "sV")
            p.wait(e, "sV")
            p.inc(e.tensor_reduce(out=pxacc[:, 2 * NCOL + 1:2 * NCOL + 2], in_=pxacc[:, NCOL:2 * NCOL],
                                  axis=AX.X, op=ALU.max), "sV")
        sV_fin = p.cnt["sV"]

        @blk.sync
        def _(e):
            p.wait(e, "sV", sV_fin)
            p.inc(e.dma_start(out=pxu_o[:, :], in_=pxacc[:, 2 * NCOL:2 * NCOL + 1]), "qS", 16)
            p.inc(e.dma_start(out=pxg_o[:, :], in_=pxacc[:, 2 * NCOL + 1:2 * NCOL + 2]), "qS", 16)
            for q in ("qS", "qG", "qD", "qW", "qP", "qP0", "qP1", "qO"):
                e.wait_ge(p.sems[q], p.cnt[q])
    return p


def build_L3(TC=2048, TSL=512):
    TTP = ceil_div(TC + 1, 128) * 128
    NTC = TC // 512
    p = Prog("L3")
    nc = p.nc
    x_fm = p.din("x_fm", [C, TC + 1])
    murstd = p.din("murstd", [2, TTP])
    pxn_row = p.din("pxn_row", [1, 1024])
    pxug_row = p.din("pxug_row", [1, 2048])
    wdqu = p.din("wdq_up", [FFI, C], BF16)
    wdqg = p.din("wdq_gate", [FFI, C], BF16)
    upq_o = p.dout("upq", [FFI, TC], I16)
    gq_o = p.dout("gq", [FFI, TC], I16)
    pxg2_o = p.dout("pxg2", [128, 1])
    sgscr = nc.dram_tensor("sgscr", [1, 4], F32, kind="Internal").ap()
    sgscr2 = nc.dram_tensor("sgscr2", [1, 8], F32, kind="Internal").ap()

    sgx = p.sb("sgx", [128, 4])
    sgug = p.sb("sgug", [128, 8])
    xq = p.sb("xq", [128, 8, TC], BF16)
    wT = p.sb("wT", [128, 8, FFI], BF16)
    utile = [p.sb(f"ut{i}", [128, 512]) for i in range(2)]
    qi16 = [p.sb(f"qi{i}", [128, 512], I16) for i in range(2)]
    gqf = [p.sb(f"gqf{i}", [128, 512]) for i in range(2)]
    sigt = [p.sb(f"sigt{i}", [128, 512]) for i in range(2)]
    sigq = [p.sb(f"sigq{i}", [128, 512]) for i in range(2)]
    g2t = [p.sb(f"g2t{i}", [128, 512]) for i in range(2)]
    pxacc = p.sb("pxacc", [128, 24 * NTC + 1])
    bufs = dict(xfs=p.sb("xfs", [128, 8, TSL]), mue=p.sb("mue", [128, TSL]),
                rse=p.sb("rse", [128, TSL]), xns=p.sb("xns", [128, 8, TSL]),
                tq=p.sb("tq", [128, 8, TSL]))
    rnd_col = p.constcol(RND)
    psD = [p.ps(f"psD{i}", [128, 512]) for i in range(2)]

    for s in ("qS", "qA", "qP", "qP0", "qP1", "qG", "qG2", "qW", "qO", "qO0", "qO1",
              "sV", "sA", "sP", "sT"):
        p.sem(s)

    with nc.Block() as blk:
        @blk.vector
        def _(e):
            p.emit_const_memsets(e, "sV")
        qs1 = _sigma_cols(p, blk, pxn_row, 1, 1024, sgscr, sgx, "qG")
        qsg = _sigma_cols(p, blk, pxug_row, 2, 1024, sgscr2, sgug, "qG2")
        emit_xq(p, blk, TC, TSL, x_fm, murstd, sgx, xq, bufs, sg_ready=("qG", qs1))
        emit_upgate_mm(p, blk, wdqu, wT, xq, TC, psD, utile, pxacc, 0,
                       i16_out=upq_o, qi16=qi16, sgu=sgug[:, 0:4], qw="qW",
                       sg_ready=("qG2", qsg))
        emit_upgate_mm(p, blk, wdqg, wT, xq, TC, psD, utile, pxacc, 0,
                       i16_out=gq_o, qi16=qi16, sgu=sgug[:, 4:8],
                       sig_chain=(gqf, sigt, sigq, g2t), qw="qW",
                       sg_ready=("qG2", qsg))
        NCOL = 24 * NTC

        @blk.vector
        def _(e):
            p.wait(e, "sA")
            p.wait(e, "sV")
            p.inc(e.tensor_reduce(out=pxacc[:, NCOL:NCOL + 1], in_=pxacc[:, 0:NCOL],
                                  axis=AX.X, op=ALU.max), "sV")
        sV_fin = p.cnt["sV"]

        @blk.sync
        def _(e):
            p.wait(e, "sV", sV_fin)
            p.inc(e.dma_start(out=pxg2_o[:, :], in_=pxacc[:, NCOL:NCOL + 1]), "qS", 16)
            for q in ("qS", "qG", "qG2", "qW", "qO", "qO0", "qO1", "qP0", "qP1"):
                e.wait_ge(p.sems[q], p.cnt[q])
    return p


def build_L4(TC=2048):
    NT = FFI // 128
    NTC = TC // 512
    p = Prog("L4")
    nc = p.nc
    upq = p.din("upq", [FFI, TC], I16)
    gq = p.din("gq", [FFI, TC], I16)
    pxug_row = p.din("pxug_row", [1, 2048])
    pxg2_row = p.din("pxg2_row", [1, 1024])
    pxo_o = p.dout("pxo", [128, 1])
    sgscr2 = nc.dram_tensor("sgscr2", [1, 8], F32, kind="Internal").ap()
    sgscr3 = nc.dram_tensor("sgscr3", [1, 4], F32, kind="Internal").ap()

    sgug = p.sb("sgug", [128, 8])
    sg3 = p.sb("sg3", [128, 4])
    ui = [p.sb(f"ui{i}", [128, 512], I16) for i in range(2)]
    gi = [p.sb(f"gi{i}", [128, 512], I16) for i in range(2)]
    gqf = [p.sb(f"gqf{i}", [128, 512]) for i in range(2)]
    sigt = [p.sb(f"sigt{i}", [128, 512]) for i in range(2)]
    sigq = [p.sb(f"sigq{i}", [128, 512]) for i in range(2)]
    g2t = [p.sb(f"g2t{i}", [128, 512]) for i in range(2)]
    g2q = [p.sb(f"g2q{i}", [128, 512]) for i in range(2)]
    ot = [p.sb(f"ot{i}", [128, 512]) for i in range(2)]
    NTL = NT * NTC
    pxacc = p.sb("pxacc", [128, NTL + 1])
    rnd_col = p.constcol(RND)

    for s in ("qS", "qG", "qG2", "qL0", "qL1", "sV", "sA", "sP", "sT"):
        p.sem(s)

    with nc.Block() as blk:
        @blk.vector
        def _(e):
            p.emit_const_memsets(e, "sV")
        qg1 = _sigma_cols(p, blk, pxug_row, 2, 1024, sgscr2, sgug, "qG")
        qg2 = _sigma_cols(p, blk, pxg2_row, 1, 1024, sgscr3, sg3, "qG2")

        chain_end = {}
        for j in range(NT):
            for tc4 in range(NTC):
                idx = j * NTC + tc4
                bi = idx % 2
                ql = "qL%d" % bi

                @blk.sync
                def _(e, j=j, tc4=tc4, bi=bi, idx=idx, ql=ql):
                    if idx >= 2:
                        p.wait(e, "sV", chain_end[idx - 2])
                    p.inc(e.dma_start(out=ui[bi][:], in_=upq[128 * j:128 * (j + 1), 512 * tc4:512 * (tc4 + 1)]), ql, 16)
                    p.inc(e.dma_start(out=gi[bi][:], in_=gq[128 * j:128 * (j + 1), 512 * tc4:512 * (tc4 + 1)]), ql, 16)
                qld = p.cnt[ql]

                @blk.vector
                def _(e, bi=bi, idx=idx, qld=qld, ql=ql):
                    e.wait_ge(p.sems[ql], qld)
                    if idx == 0:
                        e.wait_ge(p.sems["qG"], qg1)
                        e.wait_ge(p.sems["qG2"], qg2)
                    if idx >= 1:
                        p.wait(e, "sV", chain_end.get(idx - 1, 0))
                    p.inc(e.tensor_scalar(out=gqf[bi][:], in0=gi[bi][:], scalar1=sgug[:, 5:6],
                                          scalar2=None, op0=ALU.mult), "sV")
                sV_g = p.cnt["sV"]

                @blk.scalar
                def _(e, bi=bi, sV_g=sV_g):
                    p.wait(e, "sV", sV_g)
                    p.inc(e.activation(out=sigt[bi][:], in_=gqf[bi][:], func=AF.Tanh, scale=0.5), "sA")
                sA_s = p.cnt["sA"]

                @blk.vector
                def _(e, bi=bi, idx=idx, sA_s=sA_s):
                    p.wait(e, "sA", sA_s)
                    p.inc(e.tensor_scalar(out=sigq[bi][:], in0=sigt[bi][:], scalar1=32768.0, scalar2=32768.0 + RND,
                                          op0=ALU.mult, op1=ALU.add), "sV")
                    p.wait(e, "sV")
                    p.inc(e.tensor_scalar(out=sigq[bi][:], in0=sigq[bi][:], scalar1=RND + 65535.0,
                                          scalar2=RND, op0=ALU.min, op1=ALU.subtract), "sV")
                    p.wait(e, "sV")
                    p.inc(e.scalar_tensor_tensor(out=g2t[bi][:], in0=sigq[bi][:], scalar=1.0 / 65536.0,
                                                 in1=gqf[bi][:], op0=ALU.mult, op1=ALU.mult), "sV")
                    p.wait(e, "sV")
                    p.inc(e.tensor_scalar(out=g2q[bi][:], in0=g2t[bi][:], scalar1=sg3[:, 2:3],
                                          scalar2=RND, op0=ALU.mult, op1=ALU.add), "sV")
                    p.wait(e, "sV")
                    p.inc(e.tensor_scalar(out=g2q[bi][:], in0=g2q[bi][:], scalar1=RND,
                                          scalar2=sg3[:, 1:2], op0=ALU.subtract, op1=ALU.mult), "sV")
                    p.wait(e, "sV")
                    p.inc(e.scalar_tensor_tensor(out=ot[bi][:], in0=ui[bi][:], scalar=sgug[:, 1:2],
                                                 in1=g2q[bi][:], op0=ALU.mult, op1=ALU.mult), "sV")
                    p.wait(e, "sV")
                    p.inc(e.tensor_reduce(out=pxacc[:, idx:idx + 1], in_=ot[bi][:],
                                          axis=AX.X, op=ALU.max, apply_absolute_value=True), "sV")
                chain_end[idx] = p.cnt["sV"]

        @blk.vector
        def _(e):
            p.wait(e, "sV")
            p.wait(e, "sA")
            p.inc(e.tensor_reduce(out=pxacc[:, NTL:NTL + 1], in_=pxacc[:, 0:NTL],
                                  axis=AX.X, op=ALU.max), "sV")
        sV_fin = p.cnt["sV"]

        @blk.sync
        def _(e):
            p.wait(e, "sV", sV_fin)
            p.inc(e.dma_start(out=pxo_o[:, :], in_=pxacc[:, NTL:NTL + 1]), "qS", 16)
            for q in ("qS", "qG", "qG2", "qL0", "qL1"):
                e.wait_ge(p.sems[q], p.cnt[q])
    return p


def build_L5(TC=2048):
    NCH = TC // L
    NQ = TC // 512
    p = Prog("L5")
    nc = p.nc
    x_fm = p.din("x_fm", [C, TC + 1])
    qt = p.din("qt", [C, TC], BF16)
    kt = p.din("kt", [C, TC], BF16)
    vq = p.din("vq", [TC, C], BF16)
    xg = p.din("xg", [C, TC], BF16)
    b_rowT = p.din("b_rowT", [TC, 16])
    utotP = p.din("utotP", [128, 512])
    traj = p.din("traj", [128, NCH * 512])
    dpre = p.din("dpre", [128, NCH * 8])
    WgT = p.din("WgT", [C, C])
    WoT = p.din("WoT", [C, C])
    w_down = p.din("w_down", [C, FFI])
    upq = p.din("upq", [FFI, TC], I16)
    gq = p.din("gq", [FFI, TC], I16)
    pxug_row = p.din("pxug_row", [1, 2048])
    pxg2_row = p.din("pxg2_row", [1, 1024])
    pxo_row = p.din("pxo_row", [1, 1024])
    gam_row = p.din("gam_row", [1, C])
    bet_row = p.din("bet_row", [1, C])
    tri = p.din("tri", [128, 128])
    ident = p.din("ident", [128, 128])
    out_o = p.dout("out_fm", [C, TC])
    attn_scr = nc.dram_tensor("attn_scr", [C, TC], F32, kind="Internal").ap()
    wdqd_scr = nc.dram_tensor("wdqd_scr", [C, FFI], BF16, kind="Internal").ap()
    sgscr2 = nc.dram_tensor("sgscr2", [1, 8], F32, kind="Internal").ap()
    sgscr3 = nc.dram_tensor("sgscr3", [1, 4], F32, kind="Internal").ap()
    sgscr4 = nc.dram_tensor("sgscr4", [1, 4], F32, kind="Internal").ap()

    sgug = p.sb("sgug", [128, 8])
    sg3 = p.sb("sg3", [128, 4])
    sg4 = p.sb("sg4", [128, 4])
    sgb1 = (p.sb("sgb1a", [1, 2048]), p.sb("sgb1b", [1, 8]))
    sgb2 = (p.sb("sgb2a", [1, 1024]), p.sb("sgb2b", [1, 4]))
    sgb3 = (p.sb("sgb3a", [1, 1024]), p.sb("sgb3b", [1, 4]))
    rnd_col = p.constcol(RND)
    epsg_col = p.constcol(16e-5)

    for s in ("qS", "qA", "qP", "qP0", "qP1", "qG", "qG2", "qG3", "qO", "qO0", "qO1",
              "qL0", "qL1", "qC0", "qC1", "qW", "qD", "qT0", "qT1", "sV", "sA", "sP", "sT"):
        p.sem(s)

    from contextlib import ExitStack
    subA = ExitStack()

    def sba(name, shape, dtype=F32):
        return subA.enter_context(nc.sbuf_tensor(name, list(shape), dtype))

    wg = sba("wg", [128, 8, C], BF16)
    wo = sba("wo", [128, 8, C], BF16)
    u0s = sba("u0s", [128, 8, 64])
    gex = sba("gex", [128, C])
    bex = sba("bex", [128, C])
    trim = sba("trim", [128, 128])
    idn = sba("idn", [128, 128])
    stgw = sba("stgw", [128, 2, C])
    qtc = [sba(f"qtc{i}", [128, 8, 128], BF16) for i in range(4)]
    ktc = [sba(f"ktc{i}", [128, 8, 128], BF16) for i in range(4)]
    vqc = [sba(f"vqc{i}", [128, C], BF16) for i in range(4)]
    xgc = [sba(f"xgc{i}", [128, 8, 128], BF16) for i in range(4)]
    trajc = [sba(f"trajc{i}", [128, 512]) for i in range(4)]
    dpc = [sba(f"dpc{i}", [128, 8]) for i in range(4)]
    bTc = [sba(f"bTc{i}", [128, 16]) for i in range(4)]
    scg = [sba(f"scg{i}", [128, 8, 64], BF16) for i in range(4)]
    amb = [sba(f"amb{i}", [128, 16 * 128], BF16) for i in range(4)]
    ytc = [sba(f"ytc{i}", [128, C]) for i in range(4)]
    ysq = [sba(f"ysq{i}", [128, C]) for i in range(4)]
    gnt = [sba(f"gnt{i}", [128, 80]) for i in range(4)]
    gtc = [sba(f"gtc{i}", [128, C]) for i in range(4)]
    gth = [sba(f"gth{i}", [128, 512], BF16) for i in range(4)]
    gth1 = [sba(f"gth1{i}", [128, 512]) for i in range(4)]
    ygb = [sba(f"ygb{i}", [128, C], BF16) for i in range(4)]
    ygf = [sba(f"ygf{i}", [128, 8, 128], BF16) for i in range(4)]
    afm = [sba(f"afm{i}", [128, 128]) for i in range(2)]

    sub_ps = ExitStack()
    psA2 = [sub_ps.enter_context(nc.psum_tensor(f"psA{i}", [128, 512], F32)) for i in range(2)]
    psY2 = [sub_ps.enter_context(nc.psum_tensor(f"psY{i}", [128, 512], F32)) for i in range(2)]
    psWo = [sub_ps.enter_context(nc.psum_tensor(f"psWo{i}", [128, 512], F32)) for i in range(2)]
    psG = [sub_ps.enter_context(nc.psum_tensor(f"psG{i}", [128, 512], F32)) for i in range(2)]

    with nc.Block() as blk:
        @blk.vector
        def _(e):
            p.emit_const_memsets(e, "sV")
        qg1 = _sigma_cols(p, blk, pxug_row, 2, 1024, sgscr2, sgug, "qG", bufs=sgb1)
        qg2 = _sigma_cols(p, blk, pxg2_row, 1, 1024, sgscr3, sg3, "qG2", bufs=sgb2)
        qg3 = _sigma_cols(p, blk, pxo_row, 1, 1024, sgscr4, sg4, "qG3", bufs=sgb3)

        @blk.sync
        def _(e):
            p.inc(e.dma_start(out=u0s[:], in_=utotP.rearrange("p (g j) -> p g j", j=64)), "qS", 16)
            p.inc(e.dma_start(out=gex[:], in_=gam_row[0:1, :].broadcast_to([128, C])), "qS", 16)
            p.inc(e.dma_start(out=bex[:], in_=bet_row[0:1, :].broadcast_to([128, C])), "qS", 16)
            p.inc(e.dma_start(out=trim[:], in_=tri[:, :]), "qS", 16)
            p.inc(e.dma_start(out=idn[:], in_=ident[:, :]), "qS", 16)
        qS_c = p.cnt["qS"]

        for wsb, whbm in ((wg, WgT), (wo, WoT)):
            for t2 in range(0, 8, 2):
                @blk.scalar
                def _(e, wsb=wsb, whbm=whbm, t2=t2):
                    p.wait(e, "sA")
                    p.inc(e.dma_start(out=stgw[:], in_=fm3(whbm)[:, t2:t2 + 2, :]), "qP", 16)
                    e.wait_ge(p.sems["qP"], p.cnt["qP"])
                    p.inc(e.activation(out=wsb[:, t2:t2 + 2, :], in_=stgw[:], func=AF.Identity), "sA")
        qP_c = p.cnt["sA"]

        chunk_done = {}
        for _i in range(-4, 0):
            chunk_done[_i] = (0, 0, 0)
        amb_done = {}
        ybank_done = {-2: p.cnt["sV"], -1: p.cnt["sV"]}
        ymm_sT = {-4: 0, -3: 0, -2: 0, -1: 0}
        ygb_free = {-1: ("qT0", 0), -2: ("qT0", 0), -3: ("qT0", 0), -4: ("qT0", 0)}
        apply_done = {}
        womm_sT = {-4: 0, -3: 0, -2: 0, -1: 0}
        gproj_done = {-2: (0, 0), -1: (p.cnt["sA"], p.cnt["sV"])}
        afm_done = {-2: p.cnt["sA"], -1: p.cnt["sA"]}
        woship = {-2: ("qO", 0), -1: ("qO", 0)}
        s1info = {}

        def emit_s1(gc):
            bi = gc % 4
            qc = "qC%d" % (gc % 2)
            t0 = gc * L

            @blk.sync
            def _(e, gc=gc, bi=bi, qc=qc, t0=t0):
                e.wait_ge(p.sems["qS"], qS_c)
                if gc >= 4:
                    cd = chunk_done[gc - 4]
                    p.wait(e, "sV", cd[0])
                    p.wait(e, "sA", cd[1])
                    p.wait(e, "sT", cd[2])
                p.inc(e.dma_start(out=qtc[bi][:], in_=fm3(qt)[:, :, t0:t0 + L]), qc, 16)
                p.inc(e.dma_start(out=ktc[bi][:], in_=fm3(kt)[:, :, t0:t0 + L]), qc, 16)
                p.inc(e.dma_start(out=vqc[bi][:], in_=vq[t0:t0 + L, :]), qc, 16)
                p.inc(e.dma_start(out=xgc[bi][:], in_=fm3(xg)[:, :, t0:t0 + L]), qc, 16)
                p.inc(e.dma_start(out=trajc[bi][:], in_=traj[:, 512 * gc:512 * gc + 512]), qc, 16)
                p.inc(e.dma_start(out=dpc[bi][:], in_=dpre[:, 8 * gc:8 * gc + 8]), qc, 16)
                p.inc(e.dma_start(out=bTc[bi][:], in_=b_rowT[t0:t0 + L, :]), qc, 16)
            qld = p.cnt[qc]

            @blk.vector
            def _(e, gc=gc, bi=bi, qld=qld, qc=qc):
                e.wait_ge(p.sems[qc], qld)
                e.wait_ge(p.sems["qG"], qg1)
                e.wait_ge(p.sems["qG2"], qg2)
                e.wait_ge(p.sems["qG3"], qg3)
                p.wait(e, "sT", chunk_done[gc - 4][2] if gc >= 4 else 0)
                for g in range(8):
                    ii = e.scalar_tensor_tensor(out=scg[bi][:, g, :], in0=u0s[:, g, :],
                                                scalar=dpc[bi][:, g:g + 1], in1=trajc[bi][:, 64 * g:64 * g + 64],
                                                op0=ALU.mult, op1=ALU.add)
                p.inc(ii, "sV")

            for half in range(2):
                for par in range(2):
                    pb = psA2[par]
                    bp = 64 * par

                    @blk.tensor
                    def _(e, gc=gc, bi=bi, half=half, par=par, pb=pb, bp=bp, qld=qld, qc=qc):
                        e.wait_ge(p.sems[qc], qld)
                        pk = (gc, half - 1, par) if half > 0 else (gc - 1, 1, par)
                        if pk in amb_done:
                            p.wait(e, "sV", amb_done[pk])
                        for hh in range(4):
                            h = half * 8 + 2 * hh + par
                            ii = e.matmul(pb[:, 128 * hh:128 * hh + 128],
                                          ktc[bi][bp:bp + 64, h // 2, :],
                                          qtc[bi][bp:bp + 64, h // 2, :], start=True, stop=True)
                        p.inc(ii, "sT")
                    sT_a = p.cnt["sT"]

                    @blk.vector
                    def _(e, gc=gc, half=half, par=par, pb=pb, sT_a=sT_a):
                        p.wait(e, "sT", sT_a)
                        if half == 0:
                            p.wait(e, "sT", ymm_sT.get(gc - 4, 0))
                        oview = amb[gc % 4].rearrange("p (u v) -> p u v", v=128)[:, 8 * half + par:8 * half + 8:2, :]
                        p.inc(e.tensor_tensor(
                            out=oview,
                            in0=pb[:, :].rearrange("p (u v) -> p u v", v=128),
                            in1=trim.rearrange("p (o v) -> p o v", o=1).broadcast_to([128, 4, 128]),
                            op=ALU.mult), "sV")
                    amb_done[(gc, half, par)] = p.cnt["sV"]
            sV_am = p.cnt["sV"]

            for par in range(2):
                pb = psY2[par]
                bp = 64 * par

                @blk.tensor
                def _(e, gc=gc, bi=bi, par=par, pb=pb, bp=bp, sV_am=sV_am):
                    p.wait(e, "sV", sV_am)
                    p.wait(e, "sV", ybank_done[gc - 1])
                    for hh in range(8):
                        h = 2 * hh + par
                        e.matmul(pb[:, 64 * hh:64 * hh + 64], amb[bi][:, 128 * h:128 * h + 128],
                                 vqc[bi][:, 64 * h:64 * h + 64], start=True, stop=False)
                        ii = e.matmul(pb[:, 64 * hh:64 * hh + 64],
                                      qtc[bi][bp:bp + 64, h // 2, :],
                                      scg[bi][bp:bp + 64, h // 2, :], start=False, stop=True)
                    p.inc(ii, "sT")
            sT_y = p.cnt["sT"]
            ymm_sT[gc] = sT_y

            @blk.vector
            def _(e, gc=gc, bi=bi, sT_y=sT_y):
                p.wait(e, "sT", sT_y)
                p.wait(e, "sA")
                for h in range(16):
                    ii = e.scalar_tensor_tensor(out=ytc[bi][:, 64 * h:64 * h + 64],
                                                in0=vqc[bi][:, 64 * h:64 * h + 64],
                                                scalar=bTc[bi][:, h:h + 1],
                                                in1=psY2[h % 2][:, 64 * (h // 2):64 * (h // 2) + 64],
                                                op0=ALU.mult, op1=ALU.add)
                p.inc(ii, "sV")
            sV_y = p.cnt["sV"]
            ybank_done[gc] = sV_y

            @blk.vector
            def _(e, bi=bi, sV_y=sV_y):
                p.wait(e, "sV", sV_y)
                p.inc(e.tensor_tensor(out=ysq[bi][:], in0=ytc[bi][:], in1=ytc[bi][:], op=ALU.mult), "sV")
                p.wait(e, "sV")
                e.tensor_reduce(out=gnt[bi][:, 0:16], in_=ysq[bi].rearrange("p (h j) -> p h j", j=64),
                                axis=AX.X, op=ALU.add)
                p.inc(e.tensor_reduce(out=gnt[bi][:, 16:32], in_=ytc[bi].rearrange("p (h j) -> p h j", j=64),
                                      axis=AX.X, op=ALU.add), "sV")
                p.wait(e, "sV")
                e.tensor_scalar(out=gnt[bi][:, 32:48], in0=gnt[bi][:, 16:32],
                                scalar1=1.0 / 64.0, scalar2=None, op0=ALU.mult)
                p.inc(e.tensor_scalar(out=gnt[bi][:, 48:64], in0=gnt[bi][:, 0:16],
                                      scalar1=1.0 / 64.0, scalar2=None, op0=ALU.mult), "sV")
                p.wait(e, "sV")
                p.inc(e.tensor_tensor(out=gnt[bi][:, 0:16], in0=gnt[bi][:, 32:48], in1=gnt[bi][:, 32:48],
                                      op=ALU.mult), "sV")
                p.wait(e, "sV")
                p.inc(e.tensor_tensor(out=gnt[bi][:, 48:64], in0=gnt[bi][:, 48:64], in1=gnt[bi][:, 0:16],
                                      op=ALU.subtract), "sV")
            sV_gn = p.cnt["sV"]

            @blk.scalar
            def _(e, bi=bi, gc=gc, sV_gn=sV_gn):
                p.wait(e, "sV", sV_gn)
                p.wait(e, "sV", apply_done.get(gc - 4, 0))
                p.inc(e.activation(out=gnt[bi][:, 0:16], in_=gnt[bi][:, 48:64], func=AF.Sqrt,
                                   bias=epsg_col, scale=1.0), "sA")
            sA_sq = p.cnt["sA"]

            @blk.vector
            def _(e, bi=bi, sA_sq=sA_sq):
                p.wait(e, "sA", sA_sq)
                p.inc(e.reciprocal(out=gnt[bi][:, 64:80], in_=gnt[bi][:, 0:16]), "sV")
            sV_rs = p.cnt["sV"]
            s1info[gc] = (bi, qc, qld, sV_rs)

        def emit_s2(gc):
            bi, qc, qld, sV_rs = s1info[gc]
            t0 = gc * L

            for hh in range(2):
                pb = psG[hh]
                gi2 = (gc % 2) * 2 + hh

                @blk.tensor
                def _(e, gc=gc, bi=bi, hh=hh, pb=pb, qc=qc, qld=qld):
                    e.wait_ge(p.sems[qc], qld)
                    p.wait(e, "sA", qP_c)
                    p.wait(e, "sV", gproj_done[gc - 1][1])
                    for kc in range(8):
                        ii = e.matmul(pb[:, 0:512], xgc[bi][:, kc, :],
                                      wg[:, kc, 512 * hh:512 * hh + 512],
                                      start=(kc == 0), stop=(kc == 7))
                    p.inc(ii, "sT")
                sT_g = p.cnt["sT"]

                @blk.scalar
                def _(e, gc=gc, hh=hh, pb=pb, gi2=gi2, sT_g=sT_g):
                    p.wait(e, "sT", sT_g)
                    p.wait(e, "sV", gproj_done[gc - 2][1])
                    p.inc(e.activation(out=gth[gi2][:], in_=pb[:, 0:512],
                                       func=AF.Tanh, scale=0.5), "sA")
                sA_gth = p.cnt["sA"]

                @blk.vector
                def _(e, gc=gc, bi=bi, hh=hh, pb=pb, gi2=gi2, sA_gth=sA_gth):
                    p.wait(e, "sA", sA_gth)
                    p.inc(e.tensor_scalar(out=gth1[gi2][:], in0=gth[gi2][:], scalar1=1.0,
                                          scalar2=None, op0=ALU.add), "sV")
                    p.wait(e, "sV")
                    p.inc(e.scalar_tensor_tensor(out=gtc[bi][:, 512 * hh:512 * hh + 512],
                                                 in0=pb[:, 0:512], scalar=0.5, in1=gth1[gi2][:],
                                                 op0=ALU.mult, op1=ALU.mult), "sV")
            sA_g = p.cnt["sA"]
            sV_gd = p.cnt["sV"]
            gproj_done[gc] = (sA_g, sV_gd)

            @blk.vector
            def _(e, gc=gc, bi=bi, sV_rs=sV_rs, sA_g=sA_g):
                p.wait(e, "sV", sV_rs)
                p.wait(e, "sA", sA_g)
                e.wait_ge(p.sems[ygb_free[gc - 4][0]], ygb_free[gc - 4][1])
                p.inc(e.tensor_tensor(out=ysq[bi].rearrange("p (h j) -> p h j", j=64),
                                      in0=ytc[bi].rearrange("p (h j) -> p h j", j=64),
                                      in1=gnt[bi][:, 32:48].rearrange("p (h o) -> p h o", o=1).broadcast_to([128, 16, 64]),
                                      op=ALU.subtract), "sV")
                p.wait(e, "sV")
                p.inc(e.tensor_tensor(out=ysq[bi].rearrange("p (h j) -> p h j", j=64),
                                      in0=ysq[bi].rearrange("p (h j) -> p h j", j=64),
                                      in1=gnt[bi][:, 64:80].rearrange("p (h o) -> p h o", o=1).broadcast_to([128, 16, 64]),
                                      op=ALU.mult), "sV")
                p.wait(e, "sV")
                p.inc(e.tensor_tensor(out=ysq[bi][:], in0=ysq[bi][:], in1=gex[:], op=ALU.mult), "sV")
                p.wait(e, "sV")
                p.inc(e.tensor_tensor(out=ysq[bi][:], in0=ysq[bi][:], in1=bex[:], op=ALU.add), "sV")
                p.wait(e, "sV")
                p.inc(e.tensor_tensor(out=ygb[bi][:], in0=ysq[bi][:], in1=gtc[bi][:], op=ALU.mult), "sV")
            sV_yg = p.cnt["sV"]
            apply_done[gc] = sV_yg

            qtp = "qT%d" % (gc % 2)

            @blk.scalar
            def _(e, gc=gc, bi=bi, sV_yg=sV_yg, qtp=qtp):
                p.wait(e, "sV", sV_yg)
                p.wait(e, "sT", womm_sT.get(gc - 4, 0))
                e.wait_ge(p.sems[qtp], p.cnt[qtp])
                for cb in range(8):
                    p.inc(e.dma_start(out=ygf[bi][:, cb, :], in_=ygb[bi][:, 128 * cb:128 * cb + 128],
                                      transpose=True), qtp, 16)
            qT_y = p.cnt[qtp]
            ygb_free[gc] = (qtp, qT_y)

            for ct in range(8):
                wid = gc * 8 + ct
                pb = psWo[ct % 2]
                qos = "qO%d" % (ct % 2)

                @blk.tensor
                def _(e, gc=gc, ct=ct, pb=pb, qT_y=qT_y, qtp=qtp, wid=wid):
                    e.wait_ge(p.sems[qtp], qT_y)
                    p.wait(e, "sA", afm_done[wid - 2])
                    for kc in range(8):
                        ii = e.matmul(pb[:, 0:128], wo[:, kc, 128 * ct:128 * ct + 128],
                                      ygf[gc % 4][:, kc, :], start=(kc == 0), stop=(kc == 7))
                    p.inc(ii, "sT")
                sT_w = p.cnt["sT"]

                @blk.scalar
                def _(e, gc=gc, ct=ct, pb=pb, sT_w=sT_w, wid=wid, qos=qos):
                    p.wait(e, "sT", sT_w)
                    e.wait_ge(p.sems[woship[wid - 2][0]], woship[wid - 2][1])
                    p.inc(e.activation(out=afm[ct % 2][:], in_=pb[:, 0:128], func=AF.Identity), "sA")
                sA_w = p.cnt["sA"]
                afm_done[wid] = sA_w

                @blk.sync
                def _(e, gc=gc, ct=ct, t0=t0, sA_w=sA_w, qos=qos):
                    p.wait(e, "sA", sA_w)
                    p.inc(e.dma_start(out=attn_scr[128 * ct:128 * (ct + 1), t0:t0 + L],
                                      in_=afm[ct % 2][:]), qos, 16)
                woship[wid] = (qos, p.cnt[qos])
            womm_sT[gc] = p.cnt["sT"]
            chunk_done[gc] = (p.cnt["sV"], p.cnt["sA"], p.cnt["sT"])

        for it in range(NCH + 1):
            if it < NCH:
                emit_s1(it)
            if it >= 1:
                emit_s2(it - 1)

        sub_ps.close()
        subA.close()

        subB = ExitStack()

        def sbb(name, shape, dtype=F32):
            return subB.enter_context(nc.sbuf_tensor(name, list(shape), dtype))

        wtile = [sbb(f"dwt{i}", [128, FFI]) for i in range(2)]
        wq2 = [sbb(f"dwq{i}", [128, FFI], BF16) for i in range(2)]
        babs = [sbb(f"dbabs{i}", [128, FFI // BLK]) for i in range(2)]
        sblk = [sbb(f"dsblk{i}", [128, FFI // BLK]) for i in range(2)]
        rblk = [sbb(f"drblk{i}", [128, FFI // BLK]) for i in range(2)]
        wdT = sbb("wdT", [128, 24, C], BF16)
        ui = [sbb(f"ui{i}", [128, 512], I16) for i in range(2)]
        gi = [sbb(f"gi{i}", [128, 512], I16) for i in range(2)]
        gqf = [sbb(f"gqf{i}", [128, 512]) for i in range(2)]
        sigt = [sbb(f"sigt{i}", [128, 512]) for i in range(2)]
        sigq = sbb("sigq", [128, 512])
        g2t = sbb("g2t", [128, 512])
        g2q = sbb("g2q", [128, 512])
        otl = sbb("otl", [128, 512])
        oqb = sbb("oqb", [128, 512], BF16)
        xres = sbb("xres", [128, 8, 512])
        ares = sbb("ares", [128, 8, 512])
        outt = sbb("outt", [128, 8, 512])

        sub_ps2 = ExitStack()
        psD = [sub_ps2.enter_context(nc.psum_tensor(f"psDn{i}", [128, 512], F32)) for i in range(8)]

        qO_attn = p.cnt["qO"]
        for dev in (blk.sync, blk.scalar, blk.vector, blk.gpsimd, blk.tensor):
            @dev
            def _(e):
                e.wait_ge(p.sems["qO"], qO_attn)
                for qq in ("qC0", "qC1", "qS", "qP", "qO0", "qO1", "qT0", "qT1"):
                    e.wait_ge(p.sems[qq], p.cnt[qq])

        emit_dequant(p, blk, w_down, wdqd_scr, C, (wtile, wq2, babs, sblk, rblk), rnd_col, "qD")

        @blk.scalar
        def _(e):
            for qq in ("qP", "qP0", "qP1"):
                e.wait_ge(p.sems[qq], p.cnt[qq])
            for it in range(24):
                p.inc(e.dma_start(out=wdT[:, it, :], in_=wdqd_scr[0:C, 128 * it:128 * (it + 1)],
                                  transpose=True), "qW", 16)
        qW_d = p.cnt["qW"]

        chain_mark = {}

        def dn_s1(tq, it):
            bi = it % 2
            ql = "qL%d" % bi
            tt0 = tq * 512

            @blk.sync
            def _(e, tq=tq, it=it, bi=bi, ql=ql, tt0=tt0):
                if tq + it > 0:
                    p.wait(e, "sV")
                p.inc(e.dma_start(out=ui[bi][:], in_=upq[128 * it:128 * (it + 1), tt0:tt0 + 512]), ql, 16)
                p.inc(e.dma_start(out=gi[bi][:], in_=gq[128 * it:128 * (it + 1), tt0:tt0 + 512]), ql, 16)
            qld = p.cnt[ql]

            @blk.vector
            def _(e, bi=bi, qld=qld, ql=ql):
                e.wait_ge(p.sems[ql], qld)
                p.inc(e.tensor_scalar(out=gqf[bi][:], in0=gi[bi][:], scalar1=sgug[:, 5:6],
                                      scalar2=None, op0=ALU.mult), "sV")
            sV_g = p.cnt["sV"]

            @blk.scalar
            def _(e, bi=bi, it=it, sV_g=sV_g):
                p.wait(e, "sV", sV_g)
                p.wait(e, "sV", chain_mark.get(it - 2, 0))
                p.inc(e.activation(out=sigt[bi][:], in_=gqf[bi][:], func=AF.Sigmoid), "sA")
            return p.cnt["sA"]

        def dn_s2(tq, it, sA_s):
            bi = it % 2

            @blk.vector
            def _(e, bi=bi, sA_s=sA_s):
                p.wait(e, "sA", sA_s)
                p.inc(e.tensor_scalar(out=sigq[:], in0=sigt[bi][:], scalar1=65536.0, scalar2=RND,
                                      op0=ALU.mult, op1=ALU.add), "sV")
                p.wait(e, "sV")
                p.inc(e.tensor_scalar(out=sigq[:], in0=sigq[:], scalar1=RND + 65535.0,
                                      scalar2=RND, op0=ALU.min, op1=ALU.subtract), "sV")
                p.wait(e, "sV")
                p.inc(e.scalar_tensor_tensor(out=g2t[:], in0=sigq[:], scalar=1.0 / 65536.0,
                                             in1=gqf[bi][:], op0=ALU.mult, op1=ALU.mult), "sV")
                p.wait(e, "sV")
                p.inc(e.tensor_scalar(out=g2q[:], in0=g2t[:], scalar1=sg3[:, 2:3],
                                      scalar2=RND, op0=ALU.mult, op1=ALU.add), "sV")
                p.wait(e, "sV")
                p.inc(e.tensor_scalar(out=g2q[:], in0=g2q[:], scalar1=RND,
                                      scalar2=sg3[:, 1:2], op0=ALU.subtract, op1=ALU.mult), "sV")
                p.wait(e, "sV")
                p.inc(e.scalar_tensor_tensor(out=otl[:], in0=ui[bi][:], scalar=sgug[:, 1:2],
                                             in1=g2q[:], op0=ALU.mult, op1=ALU.mult), "sV")
                p.wait(e, "sV")
                p.inc(e.tensor_scalar(out=g2q[:], in0=otl[:], scalar1=sg4[:, 2:3],
                                      scalar2=RND, op0=ALU.mult, op1=ALU.add), "sV")
                p.wait(e, "sV")
                p.wait(e, "sT")
                p.inc(e.tensor_scalar(out=oqb[:], in0=g2q[:], scalar1=RND,
                                      scalar2=sg4[:, 1:2], op0=ALU.subtract, op1=ALU.mult), "sV")
            sV_oq = p.cnt["sV"]
            chain_mark[it if tq == 0 else it + 24 * tq] = sV_oq
            chain_mark[it - 2] = chain_mark.get(it - 2, 0)

            @blk.tensor
            def _(e, tq=tq, it=it, sV_oq=sV_oq):
                e.wait_ge(p.sems["qW"], qW_d)
                p.wait(e, "sV", sV_oq)
                if it == 0:
                    p.wait(e, "sA")
                for ct in range(8):
                    ii = e.matmul(psD[ct][:, :], wdT[:, it, 128 * ct:128 * ct + 128],
                                  oqb[:, :], start=(it == 0), stop=(it == 23))
                p.inc(ii, "sT")

        def dn_tail(tq):
            tt0 = tq * 512
            sT_dn = p.cnt["sT"]

            @blk.sync
            def _(e, tq=tq, tt0=tt0):
                for qq in ("qO", "qO0", "qO1"):
                    e.wait_ge(p.sems[qq], p.cnt[qq])
                p.inc(e.dma_start(out=xres[:], in_=fm3(x_fm)[:, :, tt0 + 1:tt0 + 513]), "qS", 16)
                p.inc(e.dma_start(out=ares[:], in_=fm3(attn_scr)[:, :, tt0:tt0 + 512]), "qS", 16)
            qS_r = p.cnt["qS"]

            @blk.vector
            def _(e, tq=tq, qS_r=qS_r, sT_dn=sT_dn):
                e.wait_ge(p.sems["qS"], qS_r)
                p.wait(e, "sT", sT_dn)
                p.inc(e.tensor_tensor(out=outt[:], in0=xres[:], in1=ares[:], op=ALU.add), "sV")
                p.wait(e, "sV")
                for ct in range(8):
                    ii = e.tensor_tensor(out=outt[:, ct, :], in0=outt[:, ct, :],
                                         in1=psD[ct][:, :], op=ALU.add)
                p.inc(ii, "sV")
            sV_out = p.cnt["sV"]

            @blk.scalar
            def _(e, tq=tq, sV_out=sV_out):
                p.wait(e, "sV", sV_out)
                p.inc(e.nop(), "sA")

            @blk.sync
            def _(e, tq=tq, tt0=tt0, sV_out=sV_out):
                p.wait(e, "sV", sV_out)
                e.wait_ge(p.sems["qO"], p.cnt["qO"])
                p.inc(e.dma_start(out=fm3(out_o)[:, :, tt0:tt0 + 512], in_=outt[:]), "qO", 16)
            return sV_out, qS_r

        items = [(tq, it) for tq in range(NQ) for it in range(24)]
        sA_marks = {}
        tails = []
        for k in range(len(items) + 1):
            if k < len(items):
                sA_marks[k] = dn_s1(*items[k])
            if k >= 1:
                tq, it = items[k - 1]
                dn_s2(tq, it, sA_marks[k - 1])
                if it == 23:
                    tails.append(dn_tail(tq))

        @blk.sync
        def _(e):
            for q in ("qS", "qG", "qG2", "qG3", "qO", "qO0", "qO1", "qP", "qP0", "qP1",
                      "qW", "qD", "qL0", "qL1", "qC0", "qC1", "qT0", "qT1"):
                e.wait_ge(p.sems[q], p.cnt[q])
        sub_ps2.close()
        subB.close()
    return p


from concourse.bass_utils import run_bass_kernel_spmd

TCK = 2048
TSL1 = 256
TSL2 = 512
_PROGS = {}


def _progs():
    if not _PROGS:
        _PROGS["L1"] = build_L1(TC=TCK, TSL=TSL1)
        _PROGS["L2"] = build_L2(TC=TCK, TSL=TSL2)
        _PROGS["L3"] = build_L3(TC=TCK, TSL=TSL2)
        _PROGS["L4"] = build_L4(TC=TCK)
        _PROGS["L5"] = build_L5(TC=TCK)
    return _PROGS


def kernel(L0_attn_x, L0_attn_miu_x, L0_attn_lambda_, L0_attn_A_lora, L0_attn_B_lora,
           L0_attn_td_miu, L0_attn_td_A, L0_attn_td_B, L0_attn_u, L0_attn_Wr,
           L0_attn_Wk, L0_attn_Wv, L0_attn_Wg, L0_attn_Wo, L0_attn_gn_gamma,
           L0_attn_gn_beta, L0_ffn_w_gate, L0_ffn_w_up, L0_ffn_w_down):
    x = np.asarray(L0_attn_x, np.float32)
    B, T, _ = x.shape
    NC = 8
    cores = list(range(NC))
    TTP = ceil_div(TCK + 1, 128) * 128

    lam = np.asarray(L0_attn_lambda_, np.float32).reshape(5, C)
    lam_col = np.zeros((128, 5 * 8), np.float32)
    for f in range(5):
        for t in range(8):
            lam_col[:, f * 8 + t] = lam[f][128 * t:128 * (t + 1)]
    miu_x = np.asarray(L0_attn_miu_x, np.float32).reshape(C)
    mx_col = np.ascontiguousarray(miu_x.reshape(8, 128).T)
    tdm = np.asarray(L0_attn_td_miu, np.float32).reshape(C)
    tdm_col = np.ascontiguousarray(tdm.reshape(8, 128).T)
    u = np.asarray(L0_attn_u, np.float32).reshape(16, 64)
    u16 = np.zeros((128, 8 * 16), np.float32)
    for h in range(16):
        u16[64 * (h % 2):64 * (h % 2) + 64, (h // 2) * 16 + h] = u[h]
    id128 = np.eye(128, dtype=np.float32)
    id2 = np.concatenate([np.eye(64), np.eye(64)], 0).astype(np.float32)
    tri = np.triu(np.ones((128, 128), np.float32), 1)
    WkT = np.ascontiguousarray(np.asarray(L0_attn_Wk, np.float32).T)
    WvT = np.ascontiguousarray(np.asarray(L0_attn_Wv, np.float32).T)
    WrT = np.ascontiguousarray(np.asarray(L0_attn_Wr, np.float32).T)
    WgT = np.ascontiguousarray(np.asarray(L0_attn_Wg, np.float32).T)
    WoT = np.ascontiguousarray(np.asarray(L0_attn_Wo, np.float32).T)
    A_lora = np.ascontiguousarray(np.asarray(L0_attn_A_lora, np.float32))
    B_lora = np.ascontiguousarray(np.asarray(L0_attn_B_lora, np.float32))
    td_A = np.ascontiguousarray(np.asarray(L0_attn_td_A, np.float32))
    td_B = np.ascontiguousarray(np.asarray(L0_attn_td_B, np.float32))
    w_up = np.ascontiguousarray(np.asarray(L0_ffn_w_up, np.float32))
    w_gate = np.ascontiguousarray(np.asarray(L0_ffn_w_gate, np.float32))
    w_down = np.ascontiguousarray(np.asarray(L0_ffn_w_down, np.float32))
    gam_row = np.ascontiguousarray(np.asarray(L0_attn_gn_gamma, np.float32).reshape(1, C))
    bet_row = np.ascontiguousarray(np.asarray(L0_attn_gn_beta, np.float32).reshape(1, C))

    xrows_l, xfm_l, xtm_l = [], [], []
    for c in cores:
        b, hf = c // 2, c % 2
        xr = np.zeros((TCK + 1, C), np.float32)
        xr[1:] = x[b, TCK * hf:TCK * (hf + 1)]
        if hf == 1:
            xr[0] = x[b, TCK - 1]
        xrows_l.append(xr)
        xfm_l.append(np.ascontiguousarray(xr.T))
        xtmp = np.zeros((TTP, C), np.float32)
        xtmp[:TCK + 1] = xr
        xtm_l.append(xtmp)

    pr = _progs()

    im1 = [{"x_tm": xtm_l[c], "x_fm": xfm_l[c], "WkT": WkT, "WvT": WvT, "WrT": WrT,
            "A_lora": A_lora, "B_lora": B_lora, "td_A": td_A, "td_B": td_B,
            "lam_col": lam_col, "mx_col": mx_col, "tdmiu_col": tdm_col,
            "u16": u16, "ident": id128, "ident2": id2} for c in cores]
    r1 = run_bass_kernel_spmd(pr["L1"].nc, im1, core_ids=cores).results

    pxn_row = np.concatenate([r1[c]["pxn"][:, 0] for c in cores]).reshape(1, 1024)
    pxn_row = np.ascontiguousarray(pxn_row)

    im2 = [{"x_fm": xfm_l[c], "murstd": r1[c]["murstd"], "pxn_row": pxn_row,
            "w_up": w_up, "w_gate": w_gate} for c in cores]
    r2 = run_bass_kernel_spmd(pr["L2"].nc, im2, core_ids=cores).results

    pxug = np.zeros((1, 2048), np.float32)
    pxug[0, 0:1024] = np.concatenate([r2[c]["pxu"][:, 0] for c in cores])
    pxug[0, 1024:2048] = np.concatenate([r2[c]["pxg"][:, 0] for c in cores])

    im3 = [{"x_fm": xfm_l[c], "murstd": r1[c]["murstd"], "pxn_row": pxn_row,
            "pxug_row": pxug, "wdq_up": r2[c]["wdq_up"], "wdq_gate": r2[c]["wdq_gate"]}
           for c in cores]
    r3 = run_bass_kernel_spmd(pr["L3"].nc, im3, core_ids=cores).results

    pxg2 = np.zeros((1, 1024), np.float32)
    pxg2[0, 0:1024] = np.concatenate([r3[c]["pxg2"][:, 0] for c in cores])

    im4 = [{"upq": r3[c]["upq"], "gq": r3[c]["gq"], "pxug_row": pxug,
            "pxg2_row": pxg2} for c in cores]
    r4 = run_bass_kernel_spmd(pr["L4"].nc, im4, core_ids=cores).results

    pxo = np.zeros((1, 1024), np.float32)
    pxo[0, 0:1024] = np.concatenate([r4[c]["pxo"][:, 0] for c in cores])

    zstate = np.zeros((128, 512), np.float32)
    im5 = []
    for c in cores:
        b, hf = c // 2, c % 2
        utotP = np.ascontiguousarray(r1[c - 1]["utot"]) if hf == 1 else zstate
        im5.append({"x_fm": xfm_l[c], "qt": r1[c]["qt"], "kt": r1[c]["kt"],
                    "vq": r1[c]["vq"], "xg": r1[c]["xg"],
                    "b_rowT": np.ascontiguousarray(np.asarray(r1[c]["b_row"]).T),
                    "utotP": utotP, "traj": r1[c]["traj"], "dpre": r1[c]["dpre"],
                    "WgT": WgT, "WoT": WoT, "w_down": w_down,
                    "upq": r3[c]["upq"], "gq": r3[c]["gq"], "pxug_row": pxug,
                    "pxg2_row": pxg2, "pxo_row": pxo, "gam_row": gam_row,
                    "bet_row": bet_row, "tri": tri, "ident": id128})
    r5 = run_bass_kernel_spmd(pr["L5"].nc, im5, core_ids=cores).results

    out = np.empty((B, T, C), np.float32)
    for c in cores:
        b, hf = c // 2, c % 2
        out[b, TCK * hf:TCK * (hf + 1)] = np.asarray(r5[c]["out_fm"], np.float32).T
    return out


# version 15 (final)
# speedup vs baseline: 1.1036x; 1.0143x over previous
import numpy as np
from contextlib import ExitStack
import concourse.bass as bass
import concourse.mybir as mybir

F32 = mybir.dt.float32
BF16 = mybir.dt.bfloat16
I16 = mybir.dt.int16
AF = mybir.ActivationFunctionType
ALU = mybir.AluOpType
AX = mybir.AxisListType

C = 1024
H = 16
HS = 64
L = 128
FFI = 3072
RND = 12582912.0
QMAX = 32767.0
WQMAX = 7.0
BLK = 32


def ceil_div(a, b):
    return (a + b - 1) // b


class Prog:
    def __init__(self, name="prog"):
        self.nc = bass.Bass("TRN2", target_bir_lowering=False)
        self.ctx = ExitStack()
        self.sems = {}
        self.cnt = {}

    def sb(self, name, shape, dtype=F32):
        return self.ctx.enter_context(self.nc.sbuf_tensor(name, list(shape), dtype))

    def ps(self, name, shape, dtype=F32):
        return self.ctx.enter_context(self.nc.psum_tensor(name, list(shape), dtype))

    def sem(self, name):
        if name not in self.sems:
            self.sems[name] = self.ctx.enter_context(self.nc.semaphore(name))
            self.cnt[name] = 0
        return self.sems[name]

    def inc(self, instr, name, n=1):
        instr.then_inc(self.sems[name], n)
        self.cnt[name] += n
        return self.cnt[name]

    def wait(self, eng, name, v=None):
        eng.wait_ge(self.sems[name], self.cnt[name] if v is None else v)

    def constcol(self, value):
        if not hasattr(self, "consts"):
            self.consts = {}
        if value not in self.consts:
            t = self.sb(f"cc_{len(self.consts)}", [128, 1], F32)
            self.consts[value] = t
        return self.consts[value][:, 0:1]

    def emit_const_memsets(self, eng, semname):
        if not hasattr(self, "consts"):
            self.consts = {}
        ii = None
        for v, t in self.consts.items():
            ii = eng.memset(t[:], float(v))
        if ii is not None:
            self.inc(ii, semname)

    def din(self, name, shape, dtype=F32):
        return self.nc.dram_tensor(name, list(shape), dtype, kind="ExternalInput").ap()

    def dout(self, name, shape, dtype=F32):
        return self.nc.dram_tensor(name, list(shape), dtype, kind="ExternalOutput").ap()


def fm3(ap, p=128):
    return ap.rearrange("(t p) x -> p t x", p=p)


def build_L1(TC=2048, TSL=512):
    NCH = TC // L
    NSL = TC // TSL
    CPS = TSL // L
    TTP = ceil_div(TC + 1, 128) * 128
    NTT = TTP // 128
    p = Prog("L1")
    nc = p.nc

    x_tm = p.din("x_tm", [TTP, C])
    x_fm = p.din("x_fm", [C, TC + 1])
    WkT = p.din("WkT", [C, C])
    WvT = p.din("WvT", [C, C])
    WrT = p.din("WrT", [C, C])
    A_lora = p.din("A_lora", [C, 160])
    B_lora = p.din("B_lora", [5, 32, C])
    td_A = p.din("td_A", [C, 64])
    td_B = p.din("td_B", [64, C])
    lam_col = p.din("lam_col", [128, 5 * 8])
    mx_col = p.din("mx_col", [128, 8])
    tdmiu_col = p.din("tdmiu_col", [128, 8])
    u16 = p.din("u16", [128, 8 * 16])
    ident = p.din("ident", [128, 128])
    ident2 = p.din("ident2", [128, 64])

    murstd = p.dout("murstd", [2, TTP])
    pxn_o = p.dout("pxn", [128, 1])
    qt_o = p.dout("qt", [C, TC], BF16)
    kt_o = p.dout("kt", [C, TC], BF16)
    vq_o = p.dout("vq", [TC, C], BF16)
    xg_o = p.dout("xg", [C, TC], BF16)
    b_o = p.dout("b_row", [16, TC])
    utot_o = p.dout("utot", [128, 512])
    traj_o = p.dout("traj", [128, NCH * 512])
    dpre_o = p.dout("dpre", [128, NCH * 8])

    wk = p.sb("wk", [128, 8, C], BF16)
    wv = p.sb("wv", [128, 8, C], BF16)
    wr = p.sb("wr", [128, 8, C], BF16)
    al = p.sb("al", [128, 8, 160], BF16)
    bl = p.sb("bl", [128, 5, C], BF16)
    tda = p.sb("tda", [128, 8, 64], BF16)
    tdb = p.sb("tdb", [64, C], BF16)
    lam = p.sb("lam", [128, 5, 8])
    mxc = p.sb("mxc", [128, 8])
    tdm = p.sb("tdm", [128, 8])
    u16s = p.sb("u16s", [128, 8, 16], BF16)
    idn = p.sb("idn", [128, 128])
    idnb = p.sb("idnb", [128, 64], BF16)

    xtm = [p.sb(f"xtm{i}", [128, C]) for i in range(2)]
    trash = p.sb("trash", [128, C], BF16)
    muA = p.sb("muA", [128, NTT])
    rsA = p.sb("rsA", [128, NTT])
    mrP = p.sb("mrP", [128, 2 * NTT])

    xfs = p.sb("xfs", [128, 8, TSL + 1])
    mue = p.sb("mue", [128, TSL + 1])
    rse = p.sb("rse", [128, TSL + 1])
    xns = p.sb("xns", [128, 8, TSL + 1])
    dxa = p.sb("dxa", [128, 8, TSL])
    xm = p.sb("xm", [128, 8, TSL], BF16)
    lhA = p.sb("lhA", [96, TSL], BF16)
    lh2 = p.sb("lh2", [64, TSL], BF16)
    tmpm = p.sb("tmpm", [128, 8, TSL])
    xxb = [p.sb(f"xx{i}", [128, 8, TSL], BF16) for i in range(2)]
    th = p.sb("th", [64, TSL], BF16)
    expw = p.sb("expw", [128, 8, TSL])
    spr = p.sb("spr", [128, 8, TSL])
    epos = p.sb("epos", [128, 8, TSL])
    dtot = p.sb("dtot", [128, 8, CPS])
    rf = p.sb("rf", [128, 8, TSL])
    ktb = p.sb("ktb", [128, 8, TSL], BF16)
    khb = p.sb("khb", [128, 8, TSL], BF16)
    qtb = p.sb("qtb", [128, 8, TSL], BF16)
    vtb = p.sb("vtb", [128, CPS, C], BF16)
    khT = p.sb("khT", [128, 8, 128], BF16)
    bsl = p.sb("bsl", [16, TSL])
    sown = [p.sb(f"sown{i}", [128, 8, 64]) for i in range(2)]
    dprer = [p.sb(f"dprer{i}", [128, 8]) for i in range(2)]
    pxn_t = p.sb("pxn_t", [128, NSL + 1])
    zer = p.sb("zer", [128, L])

    psA = p.ps("psA", [128, 512])
    psX = p.ps("psX", [64, 512])
    PSW = max(TSL, 512)
    psP = [p.ps(f"psP{i}", [128, PSW]) for i in range(2)]
    psU = p.ps("psU", [128, 512])

    eps_col = p.constcol(1e-5)

    for s in ("qS", "qA", "qP", "qO", "qB0", "qB1", "qT0", "qT1", "sV", "sA", "sP", "sT"):
        p.sem(s)

    with nc.Block() as blk:
        @blk.sync
        def _(e):
            p.inc(e.dma_start(out=lam[:], in_=lam_col.rearrange("p (f t) -> p f t", f=5)), "qS", 16)
            p.inc(e.dma_start(out=mxc[:], in_=mx_col[:, :]), "qS", 16)
            p.inc(e.dma_start(out=tdm[:], in_=tdmiu_col[:, :]), "qS", 16)
            p.inc(e.dma_start(out=idn[:], in_=ident[:, :]), "qS", 16)
        qS_const = p.cnt["qS"]

        @blk.gpsimd
        def _(e):
            p.inc(e.dma_start(out=wk[:], in_=fm3(WkT)), "qP", 16)
            p.inc(e.dma_start(out=wv[:], in_=fm3(WvT)), "qP", 16)
            p.inc(e.dma_start(out=wr[:], in_=fm3(WrT)), "qP", 16)
            p.inc(e.dma_start(out=al[:], in_=fm3(A_lora)), "qP", 16)
            p.inc(e.dma_start(out=bl[0:32], in_=B_lora.rearrange("f i c -> i f c")), "qP", 16)
            p.inc(e.dma_start(out=bl[32:64], in_=B_lora.rearrange("f i c -> i f c")), "qP", 16)
            p.inc(e.dma_start(out=bl[64:96], in_=B_lora.rearrange("f i c -> i f c")), "qP", 16)
            p.inc(e.dma_start(out=tda[:], in_=fm3(td_A)), "qP", 16)
            p.inc(e.dma_start(out=tdb[:], in_=td_B[:, :]), "qP", 16)
            p.inc(e.dma_start(out=u16s[:], in_=u16.rearrange("p (t h) -> p t h", t=8)), "qP", 16)
            p.inc(e.dma_start(out=idnb[:], in_=ident2[:, :]), "qP", 16)
        qP_const = p.cnt["qP"]

        @blk.vector
        def _(e):
            e.memset(zer[:], 0.0)
            e.memset(sown[0][:], 0.0)
            e.memset(dprer[0][:], 1.0)
            p.emit_const_memsets(e, "sV")
            p.inc(e.memset(pxn_t[:], 0.0), "sV")
        sV_init = p.cnt["sV"]

        sV_b0 = p.cnt["sV"]
        sA_b0 = p.cnt["sA"]

        @blk.scalar
        def _(e):
            for j in range(NTT):
                if j >= 2:
                    p.wait(e, "sV", sV_b0 + (j - 1))
                    p.wait(e, "sA", sA_b0 + (j - 1))
                p.inc(e.dma_start(out=xtm[j % 2][:], in_=x_tm[128 * j:128 * (j + 1), :]), "qA", 16)
                e.wait_ge(p.sems["qA"], 16 * (j + 1))
                p.inc(e.activation(out=trash[:], in_=xtm[j % 2][:], func=AF.Square,
                                   accum_out=rsA[:, j:j + 1]), "sA")
        sA_sq = p.cnt["sA"]

        @blk.vector
        def _(e):
            for j in range(NTT):
                e.wait_ge(p.sems["qA"], 16 * (j + 1))
                p.inc(e.tensor_reduce(out=muA[:, j:j + 1], in_=xtm[j % 2][:], axis=AX.X,
                                      op=ALU.add), "sV")
        sV_sums = p.cnt["sV"]

        sc1 = p.sb("sc1", [128, NTT])
        sc2 = p.sb("sc2", [128, NTT])

        @blk.vector
        def _(e):
            p.wait(e, "sA", sA_sq)
            p.wait(e, "sV", sV_sums)
            e.tensor_scalar(out=mrP[:, 0:NTT], in0=muA[:], scalar1=1.0 / 1024.0, scalar2=None,
                            op0=ALU.mult)
            p.inc(e.tensor_scalar(out=sc1[:], in0=rsA[:], scalar1=1.0 / 1024.0, scalar2=None,
                                  op0=ALU.mult), "sV")
            p.wait(e, "sV")
            p.inc(e.tensor_tensor(out=sc2[:], in0=mrP[:, 0:NTT], in1=mrP[:, 0:NTT], op=ALU.mult), "sV")
            p.wait(e, "sV")
            p.inc(e.tensor_tensor(out=rsA[:], in0=sc1[:], in1=sc2[:], op=ALU.subtract), "sV")
        sV_var = p.cnt["sV"]

        @blk.scalar
        def _(e):
            p.wait(e, "sV", sV_var)
            p.inc(e.activation(out=sc1[:], in_=rsA[:], func=AF.Sqrt, bias=eps_col, scale=1.0), "sA")
        sA_sqrt = p.cnt["sA"]

        @blk.vector
        def _(e):
            p.wait(e, "sA", sA_sqrt)
            p.inc(e.reciprocal(out=mrP[:, NTT:2 * NTT], in_=sc1[:]), "sV")
        sV_mr = p.cnt["sV"]

        @blk.sync
        def _(e):
            p.wait(e, "sV", sV_mr)
            e.wait_ge(p.sems["qS"], qS_const)
            for j in range(NTT):
                p.inc(e.dma_start(out=murstd[0:1, 128 * j:128 * (j + 1)], in_=mrP[:, j:j + 1]), "qS", 16)
                p.inc(e.dma_start(out=murstd[1:2, 128 * j:128 * (j + 1)], in_=mrP[:, NTT + j:NTT + j + 1]), "qS", 16)
        qS_mr = p.cnt["qS"]

        for s in range(NSL):
            c0 = s * TSL
            sl1 = slice(c0, c0 + TSL + 1)
            slt = slice(c0, c0 + TSL)

            @blk.sync
            def _(e):
                if s > 0:
                    p.wait(e, "sV")
                    p.wait(e, "sA")
                    p.wait(e, "sT")
                    p.wait(e, "sP")
                e.wait_ge(p.sems["qS"], qS_mr)
                p.inc(e.dma_start(out=xfs[:], in_=fm3(x_fm)[:, :, sl1]), "qS", 16)
                p.inc(e.dma_start(out=mue[:], in_=murstd[0:1, c0:c0 + TSL + 1].broadcast_to([128, TSL + 1])), "qS", 16)
                p.inc(e.dma_start(out=rse[:], in_=murstd[1:2, c0:c0 + TSL + 1].broadcast_to([128, TSL + 1])), "qS", 16)
            qS_ld = p.cnt["qS"]

            @blk.vector
            def _(e):
                e.wait_ge(p.sems["qS"], qS_ld)
                p.inc(e.tensor_tensor(out=xns[:], in0=xfs[:],
                                in1=mue.rearrange("p (o c) -> p o c", o=1).broadcast_to([128, 8, TSL + 1]),
                                op=ALU.subtract), "sV")
                p.wait(e, "sV")
                p.inc(e.tensor_tensor(out=xns[:], in0=xns[:],
                                in1=rse.rearrange("p (o c) -> p o c", o=1).broadcast_to([128, 8, TSL + 1]),
                                op=ALU.mult), "sV")
                p.wait(e, "sV")
                e.tensor_reduce(out=pxn_t[:, s:s + 1], in_=xns[:, :, 1:TSL + 1], axis=AX.XY,
                                op=ALU.max, apply_absolute_value=True)
                p.inc(e.tensor_tensor(out=dxa[:], in0=xns[:, :, 0:TSL], in1=xns[:, :, 1:TSL + 1],
                                op=ALU.subtract), "sV")
                p.wait(e, "sV")
                for t in range(8):
                    ii = e.scalar_tensor_tensor(out=xm[:, t, :], in0=dxa[:, t, :],
                                                scalar=mxc[:, t:t + 1], in1=xns[:, t, 1:TSL + 1],
                                                op0=ALU.mult, op1=ALU.add)
                p.inc(ii, "sV")
            sV_xm = p.cnt["sV"]

            @blk.tensor
            def _(e):
                p.wait(e, "sV", sV_xm)
                p.wait(e, "sV")
                e.wait_ge(p.sems["qP"], qP_const)
                for kc in range(8):
                    e.matmul(psA[0:96, 0:TSL], al[:, kc, 0:96], xm[:, kc, :],
                             start=(kc == 0), stop=(kc == 7))
                for kc in range(8):
                    ii = e.matmul(psX[0:64, 0:TSL], al[:, kc, 96:160], xm[:, kc, :],
                                  start=(kc == 0), stop=(kc == 7))
                p.inc(ii, "sT")
            sT_lora = p.cnt["sT"]

            @blk.scalar
            def _(e):
                p.wait(e, "sT", sT_lora)
                e.activation(out=lhA[0:96, :], in_=psA[0:96, 0:TSL], func=AF.Tanh)
                p.inc(e.activation(out=lh2[:], in_=psX[0:64, 0:TSL], func=AF.Tanh), "sA")
            sA_lh = p.cnt["sA"]

            def emit_xx(fi, slot, eng_name, prev_tmpm=None):
                ev = {-2: p.cnt["sV"], -1: p.cnt["sV"]}
                if fi < 3:
                    rhs_ap = lhA[32 * fi:32 * fi + 32, :]
                    lbase = 32 * fi
                else:
                    rhs_ap = lh2[32 * (fi - 3):32 * (fi - 3) + 32, :]
                    lbase = 32 * (fi - 3)
                for tt in range(8):
                    pp = psP[tt % 2]

                    @blk.tensor
                    def _(e, fi=fi, tt=tt, pp=pp, rhs_ap=rhs_ap, lbase=lbase):
                        p.wait(e, "sA", sA_lh)
                        p.wait(e, "sV", ev[tt - 2])
                        if tt < 2:
                            p.wait(e, "sV")
                            p.wait(e, "sA")
                        p.inc(e.matmul(pp[:, 0:TSL], bl[lbase:lbase + 32, fi, 128 * tt:128 * tt + 128],
                                       rhs_ap, start=True, stop=True), "sT")

                    @blk.vector
                    def _(e, fi=fi, tt=tt, pp=pp, prev_tmpm=prev_tmpm):
                        p.wait(e, "sT")
                        if tt == 0 and prev_tmpm is not None:
                            p.wait(e, prev_tmpm[0], prev_tmpm[1])
                        p.inc(e.tensor_scalar(out=tmpm[:, tt, :], in0=pp[:, 0:TSL],
                                              scalar1=lam[:, fi, tt:tt + 1], scalar2=None,
                                              op0=ALU.add), "sV")
                    ev[tt] = p.cnt["sV"]
                dev = blk.vector if eng_name == "sV" else blk.gpsimd

                mid = {}

                @dev
                def _(e, slot=slot, eng_name=eng_name, ev7=ev[7]):
                    p.wait(e, "sV", ev7)
                    p.wait(e, "sT")
                    e.wait_ge(p.sems["qO"], p.cnt["qO"])
                    p.inc(e.tensor_tensor(out=tmpm[:], in0=tmpm[:], in1=dxa[:], op=ALU.mult), eng_name)
                    mid["v"] = p.cnt[eng_name]
                    p.wait(e, eng_name)
                    p.inc(e.tensor_tensor(out=xxb[slot][:], in0=tmpm[:], in1=xns[:, :, 1:TSL + 1],
                                          op=ALU.add), eng_name)
                return (eng_name, p.cnt[eng_name], mid["v"])

            dep_w = emit_xx(0, 0, "sV")

            @blk.tensor
            def _(e):
                p.wait(e, dep_w[0], dep_w[1])
                p.wait(e, "sA")
                for kc in range(8):
                    ii = e.matmul(psA[0:64, 0:TSL], tda[:, kc, :], xxb[0][:, kc, :],
                                  start=(kc == 0), stop=(kc == 7))
                p.inc(ii, "sT")

            @blk.scalar
            def _(e):
                p.wait(e, "sT")
                p.inc(e.activation(out=th[:], in_=psA[0:64, 0:TSL], func=AF.Tanh), "sA")

            evw = {-2: p.cnt["sA"], -1: p.cnt["sA"]}
            for tt in range(8):
                pp = psP[tt % 2]

                @blk.tensor
                def _(e, tt=tt, pp=pp):
                    p.wait(e, "sA", evw[tt - 2])
                    if tt == 0:
                        p.wait(e, "sA")
                        p.wait(e, "sV")
                    p.inc(e.matmul(pp[:, 0:TSL], tdb[:, 128 * tt:128 * tt + 128], th[:],
                                   start=True, stop=True), "sT")

                @blk.scalar
                def _(e, tt=tt, pp=pp):
                    p.wait(e, "sT")
                    p.inc(e.activation(out=expw[:, tt, :], in_=pp[:, 0:TSL], func=AF.Exp,
                                       bias=tdm[:, tt:tt + 1], scale=1.0), "sA")
                evw[tt] = p.cnt["sA"]
            sA_expw = p.cnt["sA"]

            @blk.vector
            def _(e):
                p.wait(e, "sA", sA_expw)
                for tt in range(8):
                    for cc in range(CPS):
                        ii = e.tensor_tensor_scan(out=spr[:, tt, L * cc:L * cc + L],
                                                  data0=expw[:, tt, L * cc:L * cc + L],
                                                  data1=zer[:, 0:L], initial=0.0,
                                                  op0=ALU.add, op1=ALU.add)
                p.inc(ii, "sV")
            sV_scan = p.cnt["sV"]

            @blk.scalar
            def _(e):
                p.wait(e, "sV", sV_scan)
                for tt in range(8):
                    e.activation(out=epos[:, tt, 1:TSL], in_=spr[:, tt, 0:TSL - 1], func=AF.Exp, scale=-1.0)
                    e.activation(out=dtot[:, tt, :], in_=spr[:, tt, L - 1:TSL:L], func=AF.Exp, scale=-1.0)
                    p.inc(e.activation(out=expw[:, tt, :], in_=spr[:, tt, :], func=AF.Exp, scale=1.0), "sA")
            sA_dec = p.cnt["sA"]
            eneg = expw

            @blk.vector
            def _(e):
                p.wait(e, "sA", sA_dec)
                for tt in range(8):
                    e.memset(epos[:, tt, 0:TSL:L], 1.0)
                p.inc(e.tensor_tensor(
                    out=spr.rearrange("p t (c l) -> p t c l", l=L),
                    in0=eneg.rearrange("p t (c l) -> p t c l", l=L),
                    in1=dtot.rearrange("p t (c o) -> p t c o", o=1).broadcast_to([128, 8, CPS, L]),
                    op=ALU.mult), "sV")
            sV_fk = p.cnt["sV"]
            fk = spr

            dep_g = emit_xx(4, 1, "sP", prev_tmpm=(dep_w[0], dep_w[1]))

            @blk.sync
            def _(e):
                p.wait(e, dep_g[0], dep_g[1])
                e.wait_ge(p.sems["qO"], p.cnt["qO"])
                p.inc(e.dma_start(out=fm3(xg_o)[:, :, slt], in_=xxb[1][:]), "qO", 16)

            dep_r = emit_xx(3, 0, "sP", prev_tmpm=(dep_g[0], dep_g[1]))
            evr = {-2: (p.cnt["sA"], p.cnt["sV"]), -1: (p.cnt["sA"], p.cnt["sV"])}
            for tt in range(8):
                pp = psP[tt % 2]

                @blk.tensor
                def _(e, tt=tt, pp=pp):
                    p.wait(e, dep_r[0], dep_r[1])
                    p.wait(e, "sA", evr[tt - 2][0])
                    p.wait(e, "sV", evr[tt - 2][1])
                    if tt == 0:
                        p.wait(e, "sA")
                        p.wait(e, "sV")
                    for kc in range(8):
                        ii = e.matmul(pp[:, 0:TSL], wr[:, kc, 128 * tt:128 * tt + 128],
                                      xxb[0][:, kc, :], start=(kc == 0), stop=(kc == 7))
                    p.inc(ii, "sT")

                @blk.scalar
                def _(e, tt=tt, pp=pp):
                    p.wait(e, "sT")
                    p.inc(e.activation(out=rf[:, tt, :], in_=pp[:, 0:TSL], func=AF.Identity), "sA")

                @blk.vector
                def _(e, tt=tt, pp=pp):
                    p.wait(e, "sT")
                    if tt == 0:
                        p.wait(e, "sV", sV_fk)
                    p.inc(e.scalar_tensor_tensor(out=qtb[:, tt, :], in0=pp[:, 0:TSL], scalar=1.0,
                                                 in1=epos[:, tt, :], op0=ALU.mult, op1=ALU.mult), "sV")
                evr[tt] = (p.cnt["sA"], p.cnt["sV"])
            sA_r = p.cnt["sA"]
            sV_q = p.cnt["sV"]

            dep_k = emit_xx(1, 1, "sP", prev_tmpm=(dep_r[0], dep_r[1]))
            rkb = xm
            evk = {-2: p.cnt["sV"], -1: p.cnt["sV"]}
            for tt in range(8):
                pp = psP[tt % 2]

                @blk.tensor
                def _(e, tt=tt, pp=pp):
                    p.wait(e, dep_k[0], dep_k[1])
                    p.wait(e, "sV", evk[tt - 2])
                    if tt == 0:
                        p.wait(e, "sA")
                        p.wait(e, "sV")
                    for kc in range(8):
                        ii = e.matmul(pp[:, 0:TSL], wk[:, kc, 128 * tt:128 * tt + 128],
                                      xxb[1][:, kc, :], start=(kc == 0), stop=(kc == 7))
                    p.inc(ii, "sT")

                @blk.vector
                def _(e, tt=tt, pp=pp):
                    p.wait(e, "sT")
                    if tt == 0:
                        p.wait(e, "sA", sA_r)
                    e.scalar_tensor_tensor(out=ktb[:, tt, :], in0=pp[:, 0:TSL], scalar=1.0,
                                           in1=eneg[:, tt, :], op0=ALU.mult, op1=ALU.mult)
                    e.scalar_tensor_tensor(out=khb[:, tt, :], in0=pp[:, 0:TSL], scalar=1.0,
                                           in1=fk[:, tt, :], op0=ALU.mult, op1=ALU.mult)
                    p.inc(e.scalar_tensor_tensor(out=rkb[:, tt, :], in0=pp[:, 0:TSL], scalar=1.0,
                                                 in1=rf[:, tt, :], op0=ALU.mult, op1=ALU.mult), "sV")
                evk[tt] = p.cnt["sV"]
            sV_k = p.cnt["sV"]

            @blk.sync
            def _(e):
                p.wait(e, "sV", sV_k)
                e.wait_ge(p.sems["qO"], p.cnt["qO"])
                p.inc(e.dma_start(out=fm3(qt_o)[:, :, slt], in_=qtb[:]), "qO", 16)
                p.inc(e.dma_start(out=fm3(kt_o)[:, :, slt], in_=ktb[:]), "qO", 16)

            dep_v = emit_xx(2, 0, "sP", prev_tmpm=(dep_k[0], dep_k[1]))
            for st in range(CPS):
                for hh in range(2):
                    pp = psP[(st * 2 + hh) % 2]

                    @blk.tensor
                    def _(e, st=st, hh=hh, pp=pp):
                        p.wait(e, dep_v[0], dep_v[1])
                        p.wait(e, "sA")
                        p.wait(e, "sV")
                        for kc in range(8):
                            ii = e.matmul(pp[:, 0:512], xxb[0][:, kc, L * st:L * st + L],
                                          wv[:, kc, 512 * hh:512 * hh + 512],
                                          start=(kc == 0), stop=(kc == 7))
                        p.inc(ii, "sT")

                    @blk.vector
                    def _(e, st=st, hh=hh, pp=pp):
                        p.wait(e, "sT")
                        p.inc(e.tensor_copy(out=vtb[:, st, 512 * hh:512 * hh + 512],
                                            in_=pp[:, 0:512]), "sV")
            sV_v = p.cnt["sV"]

            @blk.sync
            def _(e):
                p.wait(e, "sV", sV_v)
                e.wait_ge(p.sems["qO"], p.cnt["qO"])
                for st in range(CPS):
                    p.inc(e.dma_start(out=vq_o[c0 + L * st:c0 + L * st + L, :], in_=vtb[:, st, :]), "qO", 16)

            @blk.tensor
            def _(e):
                p.wait(e, "sV", sV_k)
                for tt in range(8):
                    ii = e.matmul(psX[0:16, 0:TSL], u16s[:, tt, :], rkb[:, tt, :],
                                  start=(tt == 0), stop=(tt == 7))
                p.inc(ii, "sT")

            @blk.scalar
            def _(e):
                p.wait(e, "sT")
                p.inc(e.activation(out=bsl[:], in_=psX[0:16, 0:TSL], func=AF.Identity), "sA")
            sA_b = p.cnt["sA"]

            @blk.sync
            def _(e):
                p.wait(e, "sA", sA_b)
                e.wait_ge(p.sems["qO"], p.cnt["qO"])
                p.inc(e.dma_start(out=b_o[:, slt], in_=bsl[:]), "qO", 16)

            for cc in range(CPS):
                gc = s * CPS + cc
                cur = gc % 2
                nxt = (gc + 1) % 2

                qtp = "qT%d" % (cc % 2)

                @blk.scalar
                def _(e, cc=cc, qtp=qtp):
                    p.wait(e, "sV", sV_k)
                    p.wait(e, "sT")
                    e.wait_ge(p.sems[qtp], p.cnt[qtp])
                    for g in range(8):
                        p.inc(e.dma_start(out=khT[:, g, :],
                                          in_=khb[:, g, L * cc:L * cc + L],
                                          transpose=True), qtp, 16)
                qT_k = p.cnt[qtp]

                @blk.tensor
                def _(e, cc=cc, qT_k=qT_k, qtp=qtp):
                    e.wait_ge(p.sems[qtp], qT_k)
                    p.wait(e, "sV")
                    for h in range(16):
                        ii = e.matmul(psU[64 * (h % 2):64 * (h % 2) + 64, 64 * (h // 2):64 * (h // 2) + 64],
                                      khT[:, h // 2, 64 * (h % 2):64 * (h % 2) + 64],
                                      vtb[:, cc, 64 * h:64 * h + 64], start=True, stop=True)
                    p.inc(ii, "sT")
                sT_U = p.cnt["sT"]

                @blk.sync
                def _(e, gc=gc, cur=cur):
                    p.wait(e, "sV")
                    e.wait_ge(p.sems["qO"], p.cnt["qO"])
                    p.inc(e.dma_start(out=traj_o[:, 512 * gc:512 * gc + 512],
                                      in_=sown[cur].rearrange("p g j -> p (g j)")), "qO", 16)
                    p.inc(e.dma_start(out=dpre_o[:, 8 * gc:8 * gc + 8], in_=dprer[cur][:]), "qO", 16)
                qS_st = p.cnt["qO"]

                @blk.vector
                def _(e, cc=cc, cur=cur, nxt=nxt, sT_U=sT_U, qS_st=qS_st):
                    p.wait(e, "sT", sT_U)
                    e.wait_ge(p.sems["qO"], qS_st)
                    for g in range(8):
                        e.scalar_tensor_tensor(out=sown[nxt][:, g, :], in0=sown[cur][:, g, :],
                                               scalar=dtot[:, g, cc:cc + 1],
                                               in1=psU[:, 64 * g:64 * g + 64],
                                               op0=ALU.mult, op1=ALU.add)
                    p.inc(e.tensor_tensor(out=dprer[nxt][:], in0=dprer[cur][:],
                                          in1=dtot[:, :, cc], op=ALU.mult), "sV")

        fin = NCH % 2

        @blk.vector
        def _(e):
            p.inc(e.tensor_reduce(out=pxn_t[:, NSL:NSL + 1], in_=pxn_t[:, 0:NSL],
                                  axis=AX.X, op=ALU.max), "sV")
        sV_pxn = p.cnt["sV"]

        @blk.sync
        def _(e):
            p.wait(e, "sV", sV_pxn)
            p.inc(e.dma_start(out=pxn_o[:, :], in_=pxn_t[:, NSL:NSL + 1]), "qO", 16)
            p.inc(e.dma_start(out=utot_o[:, :], in_=sown[fin].rearrange("p g j -> p (g j)")), "qO", 16)
            for q in ("qS", "qA", "qP", "qO", "qB0", "qB1", "qT0", "qT1"):
                e.wait_ge(p.sems[q], p.cnt[q])

    return p


def _sigma_cols(p, blk, maxrow_ap, nmax, NV, scr_dram, out_sb, qname="qA", bufs=None):
    nc = p.nc
    if bufs is None:
        mrow = p.sb(f"sgm_{id(maxrow_ap) % 997}", [1, nmax * NV])
        mred = p.sb(f"sgr_{id(maxrow_ap) % 997}", [1, 4 * nmax])
    else:
        mrow, mred = bufs

    @blk.scalar
    def _(e):
        p.inc(e.dma_start(out=mrow[:], in_=maxrow_ap[:, :]), qname, 16)
    qv0 = p.cnt[qname]

    @blk.vector
    def _(e):
        e.wait_ge(p.sems[qname], qv0)
        for i in range(nmax):
            p.inc(e.tensor_reduce(out=mred[:, 4 * i:4 * i + 1], in_=mrow[:, NV * i:NV * (i + 1)],
                                  axis=AX.X, op=ALU.max), "sV")
            p.wait(e, "sV")
            p.inc(e.tensor_scalar(out=mred[:, 4 * i + 1:4 * i + 2], in0=mred[:, 4 * i:4 * i + 1],
                                  scalar1=1.0 / QMAX, scalar2=1e-12, op0=ALU.mult, op1=ALU.max), "sV")
            p.wait(e, "sV")
            p.inc(e.reciprocal(out=mred[:, 4 * i + 2:4 * i + 3], in_=mred[:, 4 * i + 1:4 * i + 2]), "sV")
            p.wait(e, "sV")
            p.inc(e.tensor_scalar(out=mred[:, 4 * i + 3:4 * i + 4], in0=mred[:, 4 * i + 1:4 * i + 2],
                                  scalar1=1.0 / 65536.0, scalar2=None, op0=ALU.mult), "sV")
            p.wait(e, "sV")

    @blk.scalar
    def _(e):
        p.wait(e, "sV")
        p.inc(e.dma_start(out=scr_dram[0:1, :], in_=mred[:]), qname, 16)
        e.wait_ge(p.sems[qname], p.cnt[qname])
        p.inc(e.dma_start(out=out_sb[:], in_=scr_dram[0:1, :].broadcast_to([128, 4 * nmax])), qname, 16)
    return p.cnt[qname]


def emit_xq(p, blk, TC, TSL, x_fm, murstd, sgc, xq, bufs, sg_ready=None):
    NSL = TC // TSL
    xfs, mue, rse, xns, tq = bufs["xfs"], bufs["mue"], bufs["rse"], bufs["xns"], bufs["tq"]
    for s in range(NSL):
        c0 = s * TSL

        @blk.sync
        def _(e, s=s, c0=c0):
            if s > 0:
                p.wait(e, "sV")
            p.inc(e.dma_start(out=xfs[:, :, 0:TSL], in_=fm3(x_fm)[:, :, c0 + 1:c0 + TSL + 1]), "qS", 16)
            p.inc(e.dma_start(out=mue[:, 0:TSL], in_=murstd[0:1, c0 + 1:c0 + TSL + 1].broadcast_to([128, TSL])), "qS", 16)
            p.inc(e.dma_start(out=rse[:, 0:TSL], in_=murstd[1:2, c0 + 1:c0 + TSL + 1].broadcast_to([128, TSL])), "qS", 16)
        qS_ld = p.cnt["qS"]

        @blk.vector
        def _(e, s=s, c0=c0, qS_ld=qS_ld):
            e.wait_ge(p.sems["qS"], qS_ld)
            if sg_ready is not None:
                e.wait_ge(p.sems[sg_ready[0]], sg_ready[1])
            p.inc(e.tensor_tensor(out=xns[:, :, 0:TSL], in0=xfs[:, :, 0:TSL],
                                  in1=mue[:, 0:TSL].rearrange("p (o c) -> p o c", o=1).broadcast_to([128, 8, TSL]),
                                  op=ALU.subtract), "sV")
            p.wait(e, "sV")
            p.inc(e.tensor_tensor(out=xns[:, :, 0:TSL], in0=xns[:, :, 0:TSL],
                                  in1=rse[:, 0:TSL].rearrange("p (o c) -> p o c", o=1).broadcast_to([128, 8, TSL]),
                                  op=ALU.mult), "sV")
            p.wait(e, "sV")
            p.inc(e.tensor_scalar(out=tq[:, :, 0:TSL], in0=xns[:, :, 0:TSL], scalar1=sgc[:, 2:3],
                                  scalar2=RND, op0=ALU.mult, op1=ALU.add), "sV")
            p.wait(e, "sV")
            p.inc(e.tensor_scalar(out=xq[:, :, c0:c0 + TSL], in0=tq[:, :, 0:TSL], scalar1=RND,
                                  scalar2=sgc[:, 1:2], op0=ALU.subtract, op1=ALU.mult), "sV")


def emit_dequant(p, blk, W_dram, wdq_dram, O, tiles_sb, rnd_col, qd="qA"):
    CW = W_dram.shape[1]
    NB = CW // BLK
    NT = O // 128
    wtile, wq, babs, sblk, rblk = tiles_sb
    qp_mark = {}
    q_mark = {}
    m_mark = {}

    def stageA(j):
        bi = j % 2

        @blk.scalar
        def _(e, j=j, bi=bi):
            if j >= 2:
                p.wait(e, "sV", q_mark[j - 2])
            e.wait_ge(p.sems[qd], p.cnt[qd])
            p.inc(e.dma_start(out=wtile[bi][:], in_=W_dram[128 * j:128 * (j + 1), :]), qd, 16)
        qv = p.cnt[qd]

        @blk.vector
        def _(e, j=j, bi=bi, qv=qv):
            e.wait_ge(p.sems[qd], qv)
            if j >= 2:
                p.wait(e, "sV", q_mark[j - 2])
            p.inc(e.tensor_reduce(out=babs[bi][:], in_=wtile[bi].rearrange("p (b k) -> p b k", k=BLK),
                                  axis=AX.X, op=ALU.max, apply_absolute_value=True), "sV")
            p.wait(e, "sV")
            p.inc(e.tensor_scalar(out=sblk[bi][:], in0=babs[bi][:], scalar1=1.0 / WQMAX, scalar2=1e-12,
                                  op0=ALU.mult, op1=ALU.max), "sV")
            p.wait(e, "sV")
            p.inc(e.reciprocal(out=rblk[bi][:], in_=sblk[bi][:]), "sV")
            p.wait(e, "sV")
            p.inc(e.tensor_tensor(out=wtile[bi].rearrange("p (b k) -> p b k", k=BLK),
                                  in0=wtile[bi].rearrange("p (b k) -> p b k", k=BLK),
                                  in1=rblk[bi].rearrange("p (b o) -> p b o", o=1).broadcast_to([128, NB, BLK]),
                                  op=ALU.mult), "sV")
        m_mark[j] = p.cnt["sV"]

    def stageB(j):
        bi = j % 2
        sV_m = m_mark[j]

        @blk.scalar
        def _(e, j=j, bi=bi, sV_m=sV_m):
            p.wait(e, "sV", sV_m)
            p.inc(e.activation(out=wtile[bi][:], in_=wtile[bi][:], func=AF.Identity,
                               bias=rnd_col, scale=1.0), "sA")
        sA_m = p.cnt["sA"]

        @blk.vector
        def _(e, j=j, bi=bi, sA_m=sA_m):
            p.wait(e, "sA", sA_m)
            if j >= 2:
                e.wait_ge(p.sems[qp_mark[j - 2][0]], qp_mark[j - 2][1])
            p.inc(e.scalar_tensor_tensor(out=wq[bi].rearrange("p (b k) -> p b k", k=BLK),
                                         in0=wtile[bi].rearrange("p (b k) -> p b k", k=BLK),
                                         scalar=RND, in1=sblk[bi].rearrange("p (b o) -> p b o", o=1).broadcast_to([128, NB, BLK]),
                                         op0=ALU.subtract, op1=ALU.mult), "sV")
        q_mark[j] = p.cnt["sV"]

        @blk.gpsimd
        def _(e, j=j, bi=bi, sV_q=q_mark[j]):
            p.wait(e, "sV", sV_q)
            qps = "qP%d" % bi
            p.inc(e.dma_start(out=wdq_dram[128 * j:128 * (j + 1), :], in_=wq[bi][:]), qps, 16)
        qp_mark[j] = ("qP%d" % bi, p.cnt["qP%d" % bi])

    for it in range(NT + 1):
        if it < NT:
            stageA(it)
        if it >= 1:
            stageB(it - 1)
    return p.cnt["qP"]


def emit_upgate_mm(p, blk, wdq_dram, wT, xq, TC, psD, utile, pxacc, col0,
                   i16_out=None, qi16=None, sgu=None, sig_chain=None, qw="qA",
                   sg_ready=None):
    NT = FFI // 128
    NTC = TC // 512
    for kc in range(8):
        @blk.scalar
        def _(e, kc=kc):
            if kc == 0:
                p.wait(e, "sT")
            for qq in ("qP", "qP0", "qP1"):
                e.wait_ge(p.sems[qq], p.cnt[qq])
            p.inc(e.dma_start(out=wT[:, kc, :], in_=wdq_dram[0:FFI, 128 * kc:128 * (kc + 1)],
                              transpose=True), qw, 16)
    qv_w = p.cnt[qw]
    sV_xq = p.cnt["sV"]
    evac_done = {-2: p.cnt["sA"], -1: p.cnt["sA"]}
    cons_done = {-2: p.cnt["sV"], -1: p.cnt["sV"]}
    ship_done = {-2: ("qO0", p.cnt.get("qO0", 0)), -1: ("qO1", p.cnt.get("qO1", 0))}
    for ot in range(NT):
        for tch in range(NTC):
            idx = ot * NTC + tch
            pp = psD[idx % 2]

            @blk.tensor
            def _(e, ot=ot, tch=tch, pp=pp, idx=idx):
                e.wait_ge(p.sems[qw], qv_w)
                p.wait(e, "sV", sV_xq)
                p.wait(e, "sA", evac_done[idx - 2])
                for kc in range(8):
                    ii = e.matmul(pp[:, 0:512], wT[:, kc, 128 * ot:128 * ot + 128],
                                  xq[:, kc, 512 * tch:512 * tch + 512],
                                  start=(kc == 0), stop=(kc == 7))
                p.inc(ii, "sT")

            @blk.scalar
            def _(e, ot=ot, tch=tch, pp=pp, idx=idx):
                p.wait(e, "sT")
                p.wait(e, "sV", cons_done[idx - 2])
                if i16_out is None:
                    p.inc(e.activation(out=utile[idx % 2][:], in_=pp[:, 0:512], func=AF.Identity), "sA")
                else:
                    if sg_ready is not None:
                        e.wait_ge(p.sems[sg_ready[0]], sg_ready[1])
                    p.inc(e.activation(out=utile[idx % 2][:], in_=pp[:, 0:512], func=AF.Identity,
                                       bias=p.constcol(RND), scale=sgu[:, 2:3]), "sA")
            evac_done[idx] = p.cnt["sA"]
            sA_e = p.cnt["sA"]

            @blk.vector
            def _(e, ot=ot, tch=tch, idx=idx, sA_e=sA_e):
                p.wait(e, "sA", sA_e)
                ut = utile[idx % 2]
                if i16_out is None:
                    p.inc(e.tensor_reduce(out=pxacc[:, col0 + idx:col0 + idx + 1], in_=ut[:],
                                          axis=AX.X, op=ALU.max, apply_absolute_value=True), "sV")
                else:
                    e.wait_ge(p.sems[ship_done[idx - 2][0]], ship_done[idx - 2][1])
                    p.inc(e.tensor_scalar(out=qi16[idx % 2][:], in0=ut[:], scalar1=RND, scalar2=None,
                                          op0=ALU.subtract), "sV")
            if i16_out is not None:
                sV_q = p.cnt["sV"]

                qos = "qO%d" % (idx % 2)

                @blk.sync
                def _(e, ot=ot, tch=tch, idx=idx, sV_q=sV_q, qos=qos):
                    p.wait(e, "sV", sV_q)
                    p.inc(e.dma_start(out=i16_out[128 * ot:128 * (ot + 1), 512 * tch:512 * (tch + 1)],
                                      in_=qi16[idx % 2][:]), qos, 16)
                ship_done[idx] = (qos, p.cnt[qos])
                if sig_chain is not None:
                    gqf, sigt, sigq, g2t = sig_chain
                    bi = idx % 2

                    @blk.vector
                    def _(e, idx=idx, bi=bi, sV_q=sV_q):
                        p.wait(e, "sV", sV_q)
                        p.inc(e.tensor_scalar(out=gqf[bi][:], in0=qi16[bi][:], scalar1=sgu[:, 1:2],
                                              scalar2=None, op0=ALU.mult), "sV")
                    sV_g = p.cnt["sV"]

                    @blk.scalar
                    def _(e, idx=idx, bi=bi, sV_g=sV_g):
                        p.wait(e, "sV", sV_g)
                        p.inc(e.activation(out=sigt[bi][:], in_=gqf[bi][:], func=AF.Tanh, scale=0.5), "sA")
                    sA_s = p.cnt["sA"]

                    @blk.vector
                    def _(e, idx=idx, bi=bi, sA_s=sA_s):
                        p.wait(e, "sA", sA_s)
                        p.inc(e.tensor_scalar(out=sigq[bi][:], in0=sigt[bi][:], scalar1=32768.0, scalar2=32768.0 + RND,
                                              op0=ALU.mult, op1=ALU.add), "sV")
                        p.wait(e, "sV")
                        p.inc(e.tensor_scalar(out=sigq[bi][:], in0=sigq[bi][:], scalar1=RND + 65535.0,
                                              scalar2=RND, op0=ALU.min, op1=ALU.subtract), "sV")
                        p.wait(e, "sV")
                        p.inc(e.scalar_tensor_tensor(out=g2t[bi][:], in0=sigq[bi][:], scalar=1.0 / 65536.0,
                                                     in1=gqf[bi][:], op0=ALU.mult, op1=ALU.mult), "sV")
                        p.wait(e, "sV")
                        p.inc(e.tensor_reduce(out=pxacc[:, col0 + idx:col0 + idx + 1], in_=g2t[bi][:],
                                              axis=AX.X, op=ALU.max, apply_absolute_value=True), "sV")
            cons_done[idx] = p.cnt["sV"]
    return


def build_L2(TC=2048, TSL=512):
    TTP = ceil_div(TC + 1, 128) * 128
    NTC = TC // 512
    p = Prog("L2")
    nc = p.nc
    x_fm = p.din("x_fm", [C, TC + 1])
    murstd = p.din("murstd", [2, TTP])
    pxn_row = p.din("pxn_row", [1, 1024])
    w_up = p.din("w_up", [FFI, C])
    w_gate = p.din("w_gate", [FFI, C])
    pxu_o = p.dout("pxu", [128, 1])
    pxg_o = p.dout("pxg", [128, 1])
    wdqu_o = p.dout("wdq_up", [FFI, C], BF16)
    wdqg_o = p.dout("wdq_gate", [FFI, C], BF16)
    sgscr = nc.dram_tensor("sgscr", [1, 4], F32, kind="Internal").ap()

    sgx = p.sb("sgx", [128, 4])
    xq = p.sb("xq", [128, 8, TC], BF16)
    wT = p.sb("wT", [128, 8, FFI], BF16)
    wtile = [p.sb(f"wt{i}", [128, C]) for i in range(2)]
    wq2 = [p.sb(f"wq{i}", [128, C], BF16) for i in range(2)]
    babs = [p.sb(f"babs{i}", [128, C // BLK]) for i in range(2)]
    sblk = [p.sb(f"sblk{i}", [128, C // BLK]) for i in range(2)]
    rblk = [p.sb(f"rblk{i}", [128, C // BLK]) for i in range(2)]
    utile = [p.sb(f"ut{i}", [128, 512]) for i in range(2)]
    pxacc = p.sb("pxacc", [128, 2 * 24 * NTC + 2])
    bufs = dict(xfs=p.sb("xfs", [128, 8, TSL]), mue=p.sb("mue", [128, TSL]),
                rse=p.sb("rse", [128, TSL]), xns=p.sb("xns", [128, 8, TSL]),
                tq=p.sb("tq", [128, 8, TSL]))
    rnd_col = p.constcol(RND)
    psD = [p.ps(f"psD{i}", [128, 512]) for i in range(2)]

    for s in ("qS", "qA", "qP", "qP0", "qP1", "qG", "qD", "qW", "qO", "sV", "sA", "sP", "sT"):
        p.sem(s)

    with nc.Block() as blk:
        @blk.vector
        def _(e):
            p.emit_const_memsets(e, "sV")
        qsg = _sigma_cols(p, blk, pxn_row, 1, 1024, sgscr, sgx, "qG")
        emit_xq(p, blk, TC, TSL, x_fm, murstd, sgx, xq, bufs, sg_ready=("qG", qsg))
        emit_dequant(p, blk, w_up, wdqu_o, FFI, (wtile, wq2, babs, sblk, rblk), rnd_col, "qD")
        emit_upgate_mm(p, blk, wdqu_o, wT, xq, TC, psD, utile, pxacc, 0, qw="qW")
        emit_dequant(p, blk, w_gate, wdqg_o, FFI, (wtile, wq2, babs, sblk, rblk), rnd_col, "qD")
        emit_upgate_mm(p, blk, wdqg_o, wT, xq, TC, psD, utile, pxacc, 24 * NTC, qw="qW")

        NCOL = 24 * NTC

        @blk.vector
        def _(e):
            p.wait(e, "sA")
            p.wait(e, "sV")
            p.inc(e.tensor_reduce(out=pxacc[:, 2 * NCOL:2 * NCOL + 1], in_=pxacc[:, 0:NCOL],
                                  axis=AX.X, op=ALU.max), "sV")
            p.wait(e, "sV")
            p.inc(e.tensor_reduce(out=pxacc[:, 2 * NCOL + 1:2 * NCOL + 2], in_=pxacc[:, NCOL:2 * NCOL],
                                  axis=AX.X, op=ALU.max), "sV")
        sV_fin = p.cnt["sV"]

        @blk.sync
        def _(e):
            p.wait(e, "sV", sV_fin)
            p.inc(e.dma_start(out=pxu_o[:, :], in_=pxacc[:, 2 * NCOL:2 * NCOL + 1]), "qS", 16)
            p.inc(e.dma_start(out=pxg_o[:, :], in_=pxacc[:, 2 * NCOL + 1:2 * NCOL + 2]), "qS", 16)
            for q in ("qS", "qG", "qD", "qW", "qP", "qP0", "qP1", "qO"):
                e.wait_ge(p.sems[q], p.cnt[q])
    return p


def build_L3(TC=2048, TSL=512):
    TTP = ceil_div(TC + 1, 128) * 128
    NTC = TC // 512
    p = Prog("L3")
    nc = p.nc
    x_fm = p.din("x_fm", [C, TC + 1])
    murstd = p.din("murstd", [2, TTP])
    pxn_row = p.din("pxn_row", [1, 1024])
    pxug_row = p.din("pxug_row", [1, 2048])
    wdqu = p.din("wdq_up", [FFI, C], BF16)
    wdqg = p.din("wdq_gate", [FFI, C], BF16)
    upq_o = p.dout("upq", [FFI, TC], I16)
    gq_o = p.dout("gq", [FFI, TC], I16)
    pxg2_o = p.dout("pxg2", [128, 1])
    sgscr = nc.dram_tensor("sgscr", [1, 4], F32, kind="Internal").ap()
    sgscr2 = nc.dram_tensor("sgscr2", [1, 8], F32, kind="Internal").ap()

    sgx = p.sb("sgx", [128, 4])
    sgug = p.sb("sgug", [128, 8])
    xq = p.sb("xq", [128, 8, TC], BF16)
    wT = p.sb("wT", [128, 8, FFI], BF16)
    utile = [p.sb(f"ut{i}", [128, 512]) for i in range(2)]
    qi16 = [p.sb(f"qi{i}", [128, 512], I16) for i in range(2)]
    gqf = [p.sb(f"gqf{i}", [128, 512]) for i in range(2)]
    sigt = [p.sb(f"sigt{i}", [128, 512]) for i in range(2)]
    sigq = [p.sb(f"sigq{i}", [128, 512]) for i in range(2)]
    g2t = [p.sb(f"g2t{i}", [128, 512]) for i in range(2)]
    pxacc = p.sb("pxacc", [128, 24 * NTC + 1])
    bufs = dict(xfs=p.sb("xfs", [128, 8, TSL]), mue=p.sb("mue", [128, TSL]),
                rse=p.sb("rse", [128, TSL]), xns=p.sb("xns", [128, 8, TSL]),
                tq=p.sb("tq", [128, 8, TSL]))
    rnd_col = p.constcol(RND)
    psD = [p.ps(f"psD{i}", [128, 512]) for i in range(2)]

    for s in ("qS", "qA", "qP", "qP0", "qP1", "qG", "qG2", "qW", "qO", "qO0", "qO1",
              "sV", "sA", "sP", "sT"):
        p.sem(s)

    with nc.Block() as blk:
        @blk.vector
        def _(e):
            p.emit_const_memsets(e, "sV")
        qs1 = _sigma_cols(p, blk, pxn_row, 1, 1024, sgscr, sgx, "qG")
        qsg = _sigma_cols(p, blk, pxug_row, 2, 1024, sgscr2, sgug, "qG2")
        emit_xq(p, blk, TC, TSL, x_fm, murstd, sgx, xq, bufs, sg_ready=("qG", qs1))
        emit_upgate_mm(p, blk, wdqu, wT, xq, TC, psD, utile, pxacc, 0,
                       i16_out=upq_o, qi16=qi16, sgu=sgug[:, 0:4], qw="qW",
                       sg_ready=("qG2", qsg))
        emit_upgate_mm(p, blk, wdqg, wT, xq, TC, psD, utile, pxacc, 0,
                       i16_out=gq_o, qi16=qi16, sgu=sgug[:, 4:8],
                       sig_chain=(gqf, sigt, sigq, g2t), qw="qW",
                       sg_ready=("qG2", qsg))
        NCOL = 24 * NTC

        @blk.vector
        def _(e):
            p.wait(e, "sA")
            p.wait(e, "sV")
            p.inc(e.tensor_reduce(out=pxacc[:, NCOL:NCOL + 1], in_=pxacc[:, 0:NCOL],
                                  axis=AX.X, op=ALU.max), "sV")
        sV_fin = p.cnt["sV"]

        @blk.sync
        def _(e):
            p.wait(e, "sV", sV_fin)
            p.inc(e.dma_start(out=pxg2_o[:, :], in_=pxacc[:, NCOL:NCOL + 1]), "qS", 16)
            for q in ("qS", "qG", "qG2", "qW", "qO", "qO0", "qO1", "qP0", "qP1"):
                e.wait_ge(p.sems[q], p.cnt[q])
    return p


def build_L4(TC=2048):
    NT = FFI // 128
    NTC = TC // 512
    p = Prog("L4")
    nc = p.nc
    upq = p.din("upq", [FFI, TC], I16)
    gq = p.din("gq", [FFI, TC], I16)
    pxug_row = p.din("pxug_row", [1, 2048])
    pxg2_row = p.din("pxg2_row", [1, 1024])
    pxo_o = p.dout("pxo", [128, 1])
    sgscr2 = nc.dram_tensor("sgscr2", [1, 8], F32, kind="Internal").ap()
    sgscr3 = nc.dram_tensor("sgscr3", [1, 4], F32, kind="Internal").ap()

    sgug = p.sb("sgug", [128, 8])
    sg3 = p.sb("sg3", [128, 4])
    ui = [p.sb(f"ui{i}", [128, 512], I16) for i in range(2)]
    gi = [p.sb(f"gi{i}", [128, 512], I16) for i in range(2)]
    gqf = [p.sb(f"gqf{i}", [128, 512]) for i in range(2)]
    sigt = [p.sb(f"sigt{i}", [128, 512]) for i in range(2)]
    sigq = [p.sb(f"sigq{i}", [128, 512]) for i in range(2)]
    g2t = [p.sb(f"g2t{i}", [128, 512]) for i in range(2)]
    g2q = [p.sb(f"g2q{i}", [128, 512]) for i in range(2)]
    ot = [p.sb(f"ot{i}", [128, 512]) for i in range(2)]
    NTL = NT * NTC
    pxacc = p.sb("pxacc", [128, NTL + 1])
    rnd_col = p.constcol(RND)

    for s in ("qS", "qG", "qG2", "qL0", "qL1", "sV", "sA", "sP", "sT"):
        p.sem(s)

    with nc.Block() as blk:
        @blk.vector
        def _(e):
            p.emit_const_memsets(e, "sV")
        qg1 = _sigma_cols(p, blk, pxug_row, 2, 1024, sgscr2, sgug, "qG")
        qg2 = _sigma_cols(p, blk, pxg2_row, 1, 1024, sgscr3, sg3, "qG2")

        chain_end = {}
        for j in range(NT):
            for tc4 in range(NTC):
                idx = j * NTC + tc4
                bi = idx % 2
                ql = "qL%d" % bi

                @blk.sync
                def _(e, j=j, tc4=tc4, bi=bi, idx=idx, ql=ql):
                    if idx >= 2:
                        p.wait(e, "sV", chain_end[idx - 2])
                    p.inc(e.dma_start(out=ui[bi][:], in_=upq[128 * j:128 * (j + 1), 512 * tc4:512 * (tc4 + 1)]), ql, 16)
                    p.inc(e.dma_start(out=gi[bi][:], in_=gq[128 * j:128 * (j + 1), 512 * tc4:512 * (tc4 + 1)]), ql, 16)
                qld = p.cnt[ql]

                @blk.vector
                def _(e, bi=bi, idx=idx, qld=qld, ql=ql):
                    e.wait_ge(p.sems[ql], qld)
                    if idx == 0:
                        e.wait_ge(p.sems["qG"], qg1)
                        e.wait_ge(p.sems["qG2"], qg2)
                    if idx >= 1:
                        p.wait(e, "sV", chain_end.get(idx - 1, 0))
                    p.inc(e.tensor_scalar(out=gqf[bi][:], in0=gi[bi][:], scalar1=sgug[:, 5:6],
                                          scalar2=None, op0=ALU.mult), "sV")
                sV_g = p.cnt["sV"]

                @blk.scalar
                def _(e, bi=bi, sV_g=sV_g):
                    p.wait(e, "sV", sV_g)
                    p.inc(e.activation(out=sigt[bi][:], in_=gqf[bi][:], func=AF.Tanh, scale=0.5), "sA")
                sA_s = p.cnt["sA"]

                @blk.vector
                def _(e, bi=bi, idx=idx, sA_s=sA_s):
                    p.wait(e, "sA", sA_s)
                    p.inc(e.tensor_scalar(out=sigq[bi][:], in0=sigt[bi][:], scalar1=32768.0, scalar2=32768.0 + RND,
                                          op0=ALU.mult, op1=ALU.add), "sV")
                    p.wait(e, "sV")
                    p.inc(e.tensor_scalar(out=sigq[bi][:], in0=sigq[bi][:], scalar1=RND + 65535.0,
                                          scalar2=RND, op0=ALU.min, op1=ALU.subtract), "sV")
                    p.wait(e, "sV")
                    p.inc(e.scalar_tensor_tensor(out=g2t[bi][:], in0=sigq[bi][:], scalar=1.0 / 65536.0,
                                                 in1=gqf[bi][:], op0=ALU.mult, op1=ALU.mult), "sV")
                    p.wait(e, "sV")
                    p.inc(e.tensor_scalar(out=g2q[bi][:], in0=g2t[bi][:], scalar1=sg3[:, 2:3],
                                          scalar2=RND, op0=ALU.mult, op1=ALU.add), "sV")
                    p.wait(e, "sV")
                    p.inc(e.tensor_scalar(out=g2q[bi][:], in0=g2q[bi][:], scalar1=RND,
                                          scalar2=sg3[:, 1:2], op0=ALU.subtract, op1=ALU.mult), "sV")
                    p.wait(e, "sV")
                    p.inc(e.scalar_tensor_tensor(out=ot[bi][:], in0=ui[bi][:], scalar=sgug[:, 1:2],
                                                 in1=g2q[bi][:], op0=ALU.mult, op1=ALU.mult), "sV")
                    p.wait(e, "sV")
                    p.inc(e.tensor_reduce(out=pxacc[:, idx:idx + 1], in_=ot[bi][:],
                                          axis=AX.X, op=ALU.max, apply_absolute_value=True), "sV")
                chain_end[idx] = p.cnt["sV"]

        @blk.vector
        def _(e):
            p.wait(e, "sV")
            p.wait(e, "sA")
            p.inc(e.tensor_reduce(out=pxacc[:, NTL:NTL + 1], in_=pxacc[:, 0:NTL],
                                  axis=AX.X, op=ALU.max), "sV")
        sV_fin = p.cnt["sV"]

        @blk.sync
        def _(e):
            p.wait(e, "sV", sV_fin)
            p.inc(e.dma_start(out=pxo_o[:, :], in_=pxacc[:, NTL:NTL + 1]), "qS", 16)
            for q in ("qS", "qG", "qG2", "qL0", "qL1"):
                e.wait_ge(p.sems[q], p.cnt[q])
    return p


def build_L5(TC=2048):
    NCH = TC // L
    NQ = TC // 512
    p = Prog("L5")
    nc = p.nc
    x_fm = p.din("x_fm", [C, TC + 1])
    qt = p.din("qt", [C, TC], BF16)
    kt = p.din("kt", [C, TC], BF16)
    vq = p.din("vq", [TC, C], BF16)
    xg = p.din("xg", [C, TC], BF16)
    b_rowT = p.din("b_rowT", [TC, 16])
    utotP = p.din("utotP", [128, 512])
    traj = p.din("traj", [128, NCH * 512])
    dpre = p.din("dpre", [128, NCH * 8])
    WgT = p.din("WgT", [C, C])
    WoT = p.din("WoT", [C, C])
    w_down = p.din("w_down", [C, FFI])
    upq = p.din("upq", [FFI, TC], I16)
    gq = p.din("gq", [FFI, TC], I16)
    pxug_row = p.din("pxug_row", [1, 2048])
    pxg2_row = p.din("pxg2_row", [1, 1024])
    pxo_row = p.din("pxo_row", [1, 1024])
    gam_row = p.din("gam_row", [1, C])
    bet_row = p.din("bet_row", [1, C])
    tri = p.din("tri", [128, 128])
    ident = p.din("ident", [128, 128])
    out_o = p.dout("out_fm", [C, TC])
    attn_scr = nc.dram_tensor("attn_scr", [C, TC], F32, kind="Internal").ap()
    wdqd_scr = nc.dram_tensor("wdqd_scr", [C, FFI], BF16, kind="Internal").ap()
    sgscr2 = nc.dram_tensor("sgscr2", [1, 8], F32, kind="Internal").ap()
    sgscr3 = nc.dram_tensor("sgscr3", [1, 4], F32, kind="Internal").ap()
    sgscr4 = nc.dram_tensor("sgscr4", [1, 4], F32, kind="Internal").ap()

    sgug = p.sb("sgug", [128, 8])
    sg3 = p.sb("sg3", [128, 4])
    sg4 = p.sb("sg4", [128, 4])
    sgb1 = (p.sb("sgb1a", [1, 2048]), p.sb("sgb1b", [1, 8]))
    sgb2 = (p.sb("sgb2a", [1, 1024]), p.sb("sgb2b", [1, 4]))
    sgb3 = (p.sb("sgb3a", [1, 1024]), p.sb("sgb3b", [1, 4]))
    rnd_col = p.constcol(RND)
    epsg_col = p.constcol(16e-5)

    for s in ("qS", "qA", "qP", "qP0", "qP1", "qG", "qG2", "qG3", "qO", "qO0", "qO1",
              "qL0", "qL1", "qC0", "qC1", "qW", "qD", "qT0", "qT1", "sV", "sA", "sP", "sT"):
        p.sem(s)

    from contextlib import ExitStack
    subA = ExitStack()

    def sba(name, shape, dtype=F32):
        return subA.enter_context(nc.sbuf_tensor(name, list(shape), dtype))

    wg = sba("wg", [128, 8, C], BF16)
    wo = sba("wo", [128, 8, C], BF16)
    u0s = sba("u0s", [128, 8, 64])
    gex = sba("gex", [128, C])
    bex = sba("bex", [128, C])
    trim = sba("trim", [128, 128])
    idn = sba("idn", [128, 128])
    stgw = sba("stgw", [128, 2, C])
    qtc = [sba(f"qtc{i}", [128, 8, 128], BF16) for i in range(4)]
    ktc = [sba(f"ktc{i}", [128, 8, 128], BF16) for i in range(4)]
    vqc = [sba(f"vqc{i}", [128, C], BF16) for i in range(4)]
    xgc = [sba(f"xgc{i}", [128, 8, 128], BF16) for i in range(4)]
    trajc = [sba(f"trajc{i}", [128, 512]) for i in range(4)]
    dpc = [sba(f"dpc{i}", [128, 8]) for i in range(4)]
    bTc = [sba(f"bTc{i}", [128, 16]) for i in range(4)]
    scg = [sba(f"scg{i}", [128, 8, 64], BF16) for i in range(4)]
    amb = [sba(f"amb{i}", [128, 16 * 128], BF16) for i in range(4)]
    ytc = [sba(f"ytc{i}", [128, C]) for i in range(4)]
    ysq = [sba(f"ysq{i}", [128, C]) for i in range(4)]
    gnt = [sba(f"gnt{i}", [128, 80]) for i in range(4)]
    gtc = [sba(f"gtc{i}", [128, C]) for i in range(4)]
    gth = [sba(f"gth{i}", [128, 512], BF16) for i in range(4)]
    gth1 = [sba(f"gth1{i}", [128, 512]) for i in range(4)]
    ygb = [sba(f"ygb{i}", [128, C], BF16) for i in range(4)]
    ygf = [sba(f"ygf{i}", [128, 8, 128], BF16) for i in range(4)]
    afm = [sba(f"afm{i}", [128, 128]) for i in range(2)]

    sub_ps = ExitStack()
    psA2 = [sub_ps.enter_context(nc.psum_tensor(f"psA{i}", [128, 512], F32)) for i in range(2)]
    psY2 = [sub_ps.enter_context(nc.psum_tensor(f"psY{i}", [128, 512], F32)) for i in range(2)]
    psWo = [sub_ps.enter_context(nc.psum_tensor(f"psWo{i}", [128, 512], F32)) for i in range(2)]
    psG = [sub_ps.enter_context(nc.psum_tensor(f"psG{i}", [128, 512], F32)) for i in range(2)]

    with nc.Block() as blk:
        @blk.vector
        def _(e):
            p.emit_const_memsets(e, "sV")
        qg1 = _sigma_cols(p, blk, pxug_row, 2, 1024, sgscr2, sgug, "qG", bufs=sgb1)
        qg2 = _sigma_cols(p, blk, pxg2_row, 1, 1024, sgscr3, sg3, "qG2", bufs=sgb2)
        qg3 = _sigma_cols(p, blk, pxo_row, 1, 1024, sgscr4, sg4, "qG3", bufs=sgb3)

        @blk.sync
        def _(e):
            p.inc(e.dma_start(out=u0s[:], in_=utotP.rearrange("p (g j) -> p g j", j=64)), "qS", 16)
            p.inc(e.dma_start(out=gex[:], in_=gam_row[0:1, :].broadcast_to([128, C])), "qS", 16)
            p.inc(e.dma_start(out=bex[:], in_=bet_row[0:1, :].broadcast_to([128, C])), "qS", 16)
            p.inc(e.dma_start(out=trim[:], in_=tri[:, :]), "qS", 16)
            p.inc(e.dma_start(out=idn[:], in_=ident[:, :]), "qS", 16)
        qS_c = p.cnt["qS"]

        for wsb, whbm in ((wg, WgT), (wo, WoT)):
            for t2 in range(0, 8, 2):
                @blk.scalar
                def _(e, wsb=wsb, whbm=whbm, t2=t2):
                    p.wait(e, "sA")
                    p.inc(e.dma_start(out=stgw[:], in_=fm3(whbm)[:, t2:t2 + 2, :]), "qP", 16)
                    e.wait_ge(p.sems["qP"], p.cnt["qP"])
                    p.inc(e.activation(out=wsb[:, t2:t2 + 2, :], in_=stgw[:], func=AF.Identity), "sA")
        qP_c = p.cnt["sA"]

        chunk_done = {}
        for _i in range(-4, 0):
            chunk_done[_i] = (0, 0, 0)
        amb_done = {}
        ybank_done = {-2: p.cnt["sV"], -1: p.cnt["sV"]}
        ymm_sT = {-4: 0, -3: 0, -2: 0, -1: 0}
        ygb_free = {-1: ("qT0", 0), -2: ("qT0", 0), -3: ("qT0", 0), -4: ("qT0", 0)}
        apply_done = {}
        womm_sT = {-4: 0, -3: 0, -2: 0, -1: 0}
        gproj_done = {-2: (0, 0), -1: (p.cnt["sA"], p.cnt["sV"])}
        afm_done = {-2: p.cnt["sA"], -1: p.cnt["sA"]}
        woship = {-2: ("qO", 0), -1: ("qO", 0)}
        s1info = {}

        def emit_s1(gc):
            bi = gc % 4
            qc = "qC%d" % (gc % 2)
            t0 = gc * L

            @blk.sync
            def _(e, gc=gc, bi=bi, qc=qc, t0=t0):
                e.wait_ge(p.sems["qS"], qS_c)
                if gc >= 4:
                    cd = chunk_done[gc - 4]
                    p.wait(e, "sV", cd[0])
                    p.wait(e, "sA", cd[1])
                    p.wait(e, "sT", cd[2])
                p.inc(e.dma_start(out=qtc[bi][:], in_=fm3(qt)[:, :, t0:t0 + L]), qc, 16)
                p.inc(e.dma_start(out=ktc[bi][:], in_=fm3(kt)[:, :, t0:t0 + L]), qc, 16)
                p.inc(e.dma_start(out=vqc[bi][:], in_=vq[t0:t0 + L, :]), qc, 16)
                p.inc(e.dma_start(out=xgc[bi][:], in_=fm3(xg)[:, :, t0:t0 + L]), qc, 16)
                p.inc(e.dma_start(out=trajc[bi][:], in_=traj[:, 512 * gc:512 * gc + 512]), qc, 16)
                p.inc(e.dma_start(out=dpc[bi][:], in_=dpre[:, 8 * gc:8 * gc + 8]), qc, 16)
                p.inc(e.dma_start(out=bTc[bi][:], in_=b_rowT[t0:t0 + L, :]), qc, 16)
            qld = p.cnt[qc]

            @blk.vector
            def _(e, gc=gc, bi=bi, qld=qld, qc=qc):
                e.wait_ge(p.sems[qc], qld)
                e.wait_ge(p.sems["qG"], qg1)
                e.wait_ge(p.sems["qG2"], qg2)
                e.wait_ge(p.sems["qG3"], qg3)
                p.wait(e, "sT", chunk_done[gc - 4][2] if gc >= 4 else 0)
                for g in range(8):
                    ii = e.scalar_tensor_tensor(out=scg[bi][:, g, :], in0=u0s[:, g, :],
                                                scalar=dpc[bi][:, g:g + 1], in1=trajc[bi][:, 64 * g:64 * g + 64],
                                                op0=ALU.mult, op1=ALU.add)
                p.inc(ii, "sV")

            for half in range(2):
                for par in range(2):
                    pb = psA2[par]
                    bp = 64 * par

                    @blk.tensor
                    def _(e, gc=gc, bi=bi, half=half, par=par, pb=pb, bp=bp, qld=qld, qc=qc):
                        e.wait_ge(p.sems[qc], qld)
                        pk = (gc, half - 1, par) if half > 0 else (gc - 1, 1, par)
                        if pk in amb_done:
                            p.wait(e, "sV", amb_done[pk])
                        for hh in range(4):
                            h = half * 8 + 2 * hh + par
                            ii = e.matmul(pb[:, 128 * hh:128 * hh + 128],
                                          ktc[bi][bp:bp + 64, h // 2, :],
                                          qtc[bi][bp:bp + 64, h // 2, :], start=True, stop=True)
                        p.inc(ii, "sT")
                    sT_a = p.cnt["sT"]

                    @blk.vector
                    def _(e, gc=gc, half=half, par=par, pb=pb, sT_a=sT_a):
                        p.wait(e, "sT", sT_a)
                        if half == 0:
                            p.wait(e, "sT", ymm_sT.get(gc - 4, 0))
                        oview = amb[gc % 4].rearrange("p (u v) -> p u v", v=128)[:, 8 * half + par:8 * half + 8:2, :]
                        p.inc(e.tensor_tensor(
                            out=oview,
                            in0=pb[:, :].rearrange("p (u v) -> p u v", v=128),
                            in1=trim.rearrange("p (o v) -> p o v", o=1).broadcast_to([128, 4, 128]),
                            op=ALU.mult), "sV")
                    amb_done[(gc, half, par)] = p.cnt["sV"]
            sV_am = p.cnt["sV"]

            for par in range(2):
                pb = psY2[par]
                bp = 64 * par

                @blk.tensor
                def _(e, gc=gc, bi=bi, par=par, pb=pb, bp=bp, sV_am=sV_am):
                    p.wait(e, "sV", sV_am)
                    p.wait(e, "sV", ybank_done[gc - 1])
                    for hh in range(8):
                        h = 2 * hh + par
                        e.matmul(pb[:, 64 * hh:64 * hh + 64], amb[bi][:, 128 * h:128 * h + 128],
                                 vqc[bi][:, 64 * h:64 * h + 64], start=True, stop=False)
                        ii = e.matmul(pb[:, 64 * hh:64 * hh + 64],
                                      qtc[bi][bp:bp + 64, h // 2, :],
                                      scg[bi][bp:bp + 64, h // 2, :], start=False, stop=True)
                    p.inc(ii, "sT")
            sT_y = p.cnt["sT"]
            ymm_sT[gc] = sT_y

            @blk.vector
            def _(e, gc=gc, bi=bi, sT_y=sT_y):
                p.wait(e, "sT", sT_y)
                p.wait(e, "sA")
                for h in range(16):
                    ii = e.scalar_tensor_tensor(out=ytc[bi][:, 64 * h:64 * h + 64],
                                                in0=vqc[bi][:, 64 * h:64 * h + 64],
                                                scalar=bTc[bi][:, h:h + 1],
                                                in1=psY2[h % 2][:, 64 * (h // 2):64 * (h // 2) + 64],
                                                op0=ALU.mult, op1=ALU.add)
                p.inc(ii, "sV")
            sV_y = p.cnt["sV"]
            ybank_done[gc] = sV_y

            @blk.vector
            def _(e, bi=bi, sV_y=sV_y):
                p.wait(e, "sV", sV_y)
                p.inc(e.tensor_tensor(out=ysq[bi][:], in0=ytc[bi][:], in1=ytc[bi][:], op=ALU.mult), "sV")
                p.wait(e, "sV")
                e.tensor_reduce(out=gnt[bi][:, 0:16], in_=ysq[bi].rearrange("p (h j) -> p h j", j=64),
                                axis=AX.X, op=ALU.add)
                p.inc(e.tensor_reduce(out=gnt[bi][:, 16:32], in_=ytc[bi].rearrange("p (h j) -> p h j", j=64),
                                      axis=AX.X, op=ALU.add), "sV")
                p.wait(e, "sV")
                e.tensor_scalar(out=gnt[bi][:, 32:48], in0=gnt[bi][:, 16:32],
                                scalar1=1.0 / 64.0, scalar2=None, op0=ALU.mult)
                p.inc(e.tensor_scalar(out=gnt[bi][:, 48:64], in0=gnt[bi][:, 0:16],
                                      scalar1=1.0 / 64.0, scalar2=None, op0=ALU.mult), "sV")
                p.wait(e, "sV")
                p.inc(e.tensor_tensor(out=gnt[bi][:, 0:16], in0=gnt[bi][:, 32:48], in1=gnt[bi][:, 32:48],
                                      op=ALU.mult), "sV")
                p.wait(e, "sV")
                p.inc(e.tensor_tensor(out=gnt[bi][:, 48:64], in0=gnt[bi][:, 48:64], in1=gnt[bi][:, 0:16],
                                      op=ALU.subtract), "sV")
            sV_gn = p.cnt["sV"]

            @blk.scalar
            def _(e, bi=bi, gc=gc, sV_gn=sV_gn):
                p.wait(e, "sV", sV_gn)
                p.wait(e, "sV", apply_done.get(gc - 4, 0))
                p.inc(e.activation(out=gnt[bi][:, 0:16], in_=gnt[bi][:, 48:64], func=AF.Sqrt,
                                   bias=epsg_col, scale=1.0), "sA")
            sA_sq = p.cnt["sA"]

            @blk.vector
            def _(e, bi=bi, sA_sq=sA_sq):
                p.wait(e, "sA", sA_sq)
                p.inc(e.reciprocal(out=gnt[bi][:, 64:80], in_=gnt[bi][:, 0:16]), "sV")
            sV_rs = p.cnt["sV"]
            s1info[gc] = (bi, qc, qld, sV_rs)

        def emit_s2(gc):
            bi, qc, qld, sV_rs = s1info[gc]
            t0 = gc * L

            for hh in range(2):
                pb = psG[hh]
                gi2 = (gc % 2) * 2 + hh

                @blk.tensor
                def _(e, gc=gc, bi=bi, hh=hh, pb=pb, qc=qc, qld=qld):
                    e.wait_ge(p.sems[qc], qld)
                    p.wait(e, "sA", qP_c)
                    p.wait(e, "sV", gproj_done[gc - 1][1])
                    for kc in range(8):
                        ii = e.matmul(pb[:, 0:512], xgc[bi][:, kc, :],
                                      wg[:, kc, 512 * hh:512 * hh + 512],
                                      start=(kc == 0), stop=(kc == 7))
                    p.inc(ii, "sT")
                sT_g = p.cnt["sT"]

                @blk.scalar
                def _(e, gc=gc, hh=hh, pb=pb, gi2=gi2, sT_g=sT_g):
                    p.wait(e, "sT", sT_g)
                    p.wait(e, "sV", gproj_done[gc - 2][1])
                    p.inc(e.activation(out=gth[gi2][:], in_=pb[:, 0:512],
                                       func=AF.Tanh, scale=0.5), "sA")
                sA_gth = p.cnt["sA"]

                @blk.vector
                def _(e, gc=gc, bi=bi, hh=hh, pb=pb, gi2=gi2, sA_gth=sA_gth):
                    p.wait(e, "sA", sA_gth)
                    p.inc(e.tensor_scalar(out=gth1[gi2][:], in0=gth[gi2][:], scalar1=1.0,
                                          scalar2=None, op0=ALU.add), "sV")
                    p.wait(e, "sV")
                    p.inc(e.scalar_tensor_tensor(out=gtc[bi][:, 512 * hh:512 * hh + 512],
                                                 in0=pb[:, 0:512], scalar=0.5, in1=gth1[gi2][:],
                                                 op0=ALU.mult, op1=ALU.mult), "sV")
            sA_g = p.cnt["sA"]
            sV_gd = p.cnt["sV"]
            gproj_done[gc] = (sA_g, sV_gd)

            @blk.vector
            def _(e, gc=gc, bi=bi, sV_rs=sV_rs, sA_g=sA_g):
                p.wait(e, "sV", sV_rs)
                p.wait(e, "sA", sA_g)
                e.wait_ge(p.sems[ygb_free[gc - 4][0]], ygb_free[gc - 4][1])
                p.inc(e.tensor_tensor(out=ysq[bi].rearrange("p (h j) -> p h j", j=64),
                                      in0=ytc[bi].rearrange("p (h j) -> p h j", j=64),
                                      in1=gnt[bi][:, 32:48].rearrange("p (h o) -> p h o", o=1).broadcast_to([128, 16, 64]),
                                      op=ALU.subtract), "sV")
                p.wait(e, "sV")
                p.inc(e.tensor_tensor(out=ysq[bi].rearrange("p (h j) -> p h j", j=64),
                                      in0=ysq[bi].rearrange("p (h j) -> p h j", j=64),
                                      in1=gnt[bi][:, 64:80].rearrange("p (h o) -> p h o", o=1).broadcast_to([128, 16, 64]),
                                      op=ALU.mult), "sV")
                p.wait(e, "sV")
                p.inc(e.tensor_tensor(out=ysq[bi][:], in0=ysq[bi][:], in1=gex[:], op=ALU.mult), "sV")
                p.wait(e, "sV")
                p.inc(e.tensor_tensor(out=ysq[bi][:], in0=ysq[bi][:], in1=bex[:], op=ALU.add), "sV")
                p.wait(e, "sV")
                p.inc(e.tensor_tensor(out=ygb[bi][:], in0=ysq[bi][:], in1=gtc[bi][:], op=ALU.mult), "sV")
            sV_yg = p.cnt["sV"]
            apply_done[gc] = sV_yg

            qtp = "qT%d" % (gc % 2)

            @blk.scalar
            def _(e, gc=gc, bi=bi, sV_yg=sV_yg, qtp=qtp):
                p.wait(e, "sV", sV_yg)
                p.wait(e, "sT", womm_sT.get(gc - 4, 0))
                e.wait_ge(p.sems[qtp], p.cnt[qtp])
                for cb in range(8):
                    p.inc(e.dma_start(out=ygf[bi][:, cb, :], in_=ygb[bi][:, 128 * cb:128 * cb + 128],
                                      transpose=True), qtp, 16)
            qT_y = p.cnt[qtp]
            ygb_free[gc] = (qtp, qT_y)

            for ct in range(8):
                wid = gc * 8 + ct
                pb = psWo[ct % 2]
                qos = "qO%d" % (ct % 2)

                @blk.tensor
                def _(e, gc=gc, ct=ct, pb=pb, qT_y=qT_y, qtp=qtp, wid=wid):
                    e.wait_ge(p.sems[qtp], qT_y)
                    p.wait(e, "sA", afm_done[wid - 2])
                    for kc in range(8):
                        ii = e.matmul(pb[:, 0:128], wo[:, kc, 128 * ct:128 * ct + 128],
                                      ygf[gc % 4][:, kc, :], start=(kc == 0), stop=(kc == 7))
                    p.inc(ii, "sT")
                sT_w = p.cnt["sT"]

                @blk.scalar
                def _(e, gc=gc, ct=ct, pb=pb, sT_w=sT_w, wid=wid, qos=qos):
                    p.wait(e, "sT", sT_w)
                    e.wait_ge(p.sems[woship[wid - 2][0]], woship[wid - 2][1])
                    p.inc(e.activation(out=afm[ct % 2][:], in_=pb[:, 0:128], func=AF.Identity), "sA")
                sA_w = p.cnt["sA"]
                afm_done[wid] = sA_w

                @blk.sync
                def _(e, gc=gc, ct=ct, t0=t0, sA_w=sA_w, qos=qos):
                    p.wait(e, "sA", sA_w)
                    p.inc(e.dma_start(out=attn_scr[128 * ct:128 * (ct + 1), t0:t0 + L],
                                      in_=afm[ct % 2][:]), qos, 16)
                woship[wid] = (qos, p.cnt[qos])
            womm_sT[gc] = p.cnt["sT"]
            chunk_done[gc] = (p.cnt["sV"], p.cnt["sA"], p.cnt["sT"])

        for it in range(NCH + 1):
            if it < NCH:
                emit_s1(it)
            if it >= 1:
                emit_s2(it - 1)

        sub_ps.close()
        subA.close()

        subB = ExitStack()

        def sbb(name, shape, dtype=F32):
            return subB.enter_context(nc.sbuf_tensor(name, list(shape), dtype))

        wtile = [sbb(f"dwt{i}", [128, FFI]) for i in range(2)]
        wq2 = [sbb(f"dwq{i}", [128, FFI], BF16) for i in range(2)]
        babs = [sbb(f"dbabs{i}", [128, FFI // BLK]) for i in range(2)]
        sblk = [sbb(f"dsblk{i}", [128, FFI // BLK]) for i in range(2)]
        rblk = [sbb(f"drblk{i}", [128, FFI // BLK]) for i in range(2)]
        wdT = sbb("wdT", [128, 24, C], BF16)
        ui = [sbb(f"ui{i}", [128, 512], I16) for i in range(2)]
        gi = [sbb(f"gi{i}", [128, 512], I16) for i in range(2)]
        gqf = [sbb(f"gqf{i}", [128, 512]) for i in range(2)]
        sigt = [sbb(f"sigt{i}", [128, 512]) for i in range(2)]
        sigq = sbb("sigq", [128, 512])
        g2t = sbb("g2t", [128, 512])
        g2q = sbb("g2q", [128, 512])
        otl = sbb("otl", [128, 512])
        oqb = sbb("oqb", [128, 512], BF16)
        xres = sbb("xres", [128, 8, 512])
        ares = sbb("ares", [128, 8, 512])
        outt = sbb("outt", [128, 8, 512])

        sub_ps2 = ExitStack()
        psD = [sub_ps2.enter_context(nc.psum_tensor(f"psDn{i}", [128, 512], F32)) for i in range(8)]

        qO_attn = p.cnt["qO"]
        for dev in (blk.sync, blk.scalar, blk.vector, blk.gpsimd, blk.tensor):
            @dev
            def _(e):
                e.wait_ge(p.sems["qO"], qO_attn)
                for qq in ("qC0", "qC1", "qS", "qP", "qO0", "qO1", "qT0", "qT1"):
                    e.wait_ge(p.sems[qq], p.cnt[qq])

        emit_dequant(p, blk, w_down, wdqd_scr, C, (wtile, wq2, babs, sblk, rblk), rnd_col, "qD")

        @blk.scalar
        def _(e):
            for qq in ("qP", "qP0", "qP1"):
                e.wait_ge(p.sems[qq], p.cnt[qq])
            for it in range(24):
                p.inc(e.dma_start(out=wdT[:, it, :], in_=wdqd_scr[0:C, 128 * it:128 * (it + 1)],
                                  transpose=True), "qW", 16)
        qW_d = p.cnt["qW"]

        chain_mark = {}

        def dn_s1(tq, it):
            bi = it % 2
            ql = "qL%d" % bi
            tt0 = tq * 512

            @blk.sync
            def _(e, tq=tq, it=it, bi=bi, ql=ql, tt0=tt0):
                if tq + it > 0:
                    p.wait(e, "sV")
                p.inc(e.dma_start(out=ui[bi][:], in_=upq[128 * it:128 * (it + 1), tt0:tt0 + 512]), ql, 16)
                p.inc(e.dma_start(out=gi[bi][:], in_=gq[128 * it:128 * (it + 1), tt0:tt0 + 512]), ql, 16)
            qld = p.cnt[ql]

            @blk.vector
            def _(e, bi=bi, qld=qld, ql=ql):
                e.wait_ge(p.sems[ql], qld)
                p.inc(e.tensor_scalar(out=gqf[bi][:], in0=gi[bi][:], scalar1=sgug[:, 5:6],
                                      scalar2=None, op0=ALU.mult), "sV")
            sV_g = p.cnt["sV"]

            @blk.scalar
            def _(e, bi=bi, it=it, sV_g=sV_g):
                p.wait(e, "sV", sV_g)
                p.wait(e, "sV", chain_mark.get(it - 2, 0))
                p.inc(e.activation(out=sigt[bi][:], in_=gqf[bi][:], func=AF.Sigmoid), "sA")
            return p.cnt["sA"]

        def dn_s2(tq, it, sA_s):
            bi = it % 2

            @blk.vector
            def _(e, bi=bi, sA_s=sA_s):
                p.wait(e, "sA", sA_s)
                p.inc(e.tensor_scalar(out=sigq[:], in0=sigt[bi][:], scalar1=65536.0, scalar2=RND,
                                      op0=ALU.mult, op1=ALU.add), "sV")
                p.wait(e, "sV")
                p.inc(e.tensor_scalar(out=sigq[:], in0=sigq[:], scalar1=RND + 65535.0,
                                      scalar2=RND, op0=ALU.min, op1=ALU.subtract), "sV")
                p.wait(e, "sV")
                p.inc(e.scalar_tensor_tensor(out=g2t[:], in0=sigq[:], scalar=1.0 / 65536.0,
                                             in1=gqf[bi][:], op0=ALU.mult, op1=ALU.mult), "sV")
                p.wait(e, "sV")
                p.inc(e.tensor_scalar(out=g2q[:], in0=g2t[:], scalar1=sg3[:, 2:3],
                                      scalar2=RND, op0=ALU.mult, op1=ALU.add), "sV")
                p.wait(e, "sV")
                p.inc(e.tensor_scalar(out=g2q[:], in0=g2q[:], scalar1=RND,
                                      scalar2=sg3[:, 1:2], op0=ALU.subtract, op1=ALU.mult), "sV")
                p.wait(e, "sV")
                p.inc(e.scalar_tensor_tensor(out=otl[:], in0=ui[bi][:], scalar=sgug[:, 1:2],
                                             in1=g2q[:], op0=ALU.mult, op1=ALU.mult), "sV")
                p.wait(e, "sV")
                p.inc(e.tensor_scalar(out=g2q[:], in0=otl[:], scalar1=sg4[:, 2:3],
                                      scalar2=RND, op0=ALU.mult, op1=ALU.add), "sV")
                p.wait(e, "sV")
                p.wait(e, "sT")
                p.inc(e.tensor_scalar(out=oqb[:], in0=g2q[:], scalar1=RND,
                                      scalar2=sg4[:, 1:2], op0=ALU.subtract, op1=ALU.mult), "sV")
            sV_oq = p.cnt["sV"]
            chain_mark[it if tq == 0 else it + 24 * tq] = sV_oq
            chain_mark[it - 2] = chain_mark.get(it - 2, 0)

            @blk.tensor
            def _(e, tq=tq, it=it, sV_oq=sV_oq):
                e.wait_ge(p.sems["qW"], qW_d)
                p.wait(e, "sV", sV_oq)
                if it == 0:
                    p.wait(e, "sA")
                for ct in range(8):
                    ii = e.matmul(psD[ct][:, :], wdT[:, it, 128 * ct:128 * ct + 128],
                                  oqb[:, :], start=(it == 0), stop=(it == 23))
                p.inc(ii, "sT")

        def dn_tail(tq):
            tt0 = tq * 512
            sT_dn = p.cnt["sT"]

            @blk.sync
            def _(e, tq=tq, tt0=tt0):
                for qq in ("qO", "qO0", "qO1"):
                    e.wait_ge(p.sems[qq], p.cnt[qq])
                p.inc(e.dma_start(out=xres[:], in_=fm3(x_fm)[:, :, tt0 + 1:tt0 + 513]), "qS", 16)
                p.inc(e.dma_start(out=ares[:], in_=fm3(attn_scr)[:, :, tt0:tt0 + 512]), "qS", 16)
            qS_r = p.cnt["qS"]

            @blk.vector
            def _(e, tq=tq, qS_r=qS_r, sT_dn=sT_dn):
                e.wait_ge(p.sems["qS"], qS_r)
                p.wait(e, "sT", sT_dn)
                p.inc(e.tensor_tensor(out=outt[:], in0=xres[:], in1=ares[:], op=ALU.add), "sV")
                p.wait(e, "sV")
                for ct in range(8):
                    ii = e.tensor_tensor(out=outt[:, ct, :], in0=outt[:, ct, :],
                                         in1=psD[ct][:, :], op=ALU.add)
                p.inc(ii, "sV")
            sV_out = p.cnt["sV"]

            @blk.scalar
            def _(e, tq=tq, sV_out=sV_out):
                p.wait(e, "sV", sV_out)
                p.inc(e.nop(), "sA")

            @blk.sync
            def _(e, tq=tq, tt0=tt0, sV_out=sV_out):
                p.wait(e, "sV", sV_out)
                e.wait_ge(p.sems["qO"], p.cnt["qO"])
                p.inc(e.dma_start(out=fm3(out_o)[:, :, tt0:tt0 + 512], in_=outt[:]), "qO", 16)
            return sV_out, qS_r

        items = [(tq, it) for tq in range(NQ) for it in range(24)]
        sA_marks = {}
        tails = []
        for k in range(len(items) + 1):
            if k < len(items):
                sA_marks[k] = dn_s1(*items[k])
            if k >= 1:
                tq, it = items[k - 1]
                dn_s2(tq, it, sA_marks[k - 1])
                if it == 23:
                    tails.append(dn_tail(tq))

        @blk.sync
        def _(e):
            for q in ("qS", "qG", "qG2", "qG3", "qO", "qO0", "qO1", "qP", "qP0", "qP1",
                      "qW", "qD", "qL0", "qL1", "qC0", "qC1", "qT0", "qT1"):
                e.wait_ge(p.sems[q], p.cnt[q])
        sub_ps2.close()
        subB.close()
    return p


from concourse.bass_utils import run_bass_kernel_spmd

TCK = 2048
TSL1 = 256
TSL2 = 512
_PROGS = {}


def _progs():
    if not _PROGS:
        _PROGS["L1"] = build_L1(TC=TCK, TSL=TSL1)
        _PROGS["L2"] = build_L2(TC=TCK, TSL=TSL2)
        _PROGS["L3"] = build_L3(TC=TCK, TSL=TSL2)
        _PROGS["L4"] = build_L4(TC=TCK)
        _PROGS["L5"] = build_L5(TC=TCK)
    return _PROGS


def kernel(L0_attn_x, L0_attn_miu_x, L0_attn_lambda_, L0_attn_A_lora, L0_attn_B_lora,
           L0_attn_td_miu, L0_attn_td_A, L0_attn_td_B, L0_attn_u, L0_attn_Wr,
           L0_attn_Wk, L0_attn_Wv, L0_attn_Wg, L0_attn_Wo, L0_attn_gn_gamma,
           L0_attn_gn_beta, L0_ffn_w_gate, L0_ffn_w_up, L0_ffn_w_down):
    x = np.asarray(L0_attn_x, np.float32)
    B, T, _ = x.shape
    NC = 8
    cores = list(range(NC))
    TTP = ceil_div(TCK + 1, 128) * 128

    lam = np.asarray(L0_attn_lambda_, np.float32).reshape(5, C)
    lam_col = np.zeros((128, 5 * 8), np.float32)
    for f in range(5):
        for t in range(8):
            lam_col[:, f * 8 + t] = lam[f][128 * t:128 * (t + 1)]
    miu_x = np.asarray(L0_attn_miu_x, np.float32).reshape(C)
    mx_col = np.ascontiguousarray(miu_x.reshape(8, 128).T)
    tdm = np.asarray(L0_attn_td_miu, np.float32).reshape(C)
    tdm_col = np.ascontiguousarray(tdm.reshape(8, 128).T)
    u = np.asarray(L0_attn_u, np.float32).reshape(16, 64)
    u16 = np.zeros((128, 8 * 16), np.float32)
    for h in range(16):
        u16[64 * (h % 2):64 * (h % 2) + 64, (h // 2) * 16 + h] = u[h]
    id128 = np.eye(128, dtype=np.float32)
    id2 = np.concatenate([np.eye(64), np.eye(64)], 0).astype(np.float32)
    tri = np.triu(np.ones((128, 128), np.float32), 1)
    WkT = np.ascontiguousarray(np.asarray(L0_attn_Wk, np.float32).T)
    WvT = np.ascontiguousarray(np.asarray(L0_attn_Wv, np.float32).T)
    WrT = np.ascontiguousarray(np.asarray(L0_attn_Wr, np.float32).T)
    WgT = np.ascontiguousarray(np.asarray(L0_attn_Wg, np.float32).T)
    WoT = np.ascontiguousarray(np.asarray(L0_attn_Wo, np.float32).T)
    A_lora = np.ascontiguousarray(np.asarray(L0_attn_A_lora, np.float32))
    B_lora = np.ascontiguousarray(np.asarray(L0_attn_B_lora, np.float32))
    td_A = np.ascontiguousarray(np.asarray(L0_attn_td_A, np.float32))
    td_B = np.ascontiguousarray(np.asarray(L0_attn_td_B, np.float32))
    w_up = np.ascontiguousarray(np.asarray(L0_ffn_w_up, np.float32))
    w_gate = np.ascontiguousarray(np.asarray(L0_ffn_w_gate, np.float32))
    w_down = np.ascontiguousarray(np.asarray(L0_ffn_w_down, np.float32))
    gam_row = np.ascontiguousarray(np.asarray(L0_attn_gn_gamma, np.float32).reshape(1, C))
    bet_row = np.ascontiguousarray(np.asarray(L0_attn_gn_beta, np.float32).reshape(1, C))

    xrows_l, xfm_l, xtm_l = [], [], []
    for c in cores:
        b, hf = c // 2, c % 2
        xr = np.zeros((TCK + 1, C), np.float32)
        xr[1:] = x[b, TCK * hf:TCK * (hf + 1)]
        if hf == 1:
            xr[0] = x[b, TCK - 1]
        xrows_l.append(xr)
        xfm_l.append(np.ascontiguousarray(xr.T))
        xtmp = np.zeros((TTP, C), np.float32)
        xtmp[:TCK + 1] = xr
        xtm_l.append(xtmp)

    pr = _progs()

    im1 = [{"x_tm": xtm_l[c], "x_fm": xfm_l[c], "WkT": WkT, "WvT": WvT, "WrT": WrT,
            "A_lora": A_lora, "B_lora": B_lora, "td_A": td_A, "td_B": td_B,
            "lam_col": lam_col, "mx_col": mx_col, "tdmiu_col": tdm_col,
            "u16": u16, "ident": id128, "ident2": id2} for c in cores]
    r1 = run_bass_kernel_spmd(pr["L1"].nc, im1, core_ids=cores).results

    pxn_row = np.concatenate([r1[c]["pxn"][:, 0] for c in cores]).reshape(1, 1024)
    pxn_row = np.ascontiguousarray(pxn_row)

    im2 = [{"x_fm": xfm_l[c], "murstd": r1[c]["murstd"], "pxn_row": pxn_row,
            "w_up": w_up, "w_gate": w_gate} for c in cores]
    r2 = run_bass_kernel_spmd(pr["L2"].nc, im2, core_ids=cores).results

    pxug = np.zeros((1, 2048), np.float32)
    pxug[0, 0:1024] = np.concatenate([r2[c]["pxu"][:, 0] for c in cores])
    pxug[0, 1024:2048] = np.concatenate([r2[c]["pxg"][:, 0] for c in cores])

    im3 = [{"x_fm": xfm_l[c], "murstd": r1[c]["murstd"], "pxn_row": pxn_row,
            "pxug_row": pxug, "wdq_up": r2[c]["wdq_up"], "wdq_gate": r2[c]["wdq_gate"]}
           for c in cores]
    r3 = run_bass_kernel_spmd(pr["L3"].nc, im3, core_ids=cores).results

    pxg2 = np.zeros((1, 1024), np.float32)
    pxg2[0, 0:1024] = np.concatenate([r3[c]["pxg2"][:, 0] for c in cores])

    im4 = [{"upq": r3[c]["upq"], "gq": r3[c]["gq"], "pxug_row": pxug,
            "pxg2_row": pxg2} for c in cores]
    r4 = run_bass_kernel_spmd(pr["L4"].nc, im4, core_ids=cores).results

    pxo = np.zeros((1, 1024), np.float32)
    pxo[0, 0:1024] = np.concatenate([r4[c]["pxo"][:, 0] for c in cores])

    zstate = np.zeros((128, 512), np.float32)
    im5 = []
    for c in cores:
        b, hf = c // 2, c % 2
        utotP = np.ascontiguousarray(r1[c - 1]["utot"]) if hf == 1 else zstate
        im5.append({"x_fm": xfm_l[c], "qt": r1[c]["qt"], "kt": r1[c]["kt"],
                    "vq": r1[c]["vq"], "xg": r1[c]["xg"],
                    "b_rowT": np.ascontiguousarray(np.asarray(r1[c]["b_row"]).T),
                    "utotP": utotP, "traj": r1[c]["traj"], "dpre": r1[c]["dpre"],
                    "WgT": WgT, "WoT": WoT, "w_down": w_down,
                    "upq": r3[c]["upq"], "gq": r3[c]["gq"], "pxug_row": pxug,
                    "pxg2_row": pxg2, "pxo_row": pxo, "gam_row": gam_row,
                    "bet_row": bet_row, "tri": tri, "ident": id128})
    r5 = run_bass_kernel_spmd(pr["L5"].nc, im5, core_ids=cores).results

    out = np.empty((B, T, C), np.float32)
    for c in cores:
        b, hf = c // 2, c % 2
        out[b, TCK * hf:TCK * (hf + 1)] = np.asarray(r5[c]["out_fm"], np.float32).T
    return out
